# version 02
# baseline (speedup 1.0000x reference)
import numpy as np
import ml_dtypes
import concourse.bass as bass
import concourse.mybir as mybir
from concourse.bass_utils import run_bass_kernel_spmd
from contextlib import ExitStack

F32 = mybir.dt.float32
BF16 = mybir.dt.bfloat16
AF = mybir.ActivationFunctionType
ALU = mybir.AluOpType
AXX = mybir.AxisListType.X
BFNP = ml_dtypes.bfloat16

D = 1024
SEQ = 4096
LCH = 512
NTOK = 2048
NT = D // 128
TG = 512
NTG = NTOK // TG
NSG = SEQ // TG
NKT = SEQ // 128
NHEAD_OWN = 8


class Sem:
    def __init__(self, handle):
        self.h = handle
        self.n = 0

    def tick(self, k=1):
        self.n += k
        return self.n


class Ctx:
    ENGINES = ("sync", "gpsimd", "scalar", "vector", "tensor")

    def __init__(self, nc, es):
        self.nc = nc
        self.es = es
        self.q = {e: [] for e in self.ENGINES}
        self.sem = {e: Sem(es.enter_context(nc.semaphore("s_" + e))) for e in self.ENGINES}
        self.hw = {e: {} for e in self.ENGINES}
        self.uid = 0
        self.inputs = {}
        self.dsem = {}

    def _emit(self, eng, fn, waits, inc):
        hw = self.hw[eng]
        best = {}
        for s, v in waits:
            if s is None:
                continue
            if v > best.get(id(s), (None, 0))[1]:
                best[id(s)] = (s, v)
        waits2 = []
        for s, v in best.values():
            if v > 0 and hw.get(id(s), 0) < v:
                waits2.append((s, v))
                hw[id(s)] = v

        def thunk(e, fn=fn, waits2=waits2, inc=inc):
            for s, v in waits2:
                e.wait_ge(s.h, v)
            ins = fn(e)
            if inc is not None and ins is not None:
                ins.then_inc(inc[0].h, inc[1])
        self.q[eng].append(thunk)

    def op(self, eng, fn, waits=(), inc=False, inc_amt=1):
        s = self.sem[eng]
        incpair = None
        ret = None
        if inc:
            v = s.tick(inc_amt)
            incpair = (s, inc_amt)
            ret = (s, v)
        self._emit(eng, fn, [w for w in waits if w], incpair)
        return ret

    def cc(self, kind, op, rg, ins, outs, waits=(), key="cc"):
        if key not in self.dsem:
            self.dsem[key] = Sem(self.es.enter_context(self.nc.semaphore("d_" + key)))
        s = self.dsem[key]
        v = s.tick(1)
        self._emit("gpsimd", lambda e: e.collective_compute(
            kind, op, replica_groups=rg, ins=[a.opt() for a in ins],
            outs=[a.opt() for a in outs]), [w for w in waits if w], (s, 1))
        return (s, v)

    def dma(self, eng, out, in_, waits=(), key="misc", **kw):
        if key not in self.dsem:
            self.dsem[key] = Sem(self.es.enter_context(self.nc.semaphore("d_" + key)))
        s = self.dsem[key]
        v = s.tick(16)
        self._emit(eng, lambda e: e.dma_start(out=out, in_=in_, **kw), [w for w in waits if w], (s, 16))
        return (s, v)

    def mark(self, eng):
        return (self.sem[eng], self.sem[eng].n)

    def all_marks(self):
        return [self.mark(e) for e in self.ENGINES if self.sem[e].n > 0]

    def fence(self):
        f = [self.mark(e) for e in self.ENGINES if self.sem[e].n > 0]
        f += [(s, s.n) for s in self.dsem.values() if s.n > 0]
        return f

    def barrier(self):
        f = self.fence()
        for e in self.ENGINES:
            self._emit(e, lambda en: en.nop(), f, None)

    def din(self, name, shape, dtype):
        name = getattr(self, "prefix", "") + name
        self.inputs[name] = (tuple(shape), dtype)
        return self.nc.dram_tensor(name, list(shape), dtype, kind="ExternalInput").ap()

    def dout(self, name, shape, dtype):
        return self.nc.dram_tensor(name, list(shape), dtype, kind="ExternalOutput").ap()

    def sbuf(self, name, shape, dtype, stack=None):
        self.uid += 1
        return (stack or self.es).enter_context(
            self.nc.sbuf_tensor(f"{name}_{self.uid}", shape, dtype))

    def psum(self, name, shape, dtype, stack=None):
        self.uid += 1
        return (stack or self.es).enter_context(
            self.nc.psum_tensor(f"{name}_{self.uid}", shape, dtype))

    def replay(self):
        with self.nc.allow_low_precision(reason="intentional bf16 matmul operand prep; fp32 accumulation kept"), \
             self.nc.Block() as block:
            for eng in self.ENGINES:
                thunks = self.q[eng]
                if not thunks:
                    continue
                getattr(block, eng)(lambda e, thunks=thunks: [t(e) for t in thunks])



class Shared:

    def __init__(self, C):
        self.C = C
        self.ones_col = C.sbuf("ones_col", [128, 1], BF16)
        self.onesb_row = C.sbuf("onesb_row", [1, 128], BF16)
        self.ident = C.sbuf("ident", [128, 128], BF16)
        self.trimask = C.sbuf("trimask", [128, 1152], BF16)
        self.sel8 = C.sbuf("sel8", [8, 8 * 128], BF16)
        self.xb = C.sbuf("ln_xb", [128, NT * SEQ // 2], BF16)
        self.sq = C.sbuf("ln_sq", [128, NT * SEQ // 2], BF16)
        self.musd = C.sbuf("ln_musd", [4, SEQ], F32)
        self.mu_bf = C.sbuf("ln_mubf", [1, SEQ], BF16)
        self.rs_bf = C.sbuf("ln_rsbf", [1, SEQ], BF16)
        self.mu_b = C.sbuf("ln_mub", [128, SEQ], BF16)
        self.rs_b = C.sbuf("ln_rsb", [128, SEQ], BF16)

    def load(self, C):
        ident = C.din("c_ident", [128, 128], BF16)
        trimask = C.din("c_trimask", [128, 1152], BF16)
        sel8 = C.din("c_sel8", [8, 8 * 128], BF16)
        d1 = C.dma("sync", self.ident[:], ident, key="const")
        d2 = C.dma("sync", self.trimask[:], trimask, key="const")
        d2b = C.dma("sync", self.sel8[:], sel8, key="const")
        d3 = C.op("vector", lambda e: e.memset(self.ones_col[:], 1.0), inc=True)
        d4 = C.op("vector", lambda e: e.memset(self.onesb_row[:], 1.0), inc=True)
        self.dep = [d1, d2, d3, d4]


def host_consts():
    tri = np.zeros((128, 1152), np.float32)
    for m in range(128):
        tri[m, m + 640:] = 1.0
    sel = np.zeros((8, 8 * 128), np.float32)
    for e in range(8):
        sel[e, e * 128:(e + 1) * 128] = 1.0
    return {
        "c_ident": np.eye(128, dtype=np.float32).astype(BFNP),
        "c_trimask": tri.astype(BFNP),
        "c_sel8": sel.astype(BFNP),
    }



def emit_layernorm(C, sh, x_tiles, xn_tiles, n, off=0, deps=(), eps=1e-5, affine=None):
    xb, sq, musd = sh.xb, sh.sq, sh.musd
    mu_bf, rs_bf, mu_b, rs_b = sh.mu_bf, sh.rs_bf, sh.mu_b, sh.rs_b
    ps_stat = sh.ps_stat
    ps_b = sh.ps_b

    for ct in range(NT):
        C.op("vector", lambda e, ct=ct: e.tensor_copy(xb[:, ct * n:(ct + 1) * n], x_tiles(ct)),
             waits=deps, inc=(ct == NT - 1))
        C.op("scalar", lambda e, ct=ct: e.activation(out=sq[:, ct * n:(ct + 1) * n],
                                                     in_=x_tiles(ct), func=AF.Square),
             waits=deps, inc=(ct == NT - 1))
    cv, cs = C.mark("vector"), C.mark("scalar")
    ngr = (n + TG - 1) // TG
    for g in range(ngr):
        w = min(TG, n - g * TG)
        for ct in range(NT):
            C.op("tensor", lambda e, ct=ct, g=g, w=w: e.matmul(
                ps_stat[0:1, g * TG:g * TG + w], sh.ones_col[:, 0:1],
                xb[:, ct * n + g * TG: ct * n + g * TG + w],
                start=(ct == 0), stop=(ct == NT - 1)),
                waits=[cv] if ct == 0 else (), inc=(ct == NT - 1))
        for ct in range(NT):
            C.op("tensor", lambda e, ct=ct, g=g, w=w: e.matmul(
                ps_stat[0:1, n + g * TG:n + g * TG + w], sh.ones_col[:, 0:1],
                sq[:, ct * n + g * TG: ct * n + g * TG + w],
                start=(ct == 0), stop=(ct == NT - 1)),
                waits=[cs] if ct == 0 else (), inc=(ct == NT - 1))
    mm = C.mark("tensor")
    o = off
    S0 = musd[0:1, 0:n]
    S1 = musd[0:1, n:2 * n]
    S2 = musd[0:1, 2 * n:3 * n]
    S3 = musd[0:1, 3 * n:4 * n]
    d = C.op("vector", lambda e: e.tensor_scalar(S2, ps_stat[0:1, 0:n], 1.0 / D, None, ALU.mult),
             waits=[mm], inc=True)
    d2 = C.op("vector", lambda e: e.tensor_scalar(S1, ps_stat[0:1, n:2 * n], 1.0 / D, None, ALU.mult),
              waits=[mm], inc=True)
    d3 = C.op("vector", lambda e: e.tensor_tensor(S0, S2, S2, ALU.mult),
              waits=[d], inc=True)
    d4 = C.op("vector", lambda e: e.tensor_tensor(S1, S1, S0, ALU.subtract),
              waits=[d2, d3], inc=True)
    d5 = C.op("vector", lambda e: e.tensor_scalar(S1, S1, float(eps), None, ALU.add),
              waits=[d4], inc=True)
    d6 = C.op("scalar", lambda e: e.activation(out=S3, in_=S1, func=AF.Sqrt),
              waits=[d5], inc=True)
    d7 = C.op("vector", lambda e: e.reciprocal(rs_bf[:, 0:n], S3), waits=[d6], inc=True)
    d8 = C.op("vector", lambda e: e.tensor_copy(mu_bf[:, 0:n], S2), waits=[d3], inc=True)
    for g in range(ngr):
        w = min(TG, n - g * TG)
        C.op("tensor", lambda e, g=g, w=w: e.matmul(ps_b[:, g * TG:g * TG + w], sh.onesb_row[0:1, :],
                                                    mu_bf[0:1, g * TG:g * TG + w], start=True, stop=True),
             waits=[d8], inc=(g == ngr - 1))
    bm = C.mark("tensor")
    bme = C.op("vector", lambda e: e.tensor_copy(mu_b[:, 0:n], ps_b[:, 0:n]), waits=[bm], inc=True)
    for g in range(ngr):
        w = min(TG, n - g * TG)
        C.op("tensor", lambda e, g=g, w=w: e.matmul(ps_b[:, g * TG:g * TG + w], sh.onesb_row[0:1, :],
                                                    rs_bf[0:1, g * TG:g * TG + w], start=True, stop=True),
             waits=[d7, bme], inc=(g == ngr - 1))
    br = C.mark("tensor")
    bre = C.op("vector", lambda e: e.tensor_copy(rs_b[:, 0:n], ps_b[:, 0:n]), waits=[br], inc=True)
    last = None
    for ct in range(NT):
        da = C.op("vector", lambda e, ct=ct: e.tensor_tensor(xb[:, ct * n:(ct + 1) * n], x_tiles(ct),
                                                             mu_b[:, 0:n], ALU.subtract),
                  waits=[bme], inc=True)
        if affine is None:
            last = C.op("vector", lambda e, ct=ct: e.tensor_tensor(xn_tiles(ct), xb[:, ct * n:(ct + 1) * n],
                                                                   rs_b[:, 0:n], ALU.mult),
                        waits=[da, bre], inc=True)
        else:
            gcol, bcol = affine
            db = C.op("vector", lambda e, ct=ct: e.tensor_tensor(sq[:, ct * n:(ct + 1) * n],
                                                                 xb[:, ct * n:(ct + 1) * n],
                                                                 rs_b[:, 0:n], ALU.mult),
                      waits=[da, bre, cs, mm], inc=True)
            last = C.op("vector", lambda e, ct=ct: e.tensor_scalar(
                xn_tiles(ct), sq[:, ct * n:(ct + 1) * n], gcol(ct), bcol(ct), ALU.mult, ALU.add),
                waits=[db], inc=True)
    return last



def emit_stream_linear(C, wst, n_ct, n_ot, rhs_fn, tok_ranges, evac, wbufs, psums,
                       rhs_deps=(), ot_list=None, w_waits=()):
    stage, wbf = wbufs
    nps = len(psums)
    free_dep = [None] * nps
    cast_done = [None, None]
    last_mm_w = [None, None]
    ots = list(range(n_ot)) if ot_list is None else ot_list
    slot = 0
    for oi, ot in enumerate(ots):
        b = oi % 2
        dd = C.dma("sync", stage[b][:], wst[ot],
                   waits=[cast_done[b]] if cast_done[b] else list(w_waits), key="w%d" % b)
        cast_done[b] = C.op("vector", lambda e, b=b: e.tensor_copy(wbf[b][:], stage[b][:]),
                            waits=[dd] + ([last_mm_w[b]] if last_mm_w[b] else []), inc=True)
        for (t0, t1) in tok_ranges:
            ps = psums[slot % nps]
            mmw = [cast_done[b]] + list(rhs_deps)
            if free_dep[slot % nps] is not None:
                mmw.append(free_dep[slot % nps])
            for ct in range(n_ct):
                C.op("tensor", lambda e, ct=ct, t0=t0, t1=t1, b=b, ps=ps: e.matmul(
                    ps[:, 0:t1 - t0], wbf[b][:, ct * 128:(ct + 1) * 128],
                    rhs_fn(ct, t0, t1),
                    start=(ct == 0), stop=(ct == n_ct - 1)),
                    waits=mmw if ct == 0 else (), inc=(ct == n_ct - 1))
            mm_dep = C.mark("tensor")
            last_mm_w[b] = mm_dep
            free_dep[slot % nps] = evac(ot, t0, t1, ps, mm_dep)
            slot += 1
    return C.mark("tensor")


def host_tile_w(wT, n_ct=None, n_ot=None):
    ci, co = wT.shape
    n_ct = n_ct or ci // 128
    n_ot = n_ot or co // 128
    t = wT.reshape(n_ct, 128, n_ot, 128).transpose(2, 1, 0, 3).reshape(n_ot, 128, n_ct * 128)
    return np.ascontiguousarray(t)



def emit_attn_group(C, sh, ab, heads, score_chunks, vtok_ap, av_rows, out_ap,
                    exp_scale, pre_exp=None, deps=()):
    nh = len(heads)
    exp_dep = {}
    probs_src = {}
    av_free = {h: None for h in heads}
    for s in range(NSG):
        nkt = 4 * s + 4
        score_dep = {}
        for kt in range(nkt):
            for h in heads:
                r = kt % 3
                w = []
                if kt >= 3:
                    w.append(exp_dep[(h, kt - 3)])
                if kt == 0 and av_free[h] is not None:
                    w.append(av_free[h])
                w += list(deps)
                chunks = score_chunks(h)
                for ci, (k_fn, q_fn, trow) in enumerate(chunks):
                    kap, qap, oap = k_fn(kt), q_fn(s), ab["sc"][h][r][:, 0:512]
                    C.op("tensor", lambda e, kap=kap, qap=qap, oap=oap, trow=trow,
                         st=(ci == 0), sp=(ci == len(chunks) - 1): e.matmul(
                             oap, kap, qap,
                             start=st, stop=sp, tile_position=(trow, 0)),
                         waits=w if ci == 0 else (), inc=(ci == len(chunks) - 1))
                score_dep[(h, kt)] = C.mark("tensor")
            if kt >= 1:
                for h in heads:
                    vap, pap = vtok_ap(h, kt - 1), probs_src[(h, kt - 1)]
                    C.op("tensor", lambda e, vap=vap, pap=pap, h=h, kt=kt: e.matmul(
                        ab["av"][h][0:av_rows, 0:512], vap, pap,
                        start=(kt - 1 == 0), stop=(kt - 1 == nkt - 1)),
                        waits=[exp_dep[(h, kt - 1)]], inc=True)
            for h in heads:
                r = kt % 3
                pb = ab["probs"][h][:, r * 512:(r + 1) * 512]
                src = ab["sc"][h][r][:, 0:512]
                dep = score_dep[(h, kt)]
                if pre_exp is not None:
                    src, dep = pre_exp(h, s, kt, src, dep)
                dd = C.op("scalar", lambda e, pb=pb, src=src: e.activation(
                    out=pb, in_=src, func=AF.Exp, scale=float(exp_scale)),
                    waits=[dep], inc=True)
                if kt >= 4 * s:
                    c0 = 640 + (512 * s - 128 * kt)
                    dd = C.op("vector", lambda e, pb=pb, c0=c0: e.tensor_tensor(
                        pb, pb, sh.trimask[:, c0:c0 + 512], ALU.mult),
                        waits=[dd], inc=True)
                exp_dep[(h, kt)] = dd
                probs_src[(h, kt)] = pb
        for h in heads:
            vap, pap = vtok_ap(h, nkt - 1), probs_src[(h, nkt - 1)]
            C.op("tensor", lambda e, vap=vap, pap=pap, h=h: e.matmul(
                ab["av"][h][0:av_rows, 0:512], vap, pap,
                start=(nkt - 1 == 0), stop=True),
                waits=[exp_dep[(h, nkt - 1)]], inc=True)
            avd = C.mark("tensor")
            dv = av_rows - 1
            d1 = C.op("scalar", lambda e, h=h: e.copy(ab["sums"][h][:, 0:512],
                                                      ab["av"][h][dv:dv + 1, 0:512]),
                      waits=[avd], inc=True)
            d2 = C.op("scalar", lambda e, h=h: e.copy(ab["att_sb"][h][0:dv, 0:512],
                                                      ab["av"][h][0:dv, 0:512]),
                      waits=[avd], inc=True)
            d3 = C.op("vector", lambda e, h=h: e.reciprocal(ab["recip"][h][:, 0:512],
                                                            ab["sums"][h][:, 0:512]),
                      waits=[d1], inc=True)
            d4 = C.op("tensor", lambda e, h=h, dv=dv: e.matmul(
                ab["av"][h][0:dv, 0:512], sh.onesb_row[0:1, 0:dv],
                ab["recip"][h][0:1, 0:512], start=True, stop=True),
                waits=[d2, d3], inc=True)
            oap = out_ap(h, s)
            d5 = C.op("vector", lambda e, oap=oap, h=h, dv=dv: e.tensor_tensor(
                oap, ab["att_sb"][h][0:dv, 0:512],
                ab["av"][h][0:dv, 0:512], ALU.mult),
                waits=[d4], inc=True)
            av_free[h] = d5
    return C.mark("tensor")


def make_attn_bufs(C, st, heads, av_rows):
    ab = {"sc": {}, "av": {}, "probs": {}, "att_sb": {}, "sums": {}, "recip": {}}
    for h in heads:
        ab["sc"][h] = [C.psum("sc", [128, 512], F32, st) for _ in range(3)]
        ab["av"][h] = C.psum("av", [128, 512], F32, st)
        ab["probs"][h] = C.sbuf("probs", [128, 3 * 512], BF16, st)
        ab["att_sb"][h] = C.sbuf("attsb", [av_rows - 1, 512], BF16, st)
        ab["sums"][h] = C.sbuf("sums", [1, 512], F32, st)
        ab["recip"][h] = C.sbuf("recip", [1, 512], BF16, st)
    return ab


def emit_vtok(C, sh, v_cm_rows_fn, vtok, n_heads2, deps=()):
    vst = ExitStack()
    ptr = C.psum("ptr", [128, 128], BF16, vst)
    d_ones = []
    for h in range(n_heads2):
        d_ones.append(C.op("vector", lambda e, h=h: e.memset(
            vtok[h].rearrange("p (k c) -> p k c", c=65)[:, :, 64:65], 1.0), inc=True))
    last = None
    for kt in range(NKT):
        vap = v_cm_rows_fn(kt)
        dt = C.op("tensor", lambda e, vap=vap: e.transpose(
            ptr[:], vap, sh.ident[:]),
            waits=list(deps) + ([last] if last else []), inc=True)
        cps = []
        for h in range(n_heads2):
            cps.append(C.op("scalar", lambda e, kt=kt, h=h: e.copy(
                vtok[h][:, kt * 65:kt * 65 + 64], ptr[:, h * 64:(h + 1) * 64]),
                waits=[dt], inc=True))
        last = cps[-1]
    vst.close()
    return [last] + d_ones



def emit_ffn1(C, sh, xn_own_tiles, x_tiles, win, deps=()):
    bdst = ExitStack()
    BD = [C.sbuf("BD", [128, 32 * 128], BF16, bdst) for _ in range(4)]
    bst = ExitStack()
    rm_sb = C.sbuf("rmsb", [16, 8 * 128], BF16, bst)
    rm_f = C.sbuf("rmf", [16, 8 * 128], F32, bst)
    rup_sb = C.sbuf("rupsb", [16, 8 * 64], BF16, bst)
    rup_f = C.sbuf("rupf", [16, 8 * 64], F32, bst)
    ps_bd = C.psum("psbd", [128, 128], F32, bst)
    bd_dep = []
    prev_cp = None
    c1 = c2 = None
    for h in range(4):
        for t in range(8):
            dd1 = C.dma("sync", rm_f[:], win["rm_pad"][h][:, t * 1024:(t + 1) * 1024],
                        key="w0", waits=[c1] if c1 else [])
            dd2 = C.dma("sync", rup_f[:], win["rup"][h][:, t * 512:(t + 1) * 512],
                        key="w1", waits=[c2] if c2 else [])
            c1 = C.op("vector", lambda e: e.tensor_copy(rm_sb[:], rm_f[:]),
                      waits=[dd1] + ([prev_cp] if prev_cp else []), inc=True)
            c2 = C.op("vector", lambda e: e.tensor_copy(rup_sb[:], rup_f[:]), waits=[dd2], inc=True)
            for cp in range(4):
                for wi in range(2):
                    u = 2 * cp + wi
                    C.op("tensor", lambda e, u=u, wi=wi: e.matmul(
                        ps_bd[:, wi * 64:(wi + 1) * 64],
                        rm_sb[:, u * 128:(u + 1) * 128],
                        rup_sb[:, u * 64:(u + 1) * 64],
                        start=True, stop=True),
                        waits=[c1, c2] + ([prev_cp] if prev_cp else []), inc=(wi == 1))
                mmd = C.mark("tensor")
                prev_cp = C.op("vector", lambda e, h=h, t=t, cp=cp: e.tensor_copy(
                    BD[h][:, (t * 4 + cp) * 128:(t * 4 + cp + 1) * 128], ps_bd[:]),
                    waits=[mmd], inc=True)
        bd_dep.append(prev_cp)
    bst.close()
    C.barrier()

    fst = ExitStack()
    zbuf = C.sbuf("zbuf", [128, 32 * 512], BF16, fst)
    wstage = [C.sbuf("cpw", [128, 8 * 128], F32, fst) for _ in range(2)]
    wbf = [C.sbuf("cpwb", [128, 8 * 128], BF16, fst) for _ in range(2)]
    ps_z = [C.psum("psz", [128, 512], F32, fst) for _ in range(2)]
    ps_o = [C.psum("pso", [128, 512], F32, fst) for _ in range(4)]
    out_free = [None] * 4
    wcast = [None, None]
    wlast = [None, None]
    z_free = [None, None]
    bidx = 0
    last = None
    for h in range(4):
        for tc in range(NTG):
            t0, t1 = tc * 512, (tc + 1) * 512
            for zt in range(32):
                t = zt // 4
                zp = ps_z[zt % 2]
                C.op("tensor", lambda e, zt=zt, t=t, zp=zp, h=h, t0=t0, t1=t1: e.matmul(
                    zp[:, 0:512], BD[h][:, zt * 128:(zt + 1) * 128],
                    xn_own_tiles(t, t0, t1), start=True, stop=True),
                    waits=[bd_dep[h]] + list(deps) +
                          ([z_free[zt % 2]] if z_free[zt % 2] else []), inc=True)
                zd = C.mark("tensor")
                z_free[zt % 2] = C.op("scalar", lambda e, zt=zt, zp=zp: e.activation(
                    out=zbuf[:, zt * 512:(zt + 1) * 512], in_=zp[:, 0:512], func=AF.Gelu),
                    waits=[zd], inc=True)
            zdone = z_free[31 % 2]
            for ot in range(8):
                po = ps_o[ot % 4]
                for ztg in range(4):
                    b = bidx % 2
                    bidx += 1
                    dd = C.dma("sync", wstage[b][:], win["cproj"][(h * 4 + ztg) * 8 + ot],
                               waits=[wcast[b]] if wcast[b] else [], key="w%d" % b)
                    wcast[b] = C.op("vector", lambda e, b=b: e.tensor_copy(wbf[b][:], wstage[b][:]),
                                    waits=[dd] + ([wlast[b]] if wlast[b] else []), inc=True)
                    mmw = [wcast[b], zdone]
                    if ztg == 0 and out_free[ot % 4] is not None:
                        mmw.append(out_free[ot % 4])
                    for z8 in range(8):
                        zt = ztg * 8 + z8
                        C.op("tensor", lambda e, z8=z8, zt=zt, po=po, b=b: e.matmul(
                            po[:, 0:512], wbf[b][:, z8 * 128:(z8 + 1) * 128],
                            zbuf[:, zt * 512:(zt + 1) * 512],
                            start=(zt == 0), stop=(zt == 31)),
                            waits=mmw if z8 == 0 else (), inc=(z8 == 7))
                    wlast[b] = C.mark("tensor")
                out_free[ot % 4] = C.op("vector", lambda e, ot=ot, po=po, t0=t0, t1=t1: e.tensor_tensor(
                    x_tiles(ot, t0, t1), x_tiles(ot, t0, t1), po[:, 0:512], ALU.add),
                    waits=[wlast[(bidx - 1) % 2]], inc=True)
                last = out_free[ot % 4]
    fst.close()
    bdst.close()
    return last



def emit_attn_l1(C, sh, xnf_tiles, att_tiles, win, deps=()):
    wst = ExitStack()
    wstage = [C.sbuf("aw", [128, 8 * 128], F32, wst) for _ in range(2)]
    wbf = [C.sbuf("awb", [128, 8 * 128], BF16, wst) for _ in range(2)]
    for g in range(4):
        gst = ExitStack()
        K_cm = C.sbuf("Kcm", [128, SEQ], BF16, gst)
        V_cm = C.sbuf("Vcm", [128, SEQ], BF16, gst)
        Q_cm = C.sbuf("Qcm", [128, SEQ], BF16, gst)
        vtok = [C.sbuf("vtok", [128, NKT * 65], BF16, gst) for _ in range(2)]
        pst = ExitStack()
        ps_p = [C.psum("pp", [128, 512], F32, pst) for _ in range(2)]
        dst = {0: K_cm, 1: V_cm, 2: Q_cm}
        toks = [(t0, t0 + 512) for t0 in range(0, SEQ, 512)]

        def mk_evac(dest):
            def evac(ot, t0, t1, ps, dep):
                return C.op("scalar", lambda e: e.copy(dest[:, t0:t1], ps[:, 0:t1 - t0]),
                            waits=[dep], inc=True)
            return evac
        for j, dest in dst.items():
            emit_stream_linear(C, win["qkv"], NT, 1, xnf_tiles, toks,
                               mk_evac(dest), (wstage, wbf), ps_p,
                               rhs_deps=list(deps) + [C.mark("scalar"), C.mark("vector")],
                               ot_list=[g * 3 + j],
                               w_waits=[C.mark("vector"), C.mark("tensor")])
        kvq_dep = [C.mark("scalar")]
        vt_dep = emit_vtok(C, sh,
                           lambda kt, V_cm=V_cm: V_cm[:, kt * 128:(kt + 1) * 128], vtok, 2,
                           deps=kvq_dep)
        pst.close()
        ab = make_attn_bufs(C, gst, [0, 1], 65)
        emit_attn_group(
            C, sh, ab, [0, 1],
            lambda h, K_cm=K_cm, Q_cm=Q_cm: [
                (lambda kt, h=h, K_cm=K_cm: K_cm[64 * h:64 * h + 64, kt * 128:(kt + 1) * 128],
                 lambda s, h=h, Q_cm=Q_cm: Q_cm[64 * h:64 * h + 64, s * 512:(s + 1) * 512],
                 64 * h)],
            lambda h, kt, vtok=vtok: vtok[h][:, kt * 65:kt * 65 + 65],
            65,
            lambda h, s, g=g: att_tiles(g, h, s),
            1.0 / 8.0,
            deps=kvq_dep + vt_dep)
        gst.close()
        C.barrier()
    wst.close()
    return C.mark("vector")



def alloc_lns(C, st):
    sh = C.shared
    sh.xb = C.sbuf("ln_xb", [128, NT * LCH], BF16, st)
    sh.sq = C.sbuf("ln_sq", [128, NT * LCH], BF16, st)
    sh.musd = C.sbuf("ln_musd", [1, 4 * LCH], F32, st)
    sh.mu_bf = C.sbuf("ln_mubf", [1, LCH], BF16, st)
    sh.rs_bf = C.sbuf("ln_rsbf", [1, LCH], BF16, st)
    sh.mu_b = C.sbuf("ln_mub", [128, LCH], BF16, st)
    sh.rs_b = C.sbuf("ln_rsb", [128, LCH], BF16, st)
    sh.ps_stat = C.psum("ps_stat", [1, 2 * LCH], F32, st)
    sh.ps_b = C.psum("ps_b", [128, LCH], F32, st)


class Consts:
    def __init__(self, C):
        self.ones_col = C.sbuf("ones_col", [128, 1], BF16)
        self.onesb_row = C.sbuf("onesb_row", [1, 128], BF16)
        self.ident = C.sbuf("ident", [128, 128], BF16)
        self.trimask = C.sbuf("trimask", [128, 1152], BF16)
        self.sel8 = C.sbuf("sel8", [8, 8 * 128], BF16)

    def load(self, C):
        ident = C.din("c_ident", [128, 128], BF16)
        trimask = C.din("c_trimask", [128, 1152], BF16)
        sel8 = C.din("c_sel8", [8, 8 * 128], BF16)
        d1 = C.dma("sync", self.ident[:], ident, key="const")
        d2 = C.dma("sync", self.trimask[:], trimask, key="const")
        d2b = C.dma("sync", self.sel8[:], sel8, key="const")
        d3 = C.op("vector", lambda e: e.memset(self.ones_col[:], 1.0), inc=True)
        d4 = C.op("vector", lambda e: e.memset(self.onesb_row[:], 1.0), inc=True)
        self.dep = [d1, d2, d3, d4]


def emit_ln_chunks(C, src_fn, dst_fn, nchunks, deps, affine=None, eps=1e-5):
    sh = C.shared
    last = None
    for c in range(nchunks):
        o = c * LCH
        last = emit_layernorm(C, sh,
                              lambda ct, o=o: src_fn(ct, o, LCH),
                              lambda ct, o=o: dst_fn(ct, o, LCH),
                              LCH,
                              deps=list(deps) + [C.mark("tensor"), C.mark("vector"), C.mark("scalar")],
                              affine=affine, eps=eps)
    return last


def build_A(li):
    if li == 3:
        return build_A3()
    nc = bass.Bass("TRN2", target_bir_lowering=False)
    es = ExitStack()
    C = Ctx(nc, es)
    x_own_in = C.din("x_own", [D, NTOK], F32)
    x_row_in = C.din("x_row", [D, SEQ], F32)
    x_mid = C.dout("x_mid", [D, NTOK], F32)
    att_out = C.dout("att", [4, 128, SEQ], BF16)

    cn = Consts(C)
    cn.load(C)
    C.barrier()
    C.shared = cn
    sh = cn

    xst = ExitStack()
    X = C.sbuf("X", [128, NT * NTOK], F32, xst)

    wins = host_wins_declare(C, li)

    xld = []
    for ct in range(NT):
        xld.append(C.dma("gpsimd", X[:, ct * NTOK:(ct + 1) * NTOK],
                         x_own_in[ct * 128:(ct + 1) * 128, :], key="x"))

    def x_own(ct, t0, t1):
        return X[:, ct * NTOK + t0: ct * NTOK + t1]

    ph1 = ExitStack()
    XN_OWN = C.sbuf("XN_OWN", [128, NT * NTOK], BF16, ph1)

    def xn_own(ct, t0, t1):
        return XN_OWN[:, ct * NTOK + t0: ct * NTOK + t1]

    lns1 = ExitStack()
    alloc_lns(C, lns1)
    ln_dep = emit_ln_chunks(C, lambda ct, o, n: X[:, ct * NTOK + o:ct * NTOK + o + n],
                            lambda ct, o, n: XN_OWN[:, ct * NTOK + o:ct * NTOK + o + n],
                            NTOK // LCH, deps=[cn.dep[-1]] + xld)
    lns1.close()
    C.barrier()
    ffn_dep = emit_ffn(C, li, xn_own, x_own, wins, deps=[ln_dep])
    ph1.close()
    vend0 = C.mark("vector")
    for ct in range(NT):
        C.dma("gpsimd", x_mid[ct * 128:(ct + 1) * 128, :], X[:, ct * NTOK:(ct + 1) * NTOK],
              waits=[vend0, ffn_dep], key="out")
    xst.close()
    C.barrier()

    ph2big = ExitStack()
    ATT = C.sbuf("ATT", [128, 4 * SEQ], BF16, ph2big)
    XNF = C.sbuf("XNF", [128, NT * SEQ], BF16, ph2big)
    ph2 = ExitStack()
    alloc_lns(C, ph2)
    XPART = C.sbuf("XPART", [128, NT * LCH], F32, ph2)
    for c in range(SEQ // LCH):
        o = c * LCH
        pld = [C.dma("gpsimd", XPART[:, ct * LCH:(ct + 1) * LCH],
                     x_row_in[ct * 128:(ct + 1) * 128, o:o + LCH],
                     waits=[C.mark("vector"), C.mark("scalar"), ffn_dep], key="xp")
               for ct in range(NT)]
        emit_layernorm(C, sh,
                       lambda ct: XPART[:, ct * LCH:(ct + 1) * LCH],
                       lambda ct, o=o: XNF[:, ct * SEQ + o: ct * SEQ + o + LCH],
                       LCH, deps=[cn.dep[-1], C.mark("tensor")] + pld)
    if li == 1:
        gcol = C.sbuf("gcol", [128, NT], F32, ph2)
        bcol = C.sbuf("bcol", [128, NT], F32, ph2)
        dg = C.dma("sync", gcol[:], wins["gamma"], key="c1")
        db = C.dma("sync", bcol[:], wins["beta"], key="c2")
        emit_ln_chunks(C, lambda ct, o, n: XNF[:, ct * SEQ + o:ct * SEQ + o + n],
                       lambda ct, o, n: XNF[:, ct * SEQ + o:ct * SEQ + o + n],
                       SEQ // LCH, deps=[C.mark("vector"), dg, db],
                       affine=(lambda ct: gcol[:, ct:ct + 1], lambda ct: bcol[:, ct:ct + 1]))
    xnf_dep = [C.mark("vector")]
    ph2.close()
    C.barrier()

    def xnf(ct, t0, t1):
        return XNF[:, ct * SEQ + t0: ct * SEQ + t1]


    emit_attn(C, li, xnf, lambda g, h, s: ATT[64 * h:64 * h + 64, g * SEQ + s * 512: g * SEQ + (s + 1) * 512],
              wins, deps=xnf_dep)

    vend = C.mark("vector")
    send = C.mark("scalar")
    for g4 in range(4):
        C.dma("gpsimd", att_out[g4], ATT[:, g4 * SEQ:(g4 + 1) * SEQ],
              waits=[vend, send], key="out")
    fin = (C.dsem["out"], C.dsem["out"].n)
    C.op("gpsimd", lambda e: e.nop(), waits=[fin])
    ph2big.close()
    C.replay()
    es.close()
    return nc, C


def host_wins_declare(C, li):
    wins = {}
    if li == 0:
        wins.update(_wins_l0(C))
    if li == 2:
        wins.update(_wins_l2(C))
    if li == 3:
        wins.update(_wins_l3(C))
    if li == 1:
        wins["qkv"] = C.din("w_qkv_strips", [12, 128, NT * 128], F32)
        wins["gamma"] = C.din("gamma_cols", [128, NT], F32)
        wins["beta"] = C.din("beta_cols", [128, NT], F32)
        wins["rm_pad"] = C.din("rm_pad", [4, 16, 64 * 128], F32)
        wins["rup"] = C.din("rup", [4, 16, 64 * 64], F32)
        wins["cproj"] = C.din("cproj_strips", [128, 128, NT * 128], F32)
    return wins


def emit_ffn(C, li, xn_own, x_own, wins, deps):
    if li == 0:
        return emit_ffn0(C, C.shared, xn_own, x_own, wins, deps=deps)
    if li == 1:
        return emit_ffn1(C, C.shared, xn_own, x_own, wins, deps=deps)
    if li == 2:
        return emit_ffn2(C, C.shared, xn_own, x_own, wins, deps=deps)
    if li == 3:
        return emit_ffn3(C, C.shared, xn_own, x_own, wins, deps=deps)
    raise NotImplementedError


def emit_attn(C, li, xnf, att_tiles, wins, deps):
    if li == 0:
        return emit_attn_l0(C, C.shared, xnf, att_tiles, wins, deps=deps)
    if li == 1:
        return emit_attn_l1(C, C.shared, xnf, att_tiles, wins, deps=deps)
    if li == 2:
        return emit_attn_l2(C, C.shared, xnf, att_tiles, wins, deps=deps)
    if li == 3:
        return emit_attn_l3(C, C.shared, xnf, att_tiles, wins, deps=deps)
    raise NotImplementedError



def build_B(li):
    nc = bass.Bass("TRN2", target_bir_lowering=False)
    es = ExitStack()
    C = Ctx(nc, es)
    x_mid_in = C.din("x_mid", [D, NTOK], F32)
    att_in = C.din("att_my", [D, NTOK], BF16)
    x_next = C.dout("x_next", [D, NTOK], F32)

    X = C.sbuf("X", [128, NT * NTOK], F32)
    A = C.sbuf("A", [128, NT * NTOK], BF16)
    xld, ald = [], []
    for ct in range(NT):
        xld.append(C.dma("gpsimd", X[:, ct * NTOK:(ct + 1) * NTOK],
                         x_mid_in[ct * 128:(ct + 1) * 128, :], key="x"))
        ald.append(C.dma("gpsimd", A[:, ct * NTOK:(ct + 1) * NTOK],
                         att_in[ct * 128:(ct + 1) * 128, :], key="a"))

    if li == 1:
        for ct in range(NT):
            C.op("vector", lambda e, ct=ct: e.tensor_tensor(
                X[:, ct * NTOK:(ct + 1) * NTOK], X[:, ct * NTOK:(ct + 1) * NTOK],
                A[:, ct * NTOK:(ct + 1) * NTOK], ALU.add),
                waits=[xld[-1], ald[-1]], inc=True)
    else:
        wo = C.din("wo_strips", [NT, 128, NT * 128], F32)
        bo = C.din("bo_cols", [128, NT], F32)
        bocol = C.sbuf("bocol", [128, NT], F32)
        dbo = C.dma("sync", bocol[:], bo, key="c0")
        wstage = [C.sbuf("bw", [128, NT * 128], F32) for _ in range(2)]
        wbf = [C.sbuf("bwb", [128, NT * 128], BF16) for _ in range(2)]
        psums = [C.psum("bp", [128, 512], F32) for _ in range(4)]
        sc = C.sbuf("bsc", [128, 512], F32)

        def evac(ot, t0, t1, ps, dep):
            d1 = C.op("vector", lambda e: e.tensor_scalar(ps[:, 0:t1 - t0], ps[:, 0:t1 - t0],
                                                          bocol[:, ot:ot + 1], None, ALU.add),
                      waits=[dep, dbo], inc=True)
            return C.op("vector", lambda e: e.tensor_tensor(
                X[:, ot * NTOK + t0: ot * NTOK + t1], X[:, ot * NTOK + t0: ot * NTOK + t1],
                ps[:, 0:t1 - t0], ALU.add),
                waits=[d1, xld[-1]], inc=True)
        emit_stream_linear(C, wo, NT, NT,
                           lambda ct, t0, t1: A[:, ct * NTOK + t0: ct * NTOK + t1],
                           [(i * 512, (i + 1) * 512) for i in range(NTG)],
                           evac, (wstage, wbf), psums, rhs_deps=[ald[-1]])
    vend = C.mark("vector")
    for ct in range(NT):
        C.dma("gpsimd", x_next[ct * 128:(ct + 1) * 128, :], X[:, ct * NTOK:(ct + 1) * NTOK],
              waits=[vend], key="out")
    fin = (C.dsem["out"], C.dsem["out"].n)
    C.op("gpsimd", lambda e: e.nop(), waits=[fin])
    C.replay()
    es.close()
    return nc, C


def set_size(seq):
    global SEQ, NTOK, NTG, NSG, NKT, LCH
    SEQ = seq
    NTOK = seq // 2
    NTG = max(1, NTOK // TG)
    NSG = SEQ // TG
    NKT = SEQ // 128
    LCH = min(512, NTOK)



def prep_l1(ins, hp):
    w_qkv = ins["L1_attn_w_qkv"]
    gam, bet = ins["L1_attn_gamma"], ins["L1_attn_beta"]
    rm, rup = ins["L1_ffn_rail_maker"], ins["L1_ffn_rail_up_proj"]
    wc = ins["L1_ffn_w_cproj"]
    strips = []
    for g in range(4):
        gh = 8 * hp + 2 * g
        for base in (D, 2 * D, 0):
            blk = w_qkv[base + 64 * gh: base + 64 * gh + 128, :]
            strips.append(host_tile_w(np.ascontiguousarray(blk.T))[0])
    out = {
        "w_qkv_strips": np.ascontiguousarray(np.stack(strips)),
        "gamma_cols": np.ascontiguousarray(np.asarray(gam).reshape(NT, 128).T),
        "beta_cols": np.ascontiguousarray(np.asarray(bet).reshape(NT, 128).T),
    }
    rm_pad = np.zeros((4, 16, 64, 128), np.float32)
    rup_t = np.zeros((4, 16, 64, 64), np.float32)
    for h in range(4):
        for r in range(64):
            rm_pad[h, :, r, 16 * (r % 8):16 * (r % 8) + 16] = np.asarray(rm[h, r]).T
            rup_t[h, :, r, :] = np.asarray(rup[h, r]).T
    out["rm_pad"] = rm_pad.reshape(4, 16, 64 * 128)
    out["rup"] = rup_t.reshape(4, 16, 64 * 64)
    wcT = np.ascontiguousarray(np.asarray(wc).T)
    t = wcT.reshape(16, 8, 128, 8, 128).transpose(0, 3, 2, 1, 4).reshape(128, 128, 8 * 128)
    out["cproj_strips"] = np.ascontiguousarray(t)
    return out



def emit_gating(C, sh, st, logits_cm, w_all, mode, deps=()):
    gst = ExitStack()
    lb = C.sbuf("g_lb", [8, NTOK], BF16, gst)
    ps_t = C.psum("g_pt", [128, 128], BF16, gst)
    ps_b = C.psum("g_pb", [8, 128], BF16, gst)
    L = C.sbuf("g_L", [128, 8], F32, gst)
    m1 = C.sbuf("g_m1", [128, 1], F32, gst)
    m1n = C.sbuf("g_m1n", [128, 1], F32, gst)
    m2 = C.sbuf("g_m2", [128, 1], F32, gst)
    sel = C.sbuf("g_sel", [128, 8], F32, gst)
    num = C.sbuf("g_num", [128, 8], F32, gst)
    den = C.sbuf("g_den", [128, 1], F32, gst)
    wt = C.sbuf("g_wt", [128, 8], BF16, gst)
    dl = C.op("vector", lambda e: e.tensor_copy(lb[:], logits_cm[:]), waits=deps, inc=True)
    last = None
    for tt in range(NTOK // 128):
        d1 = C.op("tensor", lambda e, tt=tt: e.transpose(
            ps_t[:, 0:8], lb[0:8, tt * 128:(tt + 1) * 128], sh.ident[0:8, 0:8]),
            waits=[dl] + ([last] if last else []), inc=True)
        d2 = C.op("vector", lambda e: e.tensor_copy(L[:], ps_t[:, 0:8]), waits=[d1], inc=True)
        d3 = C.op("vector", lambda e: e.reduce_max(m1[:], L[:], axis=AXX), waits=[d2], inc=True)
        if mode == "top1sig":
            dg = C.op("scalar", lambda e: e.activation(out=m1n[:], in_=m1[:], func=AF.Sigmoid),
                      waits=[d3], inc=True)
            d4 = C.op("vector", lambda e: e.tensor_scalar(sel[:], L[:], m1[:], None, ALU.is_ge),
                      waits=[d3], inc=True)
            d5 = C.op("vector", lambda e: e.tensor_scalar(wt[:], sel[:], m1n[:], None, ALU.mult),
                      waits=[d4, dg], inc=True)
        else:
            d3n = C.op("vector", lambda e: e.reduce_max(m1n[:], L[:], axis=AXX, negate=True),
                       waits=[d2], inc=True)
            d4 = C.op("vector", lambda e: e.tensor_scalar(sel[:], L[:], m1[:], None, ALU.is_ge),
                      waits=[d3], inc=True)
            dm = C.op("vector", lambda e: e.scalar_tensor_tensor(
                num[:], sel[:], -1e9, L[:], ALU.mult, ALU.add), waits=[d4], inc=True)
            dm2 = C.op("vector", lambda e: e.reduce_max(m2[:], num[:], axis=AXX),
                       waits=[dm], inc=True)
            de = C.op("scalar", lambda e: e.activation(out=num[:], in_=L[:], func=AF.Exp,
                                                       bias=m1n[:]),
                      waits=[d3n, dm2], inc=True)
            ds = C.op("vector", lambda e: e.tensor_scalar(sel[:], L[:], m2[:], None, ALU.is_ge),
                      waits=[dm2, de], inc=True)
            dn = C.op("vector", lambda e: e.tensor_tensor(num[:], num[:], sel[:], ALU.mult),
                      waits=[ds, de], inc=True)
            dd = C.op("vector", lambda e: e.reduce_sum(den[:], num[:], axis=AXX),
                      waits=[dn], inc=True)
            dr = C.op("vector", lambda e: e.reciprocal(den[:], den[:]), waits=[dd], inc=True)
            d5 = C.op("vector", lambda e: e.tensor_scalar(wt[:], num[:], den[:], None, ALU.mult),
                      waits=[dr], inc=True)
        d6 = C.op("tensor", lambda e, tt=tt: e.transpose(ps_b[0:8, :], wt[:, 0:8], sh.ident[:]),
                  waits=[d5], inc=True)
        last = C.op("vector", lambda e, tt=tt: e.tensor_copy(
            w_all[0:8, tt * 128:(tt + 1) * 128], ps_b[0:8, :]), waits=[d6], inc=True)
    gst.close()
    return last



def emit_ffn2(C, sh, xn_own, x_own, wins, deps):
    fst = ExitStack()
    logits = C.sbuf("f2_logits", [8, NTOK], F32, fst)
    w_all = C.sbuf("f2_wall", [8, NTOK], BF16, fst)
    ps_pool = [C.psum("f2_ps", [128, 512], F32, fst) for _ in range(4)]
    wstage = [C.sbuf("f2_ws", [128, 11 * 128], F32, fst) for _ in range(2)]
    wbf = [C.sbuf("f2_wb", [128, 11 * 128], BF16, fst) for _ in range(2)]
    rin = C.sbuf("f2_rin", [128, NT * 512], BF16, fst)
    mid = C.sbuf("f2_mid", [128, 22 * 512], BF16, fst)
    scr = C.sbuf("f2_scr", [128, 512], BF16, fst)
    gb = C.psum("f2_gb", [128, 512], F32, fst)

    rt = C.sbuf("f2_rt", [128, NT * 8], F32, fst)
    drt = C.dma("sync", rt[:], wins["router"], key="c1")
    rtb = C.sbuf("f2_rtb", [128, NT * 8], BF16, fst)
    dcb = C.op("vector", lambda e: e.tensor_copy(rtb[:], rt[:]), waits=[drt], inc=True)
    for tg in range(NTG):
        for ct in range(NT):
            C.op("tensor", lambda e, tg=tg, ct=ct: e.matmul(
                ps_pool[tg % 2][0:8, 0:512], rtb[:, ct * 8:(ct + 1) * 8],
                xn_own(ct, tg * 512, (tg + 1) * 512),
                start=(ct == 0), stop=(ct == NT - 1)),
                waits=[dcb, C.mark("scalar")] + list(deps) if ct == 0 else (), inc=(ct == NT - 1))
        mmd = C.mark("tensor")
        C.op("scalar", lambda e, tg=tg: e.copy(logits[0:8, tg * 512:(tg + 1) * 512],
                                               ps_pool[tg % 2][0:8, 0:512]),
             waits=[mmd], inc=True)
    lg_dep = C.mark("scalar")
    g_dep = emit_gating(C, sh, fst, logits, w_all, "top1sig", deps=[lg_dep])

    last = None
    for e_i in range(-1, 8):
        for tc in range(NTG):
            t0, t1 = tc * 512, (tc + 1) * 512
            if e_i >= 0:
                dbc = C.op("tensor", lambda e, e_i=e_i, t0=t0: e.matmul(
                    gb[:, 0:512], sh.sel8[:, e_i * 128:(e_i + 1) * 128],
                    w_all[0:8, t0:t0 + 512], start=True, stop=True),
                    waits=[g_dep, C.mark("vector")], inc=True)
                rdeps = []
                mtw = C.mark("tensor")
                for ct in range(NT):
                    rdeps.append(C.op("vector", lambda e, ct=ct, t0=t0, t1=t1: e.tensor_tensor(
                        rin[:, ct * 512:(ct + 1) * 512], xn_own(ct, t0, t1), gb[:, 0:512],
                        ALU.mult), waits=[dbc, mtw], inc=True))

                def rhs(ct, a, b):
                    return rin[:, ct * 512 + 0:(ct + 1) * 512]
                rhs_deps = [rdeps[-1]]
            else:
                def rhs(ct, a, b, t0=t0, t1=t1):
                    return xn_own(ct, t0, t1)
                rhs_deps = list(deps) + [g_dep]
            w1st = wins["w1"][e_i] if e_i >= 0 else wins["ws1"]
            w3st = wins["w3"][e_i] if e_i >= 0 else wins["ws3"]
            w2st = wins["w2"][e_i] if e_i >= 0 else wins["ws2"]
            for ot in range(22):
                for j, wst_ in ((0, w1st), (1, w3st)):
                    b = j
                    dd = C.dma("sync", wstage[b][:, 0:1024], wst_[ot], key="w%d" % b,
                               waits=[C.mark("tensor"), C.mark("vector")])
                    wc = C.op("vector", lambda e, b=b: e.tensor_copy(wbf[b][:, 0:1024], wstage[b][:, 0:1024]),
                              waits=[dd], inc=True)
                    ps = ps_pool[j]
                    for ct in range(NT):
                        rap = rhs(ct, t0, t1)
                        C.op("tensor", lambda e, rap=rap, ct=ct, b=b, ps=ps: e.matmul(
                            ps[:, 0:512], wbf[b][:, ct * 128:(ct + 1) * 128],
                            rap, start=(ct == 0), stop=(ct == NT - 1)),
                            waits=[wc] + rhs_deps + [C.mark("scalar"), C.mark("vector")] if ct == 0 else (),
                            inc=(ct == NT - 1))
                mmd = C.mark("tensor")
                dsl = C.op("scalar", lambda e: e.activation(out=scr[:], in_=ps_pool[0][:, 0:512],
                                                            func=AF.Silu), waits=[mmd], inc=True)
                C.op("vector", lambda e, ot=ot: e.tensor_tensor(
                    mid[:, ot * 512:(ot + 1) * 512], scr[:], ps_pool[1][:, 0:512], ALU.mult),
                    waits=[dsl], inc=True)
            mid_dep = C.mark("vector")
            for ot2 in range(8):
                ps = ps_pool[2 + ot2 % 2]
                for half in range(2):
                    b = half
                    n_h = 11
                    dd = C.dma("sync", wstage[b][:, 0:n_h * 128],
                               w2st[ot2][:, half * 1408:(half + 1) * 1408], key="w%d" % b,
                               waits=[C.mark("tensor")])
                    wc = C.op("vector", lambda e, b=b, n_h=n_h: e.tensor_copy(
                        wbf[b][:, 0:n_h * 128], wstage[b][:, 0:n_h * 128]),
                        waits=[dd], inc=True)
                    for j2 in range(n_h):
                        ct2 = half * 11 + j2
                        C.op("tensor", lambda e, ct2=ct2, j2=j2, b=b, ps=ps: e.matmul(
                            ps[:, 0:512], wbf[b][:, j2 * 128:(j2 + 1) * 128],
                            mid[:, ct2 * 512:(ct2 + 1) * 512],
                            start=(ct2 == 0), stop=(ct2 == 21)),
                            waits=[wc, mid_dep, C.mark("vector")] if j2 == 0 else (),
                            inc=(j2 == n_h - 1))
                mmd = C.mark("tensor")
                last = C.op("vector", lambda e, ot2=ot2, ps=ps, t0=t0, t1=t1: e.tensor_tensor(
                    x_own(ot2, t0, t1), x_own(ot2, t0, t1), ps[:, 0:512], ALU.add),
                    waits=[mmd], inc=True)
    fst.close()
    return last


def emit_ffn0(C, sh, xn_own, x_own, wins, deps):
    fst = ExitStack()
    H = C.sbuf("f0_h", [128, NT * NTOK], BF16, fst)
    logits = C.sbuf("f0_logits", [8, NTOK], F32, fst)
    w_all = C.sbuf("f0_wall", [8, NTOK], BF16, fst)
    ws_all = C.sbuf("f0_wsall", [8, NTOK], BF16, fst)
    cols = C.sbuf("f0_cols", [128, 1 + NT + 8 * 32], F32, fst)
    ps_pool = [C.psum("f0_ps", [128, 512], F32, fst) for _ in range(4)]
    gb = C.psum("f0_gb", [128, 512], F32, fst)
    gb2 = C.psum("f0_gb2", [1, 512], F32, fst)
    wstage = [C.sbuf("f0_ws", [128, 8 * 128], F32, fst) for _ in range(2)]
    wbf = [C.sbuf("f0_wb", [128, 8 * 128], BF16, fst) for _ in range(2)]
    act = C.sbuf("f0_act", [128, 8 * 512], BF16, fst)
    scr = C.sbuf("f0_scr", [128, 512], BF16, fst)
    scr2 = C.sbuf("f0_scr2", [128, 512], BF16, fst)
    row = C.sbuf("f0_row", [1, 1024], F32, fst)
    rs_bf = C.sbuf("f0_rsbf", [1, 512], BF16, fst)

    dcols = C.dma("sync", cols[:], wins["f0_cols"], key="c0")
    for e_i in range(8):
        o = 1 + NT + e_i * 32 + 16
        dcols = C.op("vector", lambda e, o=o: e.tensor_scalar(
            cols[:, o:o + 16], cols[:, o:o + 16], 1.0, None, ALU.add),
            waits=[dcols], inc=True)
    hdeps = []
    for g in range(NTOK // TG):
        for ct in range(NT):
            src = scr if ct % 2 == 0 else scr2
            sqd = C.op("scalar", lambda e, src=src, ct=ct, g=g: e.activation(
                out=src[:, 0:512], in_=xn_own(ct, g * TG, (g + 1) * TG), func=AF.Square),
                waits=list(deps) + [C.mark("tensor"), C.mark("vector")], inc=True)
            C.op("tensor", lambda e, src=src, st=(ct == 0), sp=(ct == NT - 1): e.matmul(
                gb2[0:1, 0:512], sh.ones_col[:, 0:1], src[:, 0:512],
                start=st, stop=sp),
                waits=[sqd], inc=True)
        mmd = C.mark("tensor")
        d1 = C.op("vector", lambda e: e.tensor_scalar(row[0:1, 0:512], gb2[0:1, 0:512],
                                                      1.0 / D, 1.1920929e-07, ALU.mult, ALU.add),
                  waits=[mmd], inc=True)
        d2 = C.op("scalar", lambda e: e.activation(out=row[0:1, 512:1024], in_=row[0:1, 0:512],
                                                   func=AF.Sqrt), waits=[d1], inc=True)
        d3 = C.op("vector", lambda e: e.reciprocal(rs_bf[:, 0:512], row[0:1, 512:1024]),
                  waits=[d2], inc=True)
        dbb = C.op("tensor", lambda e, g=g: e.matmul(gb[:, 0:512], sh.onesb_row[0:1, :],
                                                     rs_bf[0:1, 0:512],
                                                     start=True, stop=True),
                   waits=[d3, C.mark("scalar"), C.mark("vector")], inc=True)
        for ct in range(NT):
            hdeps.append(C.op("vector", lambda e, ct=ct, g=g: e.scalar_tensor_tensor(
                H[:, ct * NTOK + g * TG: ct * NTOK + (g + 1) * TG],
                xn_own(ct, g * TG, (g + 1) * TG), cols[:, 1 + ct:2 + ct],
                gb[:, 0:512], ALU.mult, ALU.mult),
                waits=[dbb, dcols], inc=True))
    h_dep = hdeps[-1]

    gt = C.sbuf("f0_gt", [128, NT * 8], F32, fst)
    gtb = C.sbuf("f0_gtb", [128, NT * 8], BF16, fst)
    gbias = C.sbuf("f0_gbias", [1, 8], F32, fst)
    dgt = C.dma("sync", gt[:], wins["gate_w"], key="c1")
    dgb2 = C.dma("sync", gbias[:], wins["gate_b"], key="c2")
    dcb = C.op("vector", lambda e: e.tensor_copy(gtb[:], gt[:]), waits=[dgt], inc=True)
    for tg in range(NTG):
        for ct in range(NT):
            C.op("tensor", lambda e, tg=tg, ct=ct: e.matmul(
                ps_pool[tg % 2][0:8, 0:512], gtb[:, ct * 8:(ct + 1) * 8],
                H[:, ct * NTOK + tg * 512: ct * NTOK + (tg + 1) * 512],
                start=(ct == 0), stop=(ct == NT - 1)),
                waits=[dcb, h_dep, C.mark("scalar")] if ct == 0 else (), inc=(ct == NT - 1))
        mmd = C.mark("tensor")
        C.op("scalar", lambda e, tg=tg: e.copy(logits[0:8, tg * 512:(tg + 1) * 512],
                                               ps_pool[tg % 2][0:8, 0:512]),
             waits=[mmd], inc=True)
    lg_dep = C.mark("scalar")
    gbc = C.sbuf("f0_gbc", [8, 1], F32, fst)
    dgbc = C.dma("sync", gbc[:], wins["gate_b_col"], key="c4")
    dlb = C.op("vector", lambda e: e.tensor_scalar(logits[0:8, :], logits[0:8, :],
                                                   gbc[:], None, ALU.add),
               waits=[lg_dep, dgbc], inc=True)
    g_dep = emit_gating(C, sh, fst, logits, w_all, "top2soft", deps=[dlb])
    dws = C.op("vector", lambda e: e.tensor_scalar(ws_all[0:8, :], w_all[0:8, :],
                                                   1.0 / 1.702, None, ALU.mult),
               waits=[g_dep], inc=True)

    last = None
    ydep = C.mark("vector")
    b2 = C.sbuf("f0_b2", [8, NT * 128], F32, fst)
    b2b = C.sbuf("f0_b2b", [8, NT * 128], BF16, fst)
    db2 = C.dma("sync", b2[:], wins["mlp2_b"], key="c3",
                waits=[g_dep, C.mark("vector"), C.mark("tensor"), C.mark("scalar")])
    db2b = C.op("vector", lambda e: e.tensor_copy(b2b[:], b2[:]), waits=[db2], inc=True)
    for tc in range(NTG):
        t0 = tc * 512
        for ot in range(NT):
            C.op("tensor", lambda e, ot=ot, t0=t0, tc=tc: e.matmul(
                ps_pool[tc % 2][:, 0:512], b2b[0:8, ot * 128:(ot + 1) * 128],
                w_all[0:8, t0:t0 + 512], start=True, stop=True),
                waits=[db2b, g_dep, ydep, C.mark("vector")], inc=True)
            mmd = C.mark("tensor")
            last = C.op("vector", lambda e, ot=ot, t0=t0, tc=tc: e.tensor_tensor(
                x_own(ot, t0, t0 + 512), x_own(ot, t0, t0 + 512),
                ps_pool[tc % 2][:, 0:512], ALU.add),
                waits=[mmd], inc=True)
    for e_i in range(8):
        for tc in range(NTG):
            t0 = tc * 512
            for i in range(8):
                for j, oti in ((0, i), (1, 8 + i)):
                    b = j
                    dd = C.dma("sync", wstage[b][:], wins["mlp1"][e_i * 16 + oti], key="w%d" % b,
                               waits=[C.mark("tensor"), C.mark("vector")])
                    wc = C.op("vector", lambda e, b=b: e.tensor_copy(wbf[b][:], wstage[b][:]),
                              waits=[dd], inc=True)
                    ps = ps_pool[j]
                    for ct in range(NT):
                        C.op("tensor", lambda e, ct=ct, b=b, ps=ps, t0=t0: e.matmul(
                            ps[:, 0:512], wbf[b][:, ct * 128:(ct + 1) * 128],
                            H[:, ct * NTOK + t0: ct * NTOK + t0 + 512],
                            start=(ct == 0), stop=(ct == NT - 1)),
                            waits=[wc, h_dep, C.mark("scalar"), C.mark("vector")] if ct == 0 else (),
                            inc=(ct == NT - 1))
                mmd = C.mark("tensor")
                cga = cols[:, 1 + NT + e_i * 32 + i:1 + NT + e_i * 32 + i + 1]
                cgb = cols[:, 1 + NT + e_i * 32 + 16 + i:1 + NT + e_i * 32 + 16 + i + 1]
                dga = C.op("vector", lambda e, cga=cga: e.tensor_scalar(
                    scr[:], ps_pool[0][:, 0:512], cga, 7.0, ALU.add, ALU.min),
                    waits=[mmd, dcols], inc=True)
                dsl = C.op("scalar", lambda e: e.activation(out=scr2[:], in_=scr[:],
                                                            func=AF.Silu, scale=1.702),
                           waits=[dga], inc=True)
                dgb_ = C.op("vector", lambda e, cgb=cgb: e.tensor_scalar(
                    scr[:], ps_pool[1][:, 0:512], cgb, 8.0, ALU.add, ALU.min),
                    waits=[mmd, dcols, dsl], inc=True)
                C.op("vector", lambda e, i=i: e.scalar_tensor_tensor(
                    act[:, i * 512:(i + 1) * 512], scr[:], -6.0, scr2[:], ALU.max, ALU.mult),
                    waits=[dgb_], inc=True)
            dbc = C.op("tensor", lambda e, e_i=e_i, t0=t0: e.matmul(
                gb[:, 0:512], sh.sel8[:, e_i * 128:(e_i + 1) * 128],
                ws_all[0:8, t0:t0 + 512],
                start=True, stop=True), waits=[dws, C.mark("vector")], inc=True)
            adeps = []
            for i in range(8):
                adeps.append(C.op("vector", lambda e, i=i: e.tensor_tensor(
                    act[:, i * 512:(i + 1) * 512], act[:, i * 512:(i + 1) * 512],
                    gb[:, 0:512], ALU.mult), waits=[dbc], inc=True))
            for ot in range(NT):
                b = ot % 2
                dd = C.dma("sync", wstage[b][:], wins["mlp2"][e_i * 8 + ot], key="w%d" % b,
                           waits=[C.mark("tensor")])
                wc = C.op("vector", lambda e, b=b: e.tensor_copy(wbf[b][:], wstage[b][:]),
                          waits=[dd], inc=True)
                ps = ps_pool[2 + ot % 2]
                for ct in range(8):
                    C.op("tensor", lambda e, ct=ct, b=b, ps=ps: e.matmul(
                        ps[:, 0:512], wbf[b][:, ct * 128:(ct + 1) * 128],
                        act[:, ct * 512:(ct + 1) * 512],
                        start=(ct == 0), stop=(ct == 7)),
                        waits=[wc, adeps[-1], C.mark("vector")] if ct == 0 else (),
                        inc=(ct == 7))
                mmd = C.mark("tensor")
                last = C.op("vector", lambda e, ot=ot, ps=ps, t0=t0: e.tensor_tensor(
                    x_own(ot, t0, t0 + 512), x_own(ot, t0, t0 + 512),
                    ps[:, 0:512], ALU.add), waits=[mmd], inc=True)
    for ct in range(NT):
        last = C.op("vector", lambda e, ct=ct: e.tensor_tensor(
            x_own(ct, 0, NTOK), x_own(ct, 0, NTOK), xn_own(ct, 0, NTOK), ALU.add),
            waits=[last], inc=True)
    fst.close()
    return last



def emit_attn_l2(C, sh, xnf_tiles, att_tiles, wins, deps=()):
    wst = ExitStack()
    wstage = [C.sbuf("aw", [128, 8 * 128], F32, wst) for _ in range(2)]
    wbf = [C.sbuf("awb", [128, 8 * 128], BF16, wst) for _ in range(2)]
    for g in range(4):
        gst = ExitStack()
        K_cm = C.sbuf("Kcm", [128, SEQ], BF16, gst)
        V_cm = C.sbuf("Vcm", [128, SEQ], BF16, gst)
        Q_cm = C.sbuf("Qcm", [128, SEQ], BF16, gst)
        vtok = [C.sbuf("vtok", [128, NKT * 65], BF16, gst) for _ in range(2)]
        rrow = C.sbuf("rrow", [1, 2 * SEQ], F32, gst)
        rcol = C.sbuf("rcol", [128, 2 * NKT], F32, gst)
        rbf = C.sbuf("rbf", [1, SEQ], BF16, gst)
        qsq = C.sbuf("qsq", [128, 512], BF16, gst)
        pst = ExitStack()
        ps_p = [C.psum("pp", [128, 512], F32, pst) for _ in range(2)]
        ps_s = C.psum("pqs", [1, 2048], F32, pst)
        dst = {0: K_cm, 1: V_cm, 2: Q_cm}
        toks = [(t0, t0 + 512) for t0 in range(0, SEQ, 512)]

        def mk_evac(dest):
            def evac(ot, t0, t1, ps, dep):
                return C.op("scalar", lambda e, dest=dest, t0=t0, t1=t1, ps=ps: e.copy(
                    dest[:, t0:t1], ps[:, 0:t1 - t0]), waits=[dep], inc=True)
            return evac
        for j, dest in dst.items():
            emit_stream_linear(C, wins["qkv"], NT, 1, xnf_tiles, toks,
                               mk_evac(dest), (wstage, wbf), ps_p,
                               rhs_deps=list(deps) + [C.mark("scalar"), C.mark("vector")],
                               ot_list=[g * 3 + j],
                               w_waits=[C.mark("vector"), C.mark("tensor")])
        kvq_dep = [C.mark("scalar")]
        def rms_block(SRC, scale_dst):
            for sgi in range(NSG):
                t0 = sgi * 512
                dq = C.op("scalar", lambda e, SRC=SRC, qsq=qsq, t0=t0: e.activation(
                    out=qsq[:, 0:512], in_=SRC[:, t0:t0 + 512], func=AF.Square),
                    waits=kvq_dep + [C.mark("tensor"), C.mark("vector")], inc=True)
                for h in range(2):
                    C.op("tensor", lambda e, qsq=qsq, ps_s=ps_s, h=h: e.matmul(
                        ps_s[0:1, h * 512:(h + 1) * 512], sh.ones_col[64 * h:64 * h + 64, 0:1],
                        qsq[64 * h:64 * h + 64, 0:512],
                        start=True, stop=True, tile_position=(64 * h, 0)),
                        waits=[dq, C.mark("vector")], inc=True)
                    mmd = C.mark("tensor")
                    C.op("vector", lambda e, rrow=rrow, ps_s=ps_s, h=h, t0=t0: e.tensor_scalar(
                        rrow[0:1, h * SEQ + t0: h * SEQ + t0 + 512], ps_s[0:1, h * 512:(h + 1) * 512],
                        1.0 / 64.0, 1.1920929e-07, ALU.mult, ALU.add), waits=[mmd], inc=True)
        rms_block(K_cm, None)
        for h in range(2):
            dsq = C.op("scalar", lambda e, rrow=rrow, h=h: e.activation(
                out=rrow[0:1, h * SEQ:(h + 1) * SEQ], in_=rrow[0:1, h * SEQ:(h + 1) * SEQ],
                func=AF.Sqrt), waits=[C.mark("vector")], inc=True)
            drc = C.op("vector", lambda e, rbf=rbf, rrow=rrow, h=h: e.reciprocal(
                rbf[0:1, 0:SEQ], rrow[0:1, h * SEQ:(h + 1) * SEQ]), waits=[dsq], inc=True)
            for kt in range(NKT):
                C.op("tensor", lambda e, rbf=rbf, ps_p=ps_p, kt=kt: e.matmul(
                    ps_p[0][:, 0:1], rbf[0:1, kt * 128:(kt + 1) * 128],
                    sh.onesb_row[0:1, 0:1], start=True, stop=True),
                    waits=[drc, C.mark("vector")], inc=True)
                mmd = C.mark("tensor")
                C.op("vector", lambda e, rcol=rcol, ps_p=ps_p, h=h, kt=kt: e.tensor_scalar(
                    rcol[:, h * NKT + kt: h * NKT + kt + 1], ps_p[0][:, 0:1],
                    1.0 / 400.0, None, ALU.mult), waits=[mmd], inc=True)
        for sgi in range(NSG):
            t0 = sgi * 512
            dq = C.op("scalar", lambda e, Q_cm=Q_cm, qsq=qsq, t0=t0: e.activation(
                out=qsq[:, 0:512], in_=Q_cm[:, t0:t0 + 512], func=AF.Square),
                waits=[C.mark("vector"), C.mark("tensor")], inc=True)
            for h in range(2):
                C.op("tensor", lambda e, qsq=qsq, ps_s=ps_s, h=h: e.matmul(
                    ps_s[0:1, 1024 + h * 512:1024 + (h + 1) * 512], sh.ones_col[64 * h:64 * h + 64, 0:1],
                    qsq[64 * h:64 * h + 64, 0:512],
                    start=True, stop=True, tile_position=(64 * h, 0)),
                    waits=[dq, C.mark("vector")], inc=True)
                mmd = C.mark("tensor")
                dt1 = C.op("vector", lambda e, rrow=rrow, ps_s=ps_s, h=h: e.tensor_scalar(
                    rrow[0:1, 0:512], ps_s[0:1, 1024 + h * 512:1024 + (h + 1) * 512],
                    1.0 / 64.0, 1.1920929e-07, ALU.mult, ALU.add), waits=[mmd], inc=True)
                dsq = C.op("scalar", lambda e, rrow=rrow: e.activation(
                    out=rrow[0:1, 0:512], in_=rrow[0:1, 0:512], func=AF.Sqrt),
                    waits=[dt1], inc=True)
                drc = C.op("vector", lambda e, rbf=rbf, rrow=rrow: e.reciprocal(
                    rbf[0:1, 0:512], rrow[0:1, 0:512]), waits=[dsq], inc=True)
                dbb = C.op("tensor", lambda e, rbf=rbf, ps_p=ps_p: e.matmul(
                    ps_p[0][0:64, 0:512], sh.onesb_row[0:1, 0:64], rbf[0:1, 0:512],
                    start=True, stop=True), waits=[drc, C.mark("vector")], inc=True)
                mm2 = C.mark("tensor")
                C.op("vector", lambda e, Q_cm=Q_cm, ps_p=ps_p, h=h, t0=t0: e.tensor_tensor(
                    Q_cm[64 * h:64 * h + 64, t0:t0 + 512],
                    Q_cm[64 * h:64 * h + 64, t0:t0 + 512],
                    ps_p[0][0:64, 0:512], ALU.mult), waits=[mm2], inc=True)
        rms_dep = [C.mark("vector"), C.mark("scalar")]
        vt_dep = emit_vtok(C, sh,
                           lambda kt, V_cm=V_cm: V_cm[:, kt * 128:(kt + 1) * 128], vtok, 2,
                           deps=kvq_dep + [C.mark("vector")])
        pst.close()
        ab = make_attn_bufs(C, gst, [0, 1], 65)
        capt = {h: C.sbuf("capt", [128, 512], F32, gst) for h in (0, 1)}

        def pre_exp(h, s_, kt, src, dep, capt=capt, rcol=rcol):
            dd = C.op("scalar", lambda e, h=h, kt=kt, src=src, capt=capt, rcol=rcol: e.activation(
                out=capt[h][:], in_=src, func=AF.Tanh,
                scale=rcol[:, h * NKT + kt: h * NKT + kt + 1]),
                waits=[dep], inc=True)
            return capt[h][:], dd
        emit_attn_group(
            C, sh, ab, [0, 1],
            lambda h, K_cm=K_cm, Q_cm=Q_cm: [
                (lambda kt, h=h, K_cm=K_cm: K_cm[64 * h:64 * h + 64, kt * 128:(kt + 1) * 128],
                 lambda s_, h=h, Q_cm=Q_cm: Q_cm[64 * h:64 * h + 64, s_ * 512:(s_ + 1) * 512],
                 64 * h)],
            lambda h, kt, vtok=vtok: vtok[h][:, kt * 65:kt * 65 + 65],
            65,
            lambda h, s_, g=g: att_tiles(g, h, s_),
            50.0, pre_exp=pre_exp,
            deps=kvq_dep + vt_dep + rms_dep)
        gst.close()
        C.barrier()
    wst.close()
    return C.mark("vector")



def emit_attn_l0(C, sh, xnf_tiles, att_tiles, wins, deps=()):
    wst = ExitStack()
    wstage = [C.sbuf("aw", [128, 8 * 128], F32, wst) for _ in range(2)]
    wbf = [C.sbuf("awb", [128, 8 * 128], BF16, wst) for _ in range(2)]
    K = [C.sbuf("K0", [128, SEQ], BF16, wst) for _ in range(4)]
    Q = [C.sbuf("Q0", [128, SEQ], BF16, wst) for _ in range(4)]
    bcols = C.sbuf("bcols", [128, 12], F32, wst)
    rst = ExitStack()
    cosT = C.sbuf("cosT", [128, SEQ], BF16, rst)
    sinT = C.sbuf("sinT", [128, SEQ], BF16, rst)
    S1 = C.sbuf("ropeS1", [128, 2048], BF16, rst)
    S2 = C.sbuf("ropeS2", [128, 2048], BF16, rst)
    dc1 = C.dma("sync", cosT[:], wins["cos"], key="c0")
    dc2 = C.dma("sync", sinT[:], wins["sin"], key="c1")
    dbc = C.dma("sync", bcols[:], wins["bcols"], key="c2")
    pst = ExitStack()
    ps_p = [C.psum("pp", [128, 512], F32, pst) for _ in range(2)]
    toks = [(t0, t0 + 512) for t0 in range(0, SEQ, 512)]

    def mk_evac(dest, bi):
        def evac(ot, t0, t1, ps, dep):
            return C.op("vector", lambda e, dest=dest, t0=t0, t1=t1, ps=ps, bi=bi: e.tensor_scalar(
                dest[:, t0:t1], ps[:, 0:t1 - t0], bcols[:, bi:bi + 1], None, ALU.add),
                waits=[dep, dbc], inc=True)
        return evac
    for rp in range(4):
        emit_stream_linear(C, wins["qkv"], NT, 1, xnf_tiles, toks,
                           mk_evac(K[rp], rp), (wstage, wbf), ps_p,
                           rhs_deps=list(deps) + [C.mark("vector")],
                           ot_list=[rp], w_waits=[C.mark("vector"), C.mark("tensor")])
    for rp in range(4):
        emit_stream_linear(C, wins["qkv"], NT, 1, xnf_tiles, toks,
                           mk_evac(Q[rp], 4 + rp), (wstage, wbf), ps_p,
                           rhs_deps=list(deps) + [C.mark("vector")],
                           ot_list=[4 + rp], w_waits=[C.mark("vector"), C.mark("tensor")])
    kq_dep = [C.mark("vector")]

    def rope_pair(ebuf, obuf, dep0):
        dd = dep0
        for c0 in range(0, SEQ, 2048):
            c1 = min(SEQ, c0 + 2048)
            w = c1 - c0
            eb, ob = ebuf[:, c0:c1], obuf[:, c0:c1]
            cs, sn = cosT[:, c0:c1], sinT[:, c0:c1]
            d1 = C.op("vector", lambda e, ob=ob, sn=sn, w=w: e.tensor_tensor(
                S1[:, 0:w], ob, sn, ALU.mult), waits=[dd, dc1, dc2], inc=True)
            d2 = C.op("vector", lambda e, ob=ob, cs=cs, w=w: e.tensor_tensor(
                S2[:, 0:w], ob, cs, ALU.mult), waits=[d1], inc=True)
            d3 = C.op("vector", lambda e, eb=eb, ob=ob, sn=sn: e.tensor_tensor(
                ob, eb, sn, ALU.mult), waits=[d2], inc=True)
            d4 = C.op("vector", lambda e, ob=ob, w=w: e.tensor_tensor(
                ob, ob, S2[:, 0:w], ALU.add), waits=[d3], inc=True)
            d5 = C.op("vector", lambda e, eb=eb, cs=cs: e.tensor_tensor(
                eb, eb, cs, ALU.mult), waits=[d4], inc=True)
            dd = C.op("vector", lambda e, eb=eb, w=w: e.tensor_tensor(
                eb, eb, S1[:, 0:w], ALU.subtract), waits=[d5], inc=True)
        return dd
    rd = kq_dep[0]
    for i in range(2):
        rd = rope_pair(K[i], K[2 + i], rd)
        rd = rope_pair(Q[i], Q[2 + i], rd)
    rope_dep = [rd]
    rst.close()
    pst.close()
    C.barrier()

    for g in range(4):
        gst = ExitStack()
        V_cm = C.sbuf("Vcm", [128, SEQ], BF16, gst)
        vtok = [C.sbuf("vtok", [128, NKT * 65], BF16, gst) for _ in range(2)]
        pst2 = ExitStack()
        ps_p2 = [C.psum("pp2", [128, 512], F32, pst2) for _ in range(2)]
        emit_stream_linear(C, wins["qkv"], NT, 1, xnf_tiles, toks,
                           mk_evac(V_cm, 8 + g), (wstage, wbf), ps_p2,
                           rhs_deps=list(deps) + [C.mark("vector")],
                           ot_list=[8 + g], w_waits=[C.mark("vector"), C.mark("tensor")])
        v_dep = [C.mark("vector")]
        vt_dep = emit_vtok(C, sh,
                           lambda kt, V_cm=V_cm: V_cm[:, kt * 128:(kt + 1) * 128], vtok, 2,
                           deps=v_dep)
        pst2.close()
        ab = make_attn_bufs(C, gst, [0, 1], 65)

        def chunks_l0(h, g=g):
            hh = 2 * g + h
            pi, rb = hh // 4, 32 * (hh % 4)
            return [
                (lambda kt, pi=pi, rb=rb: K[pi][rb:rb + 32, kt * 128:(kt + 1) * 128],
                 lambda s_, pi=pi, rb=rb: Q[pi][rb:rb + 32, s_ * 512:(s_ + 1) * 512], rb),
                (lambda kt, pi=pi, rb=rb: K[2 + pi][rb:rb + 32, kt * 128:(kt + 1) * 128],
                 lambda s_, pi=pi, rb=rb: Q[2 + pi][rb:rb + 32, s_ * 512:(s_ + 1) * 512], rb),
            ]
        emit_attn_group(
            C, sh, ab, [0, 1], chunks_l0,
            lambda h, kt, vtok=vtok: vtok[h][:, kt * 65:kt * 65 + 65],
            65,
            lambda h, s_, g=g: att_tiles(g, h, s_),
            1.0 / 8.0,
            deps=rope_dep + vt_dep)
        gst.close()
        C.barrier()
    wst.close()
    return C.mark("vector")



def _wins_l2(C):
    return {
        "qkv": C.din("w_qkv_strips", [12, 128, NT * 128], F32),
        "router": C.din("router_tiles", [128, NT * 8], F32),
        "w1": C.din("w1_strips", [8, 22, 128, NT * 128], F32),
        "w3": C.din("w3_strips", [8, 22, 128, NT * 128], F32),
        "w2": C.din("w2_strips", [8, 8, 128, 22 * 128], F32),
        "ws1": C.din("ws1_strips", [22, 128, NT * 128], F32),
        "ws3": C.din("ws3_strips", [22, 128, NT * 128], F32),
        "ws2": C.din("ws2_strips", [8, 128, 22 * 128], F32),
    }


def _wins_l0(C):
    return {
        "qkv": C.din("w_qkv_strips", [12, 128, NT * 128], F32),
        "cos": C.din("rope_cos", [128, SEQ], BF16),
        "sin": C.din("rope_sin", [128, SEQ], BF16),
        "bcols": C.din("attn_bcols", [128, 12], F32),
        "f0_cols": C.din("f0_cols", [128, 1 + NT + 8 * 32], F32),
        "gate_w": C.din("gate_w_tiles", [128, NT * 8], F32),
        "gate_b": C.din("gate_b_row", [1, 8], F32),
        "gate_b_col": C.din("gate_b_col", [8, 1], F32),
        "mlp1": C.din("mlp1_strips", [8 * 16, 128, NT * 128], F32),
        "mlp2": C.din("mlp2_strips", [8 * 8, 128, NT * 128], F32),
        "mlp2_b": C.din("mlp2_b_rows", [8, NT * 128], F32),
    }


def tiles_cm(w):
    ci, k = w.shape
    return np.ascontiguousarray(w.reshape(ci // 128, 128, k).transpose(1, 0, 2).reshape(128, -1))


def prep_l2(ins, hp):
    w_q, w_k, w_v = (np.asarray(ins["L2_attn_w_q"]), np.asarray(ins["L2_attn_w_k"]),
                     np.asarray(ins["L2_attn_w_v"]))
    strips = []
    for g in range(4):
        gh = 8 * hp + 2 * g
        for w in (w_k, w_v, w_q):
            blk = w[64 * gh: 64 * gh + 128, :]
            strips.append(host_tile_w(np.ascontiguousarray(blk.T))[0])
    out = {"w_qkv_strips": np.ascontiguousarray(np.stack(strips)),
           "router_tiles": tiles_cm(np.asarray(ins["L2_ffn_router_DE"]))}
    out["w1_strips"] = np.stack([host_tile_w(np.asarray(ins["L2_ffn_w1_eDF"][e])) for e in range(8)])
    out["w3_strips"] = np.stack([host_tile_w(np.asarray(ins["L2_ffn_w3_eDF"][e])) for e in range(8)])
    out["w2_strips"] = np.stack([host_tile_w(np.asarray(ins["L2_ffn_w2_eFD"][e])) for e in range(8)])
    out["ws1_strips"] = host_tile_w(np.asarray(ins["L2_ffn_ws1_DF"]))
    out["ws3_strips"] = host_tile_w(np.asarray(ins["L2_ffn_ws3_DF"]))
    out["ws2_strips"] = host_tile_w(np.asarray(ins["L2_ffn_ws2_FD"]))
    out["wo_strips"] = host_tile_w(np.ascontiguousarray(np.asarray(ins["L2_attn_w_o"]).T))
    out["bo_cols"] = np.zeros((128, NT), np.float32)
    return out


def prep_l0(ins, hp):
    Wq, Wk, Wv = (np.asarray(ins["L0_attn_Wq"]), np.asarray(ins["L0_attn_Wk"]),
                  np.asarray(ins["L0_attn_Wv"]))
    bq, bk, bv = (np.asarray(ins["L0_attn_bq"]), np.asarray(ins["L0_attn_bk"]),
                  np.asarray(ins["L0_attn_bv"]))
    strips = []
    bcols = np.zeros((128, 12), np.float32)
    for j, (w, b) in enumerate(((Wk, bk), (Wq, bq))):
        for par in (0, 1):
            for pi in range(2):
                rows = []
                for hh in range(4 * pi, 4 * pi + 4):
                    gh = 8 * hp + hh
                    rows += [64 * gh + 2 * jj + par for jj in range(32)]
                strips_idx = j * 4 + par * 2 + pi if False else None
                strips.append((j * 4 + par * 2 + pi, host_tile_w(np.ascontiguousarray(w[rows, :].T))[0]))
                bcols[:, j * 4 + par * 2 + pi] = b[rows]
    for g in range(4):
        gh = 8 * hp + 2 * g
        rows = list(range(64 * gh, 64 * gh + 128))
        strips.append((8 + g, host_tile_w(np.ascontiguousarray(Wv[rows, :].T))[0]))
        bcols[:, 8 + g] = bv[rows]
    strips.sort(key=lambda t: t[0])
    out = {"w_qkv_strips": np.ascontiguousarray(np.stack([t[1] for t in strips])),
           "attn_bcols": bcols}
    p = np.arange(32)
    invf = 10000.0 ** (-2.0 * p / 64.0)
    t = np.arange(SEQ)
    ang = t[None, :] * invf[:, None]
    out["rope_cos"] = np.ascontiguousarray(np.tile(np.cos(ang), (4, 1))).astype(BFNP)
    out["rope_sin"] = np.ascontiguousarray(np.tile(np.sin(ang), (4, 1))).astype(BFNP)
    cols = np.zeros((128, 1 + NT + 8 * 32), np.float32)
    cols[:, 1:1 + NT] = np.asarray(ins["L0_ffn_norm_w"]).reshape(NT, 128).T
    m1b = np.asarray(ins["L0_ffn_mlp1_b"])
    for e in range(8):
        glu_b, lin_b = m1b[e][0::2], m1b[e][1::2]
        for i in range(8):
            cols[:, 1 + NT + e * 32 + i] = glu_b[i * 128:(i + 1) * 128]
            cols[:, 1 + NT + e * 32 + 16 + i] = lin_b[i * 128:(i + 1) * 128]
    out["f0_cols"] = cols
    out["gate_w_tiles"] = tiles_cm(np.ascontiguousarray(np.asarray(ins["L0_ffn_gate_w"]).T))
    out["gate_b_row"] = np.asarray(ins["L0_ffn_gate_b"]).reshape(1, 8)
    out["gate_b_col"] = np.asarray(ins["L0_ffn_gate_b"]).reshape(8, 1)
    m1w = np.asarray(ins["L0_ffn_mlp1_w"])
    mlp1 = []
    for e in range(8):
        perm = np.concatenate([m1w[e][0::2], m1w[e][1::2]], axis=0)
        mlp1.append(host_tile_w(np.ascontiguousarray(perm.T)))
    out["mlp1_strips"] = np.ascontiguousarray(np.concatenate(mlp1))
    m2w = np.asarray(ins["L0_ffn_mlp2_w"])
    mlp2 = [host_tile_w(np.ascontiguousarray(m2w[e].T)) for e in range(8)]
    out["mlp2_strips"] = np.ascontiguousarray(np.concatenate(mlp2))
    out["mlp2_b_rows"] = np.asarray(ins["L0_ffn_mlp2_b"]).reshape(8, NT * 128)
    out["wo_strips"] = host_tile_w(np.ascontiguousarray(np.asarray(ins["L0_attn_Wo"]).T))
    out["bo_cols"] = np.asarray(ins["L0_attn_bo"]).reshape(NT, 128).T.copy()
    return out



def _wins_l3(C):
    return {
        "qkv": C.din("w_qkv_strips", [12, 128, NT * 128], F32),
        "qkv_b": C.din("qkv_bcols", [128, 12], F32),
        "lat": C.din("lat_bd", [8, 128, 128], F32),
        "v2l": C.din("v2l_bd", [128, 64], F32),
        "l2o": C.din("l2o_bd", [64, 128], F32),
        "lat_b": C.din("lat_bcols", [128, 8], F32),
        "l2o_b": C.din("l2o_bcols", [128, 1], F32),
        "cos": C.din("rope_cos", [128, SEQ], BF16),
        "sin": C.din("rope_sin", [128, SEQ], BF16),
        "Wr": C.din("wr_tiles", [128, NT * 8], F32),
        "A": C.din("a_tiles", [128, 8 * NT * 64], F32),
        "Bw": C.din("bw_strips", [8, 64, 8 * 128], F32),
    }


def emit_ffn3_pre(C, sh, xn_own, x_own, MIDs, wins, deps, dram):
    mid_dram, _ = dram
    fst = ExitStack()
    Asf = C.sbuf("f3_Af", [128, NT * 64], F32, fst)
    Asb = C.sbuf("f3_A", [128, NT * 64], BF16, fst)
    ps = [C.psum("f3_ps", [128, 512], F32, fst) for _ in range(2)]
    dc = None
    for e_i in range(8):
        dd = C.dma("sync", Asf[:], wins["A"][:, e_i * NT * 64:(e_i + 1) * NT * 64],
                   key="w0", waits=[dc] if dc else [])
        dc = C.op("vector", lambda e: e.tensor_copy(Asb[:], Asf[:]),
                  waits=[dd, C.mark("tensor")], inc=True)
        sdone = []
        for tg in range(NTG):
            pp = ps[tg % 2]
            for ct in range(NT):
                C.op("tensor", lambda e, ct=ct, tg=tg, pp=pp: e.matmul(
                    pp[0:64, 0:512], Asb[:, ct * 64:(ct + 1) * 64],
                    xn_own(ct, tg * 512, (tg + 1) * 512),
                    start=(ct == 0), stop=(ct == NT - 1)),
                    waits=[dc, C.mark("scalar"), (C.dsem.get("mb"), C.dsem["mb"].n) if "mb" in C.dsem else None] + list(deps) if ct == 0 else (),
                    inc=(ct == NT - 1))
            mmd = C.mark("tensor")
            sdone.append(C.op("scalar", lambda e, tg=tg, pp=pp: e.copy(
                MIDs[0:64, tg * 512:(tg + 1) * 512], pp[0:64, 0:512]),
                waits=[mmd], inc=True))
        C.dma("gpsimd", mid_dram[0:64, e_i * NTOK:(e_i + 1) * NTOK], MIDs[0:64, :],
              waits=[sdone[-1]], key="mb")
    last2 = None
    for ct in range(NT):
        last2 = C.op("vector", lambda e, ct=ct: e.tensor_tensor(
            x_own(ct, 0, NTOK), x_own(ct, 0, NTOK), xn_own(ct, 0, NTOK), ALU.add),
            waits=list(deps), inc=True)
    fst.close()
    return [last2, (C.dsem["mb"], C.dsem["mb"].n)]


def emit_ffn3_post(C, sh, cls_buf, x_own, mid_dram, wins, deps):
    fst = ExitStack()
    wr = C.sbuf("f3_wr", [128, NT * 8], F32, fst)
    wrb = C.sbuf("f3_wrb", [128, NT * 8], BF16, fst)
    L8 = C.sbuf("f3_L8", [1, 48], F32, fst)
    g8b = C.sbuf("f3_g8b", [1, 8], BF16, fst)
    gcolf = C.sbuf("f3_gcol", [64, 8], F32, fst)
    MID = C.sbuf("f3_mid", [64, 8 * NTOK], BF16, fst)
    Bf = C.sbuf("f3_Bf", [64, 8 * 128], F32, fst)
    Bb = C.sbuf("f3_B", [64, 8 * 128], BF16, fst)
    ps = [C.psum("f3_ps2", [128, 512], F32, fst) for _ in range(2)]
    mld = [C.dma("gpsimd", MID[0:64, :], mid_dram[0:64, :], waits=list(deps), key="mr")]
    dwr = C.dma("sync", wr[:], wins["Wr"], key="c1", waits=list(deps))
    dwrb = C.op("vector", lambda e: e.tensor_copy(wrb[:], wr[:]), waits=[dwr], inc=True)
    for ct in range(NT):
        C.op("tensor", lambda e, ct=ct: e.matmul(
            ps[0][0:1, 0:8], cls_buf[:, ct:ct + 1], wrb[:, ct * 8:(ct + 1) * 8],
            start=(ct == 0), stop=(ct == NT - 1)),
            waits=[dwrb] + list(deps), inc=(ct == NT - 1))
    mmd = C.mark("tensor")
    l = L8[0:1, 0:8]; m1 = L8[0:1, 8:9]; m1n = L8[0:1, 9:10]; m2 = L8[0:1, 10:11]
    sel = L8[0:1, 16:24]; num = L8[0:1, 24:32]; den = L8[0:1, 32:33]
    d0 = C.op("vector", lambda e: e.tensor_copy(l, ps[0][0:1, 0:8]), waits=[mmd], inc=True)
    d1 = C.op("vector", lambda e: e.reduce_max(m1, l, axis=AXX), waits=[d0], inc=True)
    d1n = C.op("vector", lambda e: e.reduce_max(m1n, l, axis=AXX, negate=True), waits=[d1], inc=True)
    d2 = C.op("vector", lambda e: e.tensor_scalar(sel, l, m1, None, ALU.is_ge), waits=[d1], inc=True)
    d3 = C.op("vector", lambda e: e.scalar_tensor_tensor(num, sel, -1e9, l, ALU.mult, ALU.add),
              waits=[d2], inc=True)
    d4 = C.op("vector", lambda e: e.reduce_max(m2, num, axis=AXX), waits=[d3], inc=True)
    d5 = C.op("scalar", lambda e: e.activation(out=num, in_=l, func=AF.Exp, bias=m1n),
              waits=[d4, d1n], inc=True)
    d6 = C.op("vector", lambda e: e.tensor_scalar(sel, l, m2, None, ALU.is_ge), waits=[d4, d5], inc=True)
    d7 = C.op("vector", lambda e: e.tensor_tensor(num, num, sel, ALU.mult), waits=[d6], inc=True)
    d8 = C.op("vector", lambda e: e.reduce_sum(den, num, axis=AXX), waits=[d7], inc=True)
    d9 = C.op("vector", lambda e: e.reciprocal(den, den), waits=[d8], inc=True)
    d10 = C.op("vector", lambda e: e.tensor_scalar(g8b[0:1, 0:8], num, den, None, ALU.mult), waits=[d9], inc=True)
    d11 = C.op("tensor", lambda e: e.matmul(ps[0][0:64, 0:8], sh.onesb_row[0:1, 0:64],
                                            g8b[0:1, 0:8], start=True, stop=True),
               waits=[d10, C.mark("vector")], inc=True)
    d12 = C.op("vector", lambda e: e.tensor_copy(gcolf[:], ps[0][0:64, 0:8]), waits=[d11], inc=True)
    sdeps = []
    for e_i in range(8):
        sdeps.append(C.op("vector", lambda e, e_i=e_i: e.tensor_scalar(
            MID[0:64, e_i * NTOK:(e_i + 1) * NTOK], MID[0:64, e_i * NTOK:(e_i + 1) * NTOK],
            gcolf[:, e_i:e_i + 1], None, ALU.mult), waits=[d12] + mld, inc=True))
    last = None
    dc = None
    for ot in range(NT):
        dd = C.dma("sync", Bf[:], wins["Bw"][ot], key="w1", waits=[dc] if dc else [])
        dc = C.op("vector", lambda e: e.tensor_copy(Bb[:], Bf[:]),
                  waits=[dd, C.mark("tensor")], inc=True)
        for tg in range(NTG):
            pp = ps[(ot * NTG + tg) % 2]
            for e_i in range(8):
                C.op("tensor", lambda e, e_i=e_i, ot=ot, tg=tg, pp=pp: e.matmul(
                    pp[:, 0:512], Bb[0:64, e_i * 128:(e_i + 1) * 128],
                    MID[0:64, e_i * NTOK + tg * 512: e_i * NTOK + (tg + 1) * 512],
                    start=(e_i == 0), stop=(e_i == 7)),
                    waits=[dc, sdeps[-1], C.mark("vector")] if e_i == 0 else (),
                    inc=(e_i == 7))
            mmd = C.mark("tensor")
            last = C.op("vector", lambda e, ot=ot, tg=tg, pp=pp: e.tensor_tensor(
                x_own(ot, tg * 512, (tg + 1) * 512), x_own(ot, tg * 512, (tg + 1) * 512),
                pp[:, 0:512], ALU.add), waits=[mmd], inc=True)
    fst.close()
    return last


def emit_attn_l3(C, sh, xnf_tiles, att_tiles, wins, deps=()):
    wst = ExitStack()
    wstage = [C.sbuf("aw", [128, 8 * 128], F32, wst) for _ in range(2)]
    wbf = [C.sbuf("awb", [128, 8 * 128], BF16, wst) for _ in range(2)]
    bcols = C.sbuf("bcols", [128, 12], F32, wst)
    latw_f = C.sbuf("latwf", [128, 128], F32, wst)
    LW = {}
    for nm in ("q2l_e", "q2l_o", "kv2l_e", "kv2l_o", "wqk_ee", "wqk_oe", "wqk_eo", "wqk_oo"):
        LW[nm] = C.sbuf(nm, [128, 128], BF16, wst)
    v2l = C.sbuf("v2l", [128, 64], BF16, wst)
    l2o = C.sbuf("l2o", [64, 128], BF16, wst)
    latb = C.sbuf("latb", [128, 8], F32, wst)
    l2ob = C.sbuf("l2ob", [128, 1], F32, wst)
    cosT = C.sbuf("cosT", [128, SEQ], BF16, wst)
    sinT = C.sbuf("sinT", [128, SEQ], BF16, wst)
    dl = C.dma("sync", bcols[:], wins["qkv_b"], key="c2")
    cv = None
    for i, nm in enumerate(("q2l_e", "q2l_o", "kv2l_e", "kv2l_o", "wqk_ee", "wqk_oe", "wqk_eo", "wqk_oo")):
        dd = C.dma("sync", latw_f[:], wins["lat"][i], key="c0", waits=[cv] if cv else [])
        cv = C.op("vector", lambda e, nm=nm: e.tensor_copy(LW[nm][:], latw_f[:]), waits=[dd], inc=True)
    dd = C.dma("sync", latw_f[:, 0:64], wins["v2l"], key="c0", waits=[cv])
    cv = C.op("vector", lambda e: e.tensor_copy(v2l[:], latw_f[:, 0:64]), waits=[dd], inc=True)
    dd = C.dma("sync", latw_f[0:64, 0:128], wins["l2o"], key="c0", waits=[cv])
    cv = C.op("vector", lambda e: e.tensor_copy(l2o[:], latw_f[0:64, 0:128]), waits=[dd], inc=True)
    d_lb = C.dma("sync", latb[:], wins["lat_b"], key="c1")
    d_ob = C.dma("sync", l2ob[:], wins["l2o_b"], key="c3")
    d_c1 = C.dma("sync", cosT[:], wins["cos"], key="c4")
    d_c2 = C.dma("sync", sinT[:], wins["sin"], key="c5")
    wdep = [cv, dl, d_lb, d_ob, d_c1, d_c2]
    toks = [(t0, t0 + 512) for t0 in range(0, SEQ, 512)]

    for g in range(4):
        gst = ExitStack()
        QLe = C.sbuf("QLe", [128, SEQ], BF16, gst)
        QLo = C.sbuf("QLo", [128, SEQ], BF16, gst)
        KLe = C.sbuf("KLe", [128, SEQ], BF16, gst)
        KLo = C.sbuf("KLo", [128, SEQ], BF16, gst)
        VL = C.sbuf("VL", [64, SEQ], BF16, gst)
        YL = C.sbuf("YL", [64, SEQ], BF16, gst)
        vtok = [C.sbuf("vtok", [128, NKT * 33], BF16, gst) for _ in range(2)]
        rS = C.sbuf("rS", [128, 1024], BF16, gst)
        HC = C.sbuf("HC", [128, 512], BF16, gst)
        pst = ExitStack()
        ps_p = [C.psum("pp", [128, 512], F32, pst) for _ in range(2)]
        ps_l = [C.psum("pl", [128, 512], F32, pst) for _ in range(2)]

        plan = ((0, ((LW["kv2l_e"], KLe, 2, 128), (LW["kv2l_o"], KLo, 3, 128))),
                (1, ((v2l, VL, 4, 64),)),
                (2, ((LW["q2l_e"], QLe, 0, 128), (LW["q2l_o"], QLo, 1, 128))))
        for j, lats in plan:
            b = j % 2
            dd = C.dma("sync", wstage[b][:], wins["qkv"][g * 3 + j], key="w%d" % b,
                       waits=[C.mark("vector"), C.mark("tensor")])
            wc = C.op("vector", lambda e, b=b: e.tensor_copy(wbf[b][:], wstage[b][:]),
                      waits=[dd], inc=True)
            for t0, t1 in toks:
                pc = ps_p[(t0 // 512) % 2]
                for ct in range(NT):
                    rap = xnf_tiles(ct, t0, t1)
                    C.op("tensor", lambda e, rap=rap, ct=ct, b=b, pc=pc: e.matmul(
                        pc[:, 0:512], wbf[b][:, ct * 128:(ct + 1) * 128], rap,
                        start=(ct == 0), stop=(ct == NT - 1)),
                        waits=[wc, C.mark("vector")] + list(deps) if ct == 0 else (),
                        inc=(ct == NT - 1))
                mmd = C.mark("tensor")
                hcd = C.op("vector", lambda e, HC=HC, pc=pc, j=j, g=g: e.tensor_scalar(
                    HC[:], pc[:, 0:512], bcols[:, g * 3 + j:g * 3 + j + 1], None, ALU.add),
                    waits=[mmd, dl, C.mark("tensor")], inc=True)
                for (W, DST, bi, rows) in lats:
                    pl = ps_l[bi % 2]
                    C.op("tensor", lambda e, W=W, pl=pl, HC=HC, rows=rows: e.matmul(
                        pl[0:rows, 0:512], W[:, 0:rows], HC[:],
                        start=True, stop=True),
                        waits=[hcd, cv, C.mark("vector")], inc=True)
                    mm2 = C.mark("tensor")
                    C.op("vector", lambda e, DST=DST, pl=pl, t0=t0, t1=t1, bi=bi, rows=rows: e.tensor_scalar(
                        DST[0:rows, t0:t1], pl[0:rows, 0:512], latb[0:rows, bi:bi + 1],
                        None, ALU.add),
                        waits=[mm2, d_lb], inc=True)
        kl_dep = [C.mark("vector")]

        def rope_pair(EB, OB, dep0):
            dd = dep0
            for c0 in range(0, SEQ, 512):
                w = 512
                for bb in (0, 64):
                    eb, ob = EB[bb:bb + 16, c0:c0 + w], OB[bb:bb + 16, c0:c0 + w]
                    cs, sn = cosT[bb:bb + 16, c0:c0 + w], sinT[bb:bb + 16, c0:c0 + w]
                    s1 = rS[bb:bb + 16, 0:w]
                    s2 = rS[bb:bb + 16, 512:512 + w]
                    d1 = C.op("vector", lambda e, s1=s1, ob=ob, sn=sn: e.tensor_tensor(s1, ob, sn, ALU.mult),
                              waits=[dd, d_c1, d_c2], inc=True)
                    d2 = C.op("vector", lambda e, s2=s2, ob=ob, cs=cs: e.tensor_tensor(s2, ob, cs, ALU.mult),
                              waits=[d1], inc=True)
                    d3 = C.op("vector", lambda e, ob=ob, eb=eb, sn=sn: e.tensor_tensor(ob, eb, sn, ALU.mult),
                              waits=[d2], inc=True)
                    d4 = C.op("vector", lambda e, ob=ob, s2=s2: e.tensor_tensor(ob, ob, s2, ALU.add),
                              waits=[d3], inc=True)
                    d5 = C.op("vector", lambda e, eb=eb, cs=cs: e.tensor_tensor(eb, eb, cs, ALU.mult),
                              waits=[d4], inc=True)
                    dd = C.op("vector", lambda e, eb=eb, s1=s1: e.tensor_tensor(eb, eb, s1, ALU.subtract),
                              waits=[d5], inc=True)
            return dd
        rd = C.mark("vector")
        rd = rope_pair(QLe, QLo, rd)
        rd = rope_pair(KLe, KLo, rd)
        qfk_dep = rd
        for t0, t1 in toks:
            m1w = [qfk_dep, cv, C.mark("vector")]
            C.op("tensor", lambda e, QLe=QLe, pl=ps_l[0], t0=t0, t1=t1: e.matmul(
                pl[:, 0:512], LW["wqk_ee"][:, 0:128], QLe[:, t0:t1],
                start=True, stop=False), waits=m1w, inc=False)
            C.op("tensor", lambda e, QLo=QLo, pl=ps_l[0], t0=t0, t1=t1: e.matmul(
                pl[:, 0:512], LW["wqk_oe"][:, 0:128], QLo[:, t0:t1],
                start=False, stop=True), waits=(), inc=True)
            C.op("tensor", lambda e, QLe=QLe, pl=ps_l[1], t0=t0, t1=t1: e.matmul(
                pl[:, 0:512], LW["wqk_eo"][:, 0:128], QLe[:, t0:t1],
                start=True, stop=False), waits=(), inc=False)
            C.op("tensor", lambda e, QLo=QLo, pl=ps_l[1], t0=t0, t1=t1: e.matmul(
                pl[:, 0:512], LW["wqk_oo"][:, 0:128], QLo[:, t0:t1],
                start=False, stop=True), waits=(), inc=True)
            mmq = C.mark("tensor")
            C.op("vector", lambda e, QLe=QLe, pl=ps_l[0], t0=t0, t1=t1: e.tensor_scalar(
                QLe[:, t0:t1], pl[:, 0:512], latb[:, 5:6], None, ALU.add),
                waits=[mmq, d_lb], inc=True)
            qfk_dep = C.op("vector", lambda e, QLo=QLo, pl=ps_l[1], t0=t0, t1=t1: e.tensor_scalar(
                QLo[:, t0:t1], pl[:, 0:512], latb[:, 6:7], None, ALU.add),
                waits=[mmq], inc=True)
        lat_dep = [C.mark("vector")]
        vt_dep = emit_vtok33(C, sh, VL, vtok, deps=lat_dep)
        pst.close()
        ab = make_attn_bufs(C, gst, [0, 1], 33)

        def chunks_l3(h):
            b = 64 * h
            return [
                (lambda kt, b=b, KLe=KLe: KLe[b:b + 16, kt * 128:(kt + 1) * 128],
                 lambda s_, b=b, QLe=QLe: QLe[b:b + 16, s_ * 512:(s_ + 1) * 512], b),
                (lambda kt, b=b, KLo=KLo: KLo[b:b + 16, kt * 128:(kt + 1) * 128],
                 lambda s_, b=b, QLo=QLo: QLo[b:b + 16, s_ * 512:(s_ + 1) * 512], b),
            ]
        emit_attn_group(
            C, sh, ab, [0, 1], chunks_l3,
            lambda h, kt, vtok=vtok: vtok[h][:, kt * 33:kt * 33 + 33],
            33,
            lambda h, s_, YL=YL: YL[32 * h:32 * h + 32, s_ * 512:(s_ + 1) * 512],
            float(32 ** -0.5),
            deps=lat_dep + vt_dep)
        yd = C.mark("vector")
        for s_, (t0, t1) in enumerate(toks):
            pv = ab["av"][0]
            C.op("tensor", lambda e, pv=pv, YL=YL, t0=t0, t1=t1: e.matmul(
                pv[:, 0:512], l2o[:, 0:128], YL[0:64, t0:t1],
                start=True, stop=True),
                waits=[yd, C.mark("scalar"), C.mark("vector")], inc=True)
            mmd = C.mark("tensor")
            oap = att_tiles(g, t0, t1)
            C.op("vector", lambda e, oap=oap, pv=pv: e.tensor_scalar(
                oap, pv[:, 0:512], l2ob[:, 0:1], None, ALU.add),
                waits=[mmd, d_ob], inc=True)
        gst.close()
        C.barrier()
    wst.close()
    return C.mark("vector")


def emit_vtok33(C, sh, VL, vtok, deps=()):
    vst = ExitStack()
    ptr = C.psum("ptr", [128, 64], BF16, vst)
    d_ones = []
    for h in range(2):
        d_ones.append(C.op("vector", lambda e, h=h: e.memset(
            vtok[h].rearrange("p (k c) -> p k c", c=33)[:, :, 32:33], 1.0), inc=True))
    last = None
    for kt in range(NKT):
        vap = VL[0:64, kt * 128:(kt + 1) * 128]
        dt = C.op("tensor", lambda e, vap=vap: e.transpose(ptr[:, 0:64], vap, sh.ident[0:64, 0:64]),
                  waits=list(deps) + ([last] if last else []), inc=True)
        cps = []
        for h in range(2):
            cps.append(C.op("scalar", lambda e, kt=kt, h=h: e.copy(
                vtok[h][:, kt * 33:kt * 33 + 32], ptr[:, h * 32:(h + 1) * 32]),
                waits=[dt], inc=True))
        last = cps[-1]
    vst.close()
    return [last] + d_ones


def build_A3():
    li = 3
    nc = bass.Bass("TRN2", target_bir_lowering=False)
    es = ExitStack()
    C = Ctx(nc, es)
    x_own_in = C.din("x_own", [D, NTOK], F32)
    x_row_in = C.din("x_row", [D, SEQ], F32)
    x_mid = C.dout("x_mid", [D, NTOK], F32)
    att_out = C.dout("att", [4, 128, SEQ], BF16)
    mid_dram = nc.dram_tensor("mid_bounce", [64, 8 * NTOK], BF16, kind="Internal").ap()
    xnf_dram = nc.dram_tensor("xnf_bounce", [D, SEQ], BF16, kind="Internal").ap()

    cn = Consts(C)
    cn.load(C)
    C.barrier()
    C.shared = cn
    sh = cn
    wins = host_wins_declare(C, li)
    cls_buf = C.sbuf("cls_keep", [128, NT], BF16)

    xst = ExitStack()
    X = C.sbuf("X", [128, NT * NTOK], F32, xst)
    xld = []
    for ct in range(NT):
        xld.append(C.dma("gpsimd", X[:, ct * NTOK:(ct + 1) * NTOK],
                         x_own_in[ct * 128:(ct + 1) * 128, :], key="x"))

    def x_own(ct, t0, t1):
        return X[:, ct * NTOK + t0: ct * NTOK + t1]

    ph1 = ExitStack()
    XN_OWN = C.sbuf("XN_OWN", [128, NT * NTOK], BF16, ph1)

    def xn_own(ct, t0, t1):
        return XN_OWN[:, ct * NTOK + t0: ct * NTOK + t1]

    lns1 = ExitStack()
    alloc_lns(C, lns1)
    ln_dep = emit_ln_chunks(C, lambda ct, o, n: X[:, ct * NTOK + o:ct * NTOK + o + n],
                            lambda ct, o, n: XN_OWN[:, ct * NTOK + o:ct * NTOK + o + n],
                            NTOK // LCH, deps=[cn.dep[-1]] + xld)
    lns1.close()
    C.barrier()
    MIDs = C.sbuf("MIDstage", [64, NTOK], BF16, ph1)
    pre_dep = emit_ffn3_pre(C, cn, xn_own, x_own, MIDs, wins, deps=[ln_dep],
                            dram=(mid_dram, C))
    ph1.close()
    C.barrier()

    ph2big = ExitStack()
    XNF = C.sbuf("XNF", [128, NT * SEQ], BF16, ph2big)
    ph2 = ExitStack()
    alloc_lns(C, ph2)
    XPART = C.sbuf("XPART", [128, NT * LCH], F32, ph2)
    for c in range(SEQ // LCH):
        o = c * LCH
        pld = [C.dma("gpsimd", XPART[:, ct * LCH:(ct + 1) * LCH],
                     x_row_in[ct * 128:(ct + 1) * 128, o:o + LCH],
                     waits=[C.mark("vector"), C.mark("scalar")], key="xp")
               for ct in range(NT)]
        emit_layernorm(C, sh,
                       lambda ct: XPART[:, ct * LCH:(ct + 1) * LCH],
                       lambda ct, o=o: XNF[:, ct * SEQ + o: ct * SEQ + o + LCH],
                       LCH, deps=[cn.dep[-1], C.mark("tensor")] + pld)
    xnf_dep = [C.mark("vector")]
    ph2.close()
    for ct in range(NT):
        C.op("vector", lambda e, ct=ct: e.tensor_copy(cls_buf[:, ct:ct + 1],
                                                      XNF[:, ct * SEQ:ct * SEQ + 1]),
             waits=xnf_dep, inc=True)
        C.dma("gpsimd", xnf_dram[ct * 128:(ct + 1) * 128, :], XNF[:, ct * SEQ:(ct + 1) * SEQ],
              waits=xnf_dep, key="xb")
    cls_dep = [C.mark("vector"), (C.dsem["xb"], C.dsem["xb"].n)]
    ph2big.close()
    C.barrier()

    fp = emit_ffn3_post(C, cn, cls_buf, x_own, mid_dram, wins, deps=cls_dep)
    vend0 = C.mark("vector")
    for ct in range(NT):
        C.dma("gpsimd", x_mid[ct * 128:(ct + 1) * 128, :], X[:, ct * NTOK:(ct + 1) * NTOK],
              waits=[vend0, fp], key="out")
    xst.close()
    C.barrier()

    ast = ExitStack()
    ATT = C.sbuf("ATT", [128, 4 * SEQ], BF16, ast)
    XNF2 = C.sbuf("XNF2", [128, NT * SEQ], BF16, ast)
    xrl = []
    for ct in range(NT):
        xrl.append(C.dma("gpsimd", XNF2[:, ct * SEQ:(ct + 1) * SEQ],
                         xnf_dram[ct * 128:(ct + 1) * 128, :], key="xr"))

    def xnf2(ct, t0, t1):
        return XNF2[:, ct * SEQ + t0: ct * SEQ + t1]
    emit_attn_l3(C, cn, xnf2,
                 lambda g, t0, t1: ATT[:, g * SEQ + t0: g * SEQ + t1],
                 wins, deps=xrl)
    vend = C.mark("vector")
    send = C.mark("scalar")
    for g4 in range(4):
        C.dma("gpsimd", att_out[g4], ATT[:, g4 * SEQ:(g4 + 1) * SEQ],
              waits=[vend, send], key="out")
    fin = (C.dsem["out"], C.dsem["out"].n)
    C.op("gpsimd", lambda e: e.nop(), waits=[fin])
    ast.close()
    C.replay()
    es.close()
    return nc, C


def prep_l3(ins, hp):
    caw = np.asarray(ins["L3_attn_c_attn_w"])
    cab = np.asarray(ins["L3_attn_c_attn_b"])
    strips = []
    bcols = np.zeros((128, 12), np.float32)
    for g in range(4):
        gh = 8 * hp + 2 * g
        for j, base in enumerate((D, 2 * D, 0)):
            rows = slice(base + 64 * gh, base + 64 * gh + 128)
            strips.append(host_tile_w(np.ascontiguousarray(caw[rows, :].T))[0])
            bcols[:, g * 3 + j] = cab[rows]
    out = {"w_qkv_strips": np.ascontiguousarray(np.stack(strips)), "qkv_bcols": bcols}
    q2l = np.asarray(ins["L3_attn_q2l_w"])
    kv2l = np.asarray(ins["L3_attn_kv2l_w"])
    wqk = np.asarray(ins["L3_attn_wqk_w"])
    l2ow = np.asarray(ins["L3_attn_l2o_w"])
    lat = np.zeros((8, 128, 128), np.float32)
    for i, (w, par) in enumerate(((q2l, 0), (q2l, 1), (kv2l, 0), (kv2l, 1))):
        for hb in (0, 1):
            for j in range(16):
                lat[i, hb * 64:(hb + 1) * 64, hb * 64 + j] = w[2 * j + par, :]
    for idx, (pin, pout) in enumerate(((0, 0), (1, 0), (0, 1), (1, 1))):
        for hb in (0, 1):
            for jin in range(16):
                for jout in range(16):
                    lat[4 + idx, hb * 64 + jin, hb * 64 + jout] = wqk[2 * jout + pout, 2 * jin + pin]
    out["lat_bd"] = lat
    v2l = np.zeros((128, 64), np.float32)
    for hb in (0, 1):
        for j in range(32):
            v2l[hb * 64:(hb + 1) * 64, hb * 32 + j] = kv2l[j, :]
    out["v2l_bd"] = v2l
    l2o_bd = np.zeros((64, 128), np.float32)
    for hb in (0, 1):
        for l in range(32):
            l2o_bd[hb * 32 + l, hb * 64:(hb + 1) * 64] = l2ow[:, l]
    out["l2o_bd"] = l2o_bd
    latb = np.zeros((128, 8), np.float32)
    q2lb = np.asarray(ins["L3_attn_q2l_b"])
    kv2lb = np.asarray(ins["L3_attn_kv2l_b"])
    wqkb = np.asarray(ins["L3_attn_wqk_b"])
    for hb in (0, 1):
        for j in range(16):
            latb[hb * 64 + j, 0] = q2lb[2 * j]
            latb[hb * 64 + j, 1] = q2lb[2 * j + 1]
            latb[hb * 64 + j, 2] = kv2lb[2 * j]
            latb[hb * 64 + j, 3] = kv2lb[2 * j + 1]
            latb[hb * 64 + j, 5] = wqkb[2 * j]
            latb[hb * 64 + j, 6] = wqkb[2 * j + 1]
        for j in range(32):
            latb[hb * 64 + j, 4] = kv2lb[j]
    latb[:, 4] = 0.0
    vb = np.zeros(128, np.float32)
    vb[0:32] = kv2lb
    vb[32:64] = kv2lb
    latb[:, 4] = vb
    out["lat_bcols"] = latb
    l2obc = np.zeros((128, 1), np.float32)
    l2obc[0:64, 0] = np.asarray(ins["L3_attn_l2o_b"])
    l2obc[64:128, 0] = np.asarray(ins["L3_attn_l2o_b"])
    out["l2o_bcols"] = l2obc
    i = np.arange(16)
    invf = 10000.0 ** (-2.0 * i / 32.0)
    t = np.arange(SEQ)
    ang = t[None, :] * invf[:, None]
    blk = np.zeros((32, SEQ), np.float32)
    cs = blk.copy(); cs[0:16] = np.cos(ang)
    sn = blk.copy(); sn[0:16] = np.sin(ang)
    out["rope_cos"] = np.ascontiguousarray(np.tile(cs, (4, 1))).astype(BFNP)
    out["rope_sin"] = np.ascontiguousarray(np.tile(sn, (4, 1))).astype(BFNP)
    out["wr_tiles"] = tiles_cm(np.ascontiguousarray(np.asarray(ins["L3_ffn_Wr"]).T))
    A = np.asarray(ins["L3_ffn_A"])
    a_tiles = np.concatenate([tiles_cm(np.ascontiguousarray(A[e].T)) for e in range(8)], axis=1)
    out["a_tiles"] = np.ascontiguousarray(a_tiles)
    Bw = np.asarray(ins["L3_ffn_Bw"])
    bw = np.zeros((8, 64, 8 * 128), np.float32)
    for ot in range(8):
        for e in range(8):
            bw[ot, :, e * 128:(e + 1) * 128] = Bw[e, ot * 128:(ot + 1) * 128, :].T
    out["bw_strips"] = bw
    out["wo_strips"] = host_tile_w(np.ascontiguousarray(np.asarray(ins["L3_attn_cproj_w"]).T))
    out["bo_cols"] = np.asarray(ins["L3_attn_cproj_b"]).reshape(NT, 128).T.copy()
    return out



RG_PAIRS = [[0, 1], [2, 3], [4, 5], [6, 7]]


def build_fused():
    nc = bass.Bass("TRN2", target_bir_lowering=False)
    es = ExitStack()
    C = Ctx(nc, es)
    x_in = C.din("x_own", [D, NTOK], F32)
    x_out = C.dout("x_next", [D, NTOK], F32)
    x_cur = nc.dram_tensor("x_cur", [D, NTOK], F32, kind="Internal").ap()
    xn_gin = nc.dram_tensor("xn_gin", [D, NTOK], BF16, kind="Internal").ap()
    xn_gout = nc.dram_tensor("xn_gout", [2 * D, NTOK], BF16, kind="Internal").ap()
    u_in = nc.dram_tensor("u_in", [2 * D, NTOK], BF16, kind="Internal").ap()
    u_out = nc.dram_tensor("u_out", [D, NTOK], BF16, kind="Internal").ap()
    mid_dram = nc.dram_tensor("mid_bounce", [64, 8 * NTOK], BF16, kind="Internal").ap()

    cn = Consts(C)
    cn.load(C)
    C.barrier()
    C.shared = cn
    sh = cn
    WINS = {}
    for li in range(4):
        C.prefix = "L%d_" % li
        WINS[li] = host_wins_declare(C, li)
        WINS[li]["wo_my"] = C.din("wo_my", [NT, 128, 4 * 128], F32)
        WINS[li]["bo"] = C.din("bo_cols", [128, NT], F32)
    C.prefix = ""

    for li in range(4):
        wins = WINS[li]
        st1 = ExitStack()
        X = C.sbuf("X", [128, NT * NTOK], F32, st1)
        XN_OWN = C.sbuf("XN_OWN", [128, NT * NTOK], BF16, st1)
        xld = []
        src = x_in if li == 0 else x_cur
        for ct in range(NT):
            xld.append(C.dma("gpsimd", X[:, ct * NTOK:(ct + 1) * NTOK],
                             src[ct * 128:(ct + 1) * 128, :], key="x",
                             waits=C.fence() if li > 0 else []))

        def x_own(ct, t0, t1, X=X):
            return X[:, ct * NTOK + t0: ct * NTOK + t1]

        def xn_own(ct, t0, t1, XN_OWN=XN_OWN):
            return XN_OWN[:, ct * NTOK + t0: ct * NTOK + t1]

        lns1 = ExitStack()
        alloc_lns(C, lns1)
        ln_dep = emit_ln_chunks(C, lambda ct, o, n, X=X: X[:, ct * NTOK + o:ct * NTOK + o + n],
                                lambda ct, o, n, XN_OWN=XN_OWN: XN_OWN[:, ct * NTOK + o:ct * NTOK + o + n],
                                NTOK // LCH, deps=[cn.dep[-1]] + xld)
        lns1.close()
        C.barrier()
        gi = []
        for ct in range(NT):
            gi.append(C.dma("gpsimd", xn_gin[ct * 128:(ct + 1) * 128, :],
                            XN_OWN[:, ct * NTOK:(ct + 1) * NTOK], key="gi",
                            waits=[ln_dep]))
        ag = C.cc("AllGather", ALU.bypass, RG_PAIRS, [xn_gin], [xn_gout],
                  waits=gi, key="ag")
        if li == 3:
            m3 = ExitStack()
            MIDs = C.sbuf("MIDstage", [64, NTOK], BF16, m3)
            pre = emit_ffn3_pre(C, cn, xn_own, x_own, MIDs, wins, deps=[ln_dep],
                                dram=(mid_dram, C))
            m3.close()
            p3 = ExitStack()
            cls_buf = C.sbuf("cls3", [128, NT], BF16, p3)
            cl = []
            for ct in range(NT):
                cl.append(C.dma("gpsimd", cls_buf[:, ct:ct + 1],
                                xn_gout[ct * 128:ct * 128 + 128, 0:1], key="cl",
                                waits=[ag] + pre, allow_slow_non_contiguous=True))
            fp = emit_ffn3_post(C, cn, cls_buf, x_own, mid_dram, wins,
                                deps=cl + pre)
            p3.close()
            ffn_dep = fp
        else:
            ffn_dep = emit_ffn(C, li, xn_own, x_own, wins, deps=[ln_dep])
        vend0 = C.mark("vector")
        for ct in range(NT):
            C.dma("gpsimd", x_cur[ct * 128:(ct + 1) * 128, :], X[:, ct * NTOK:(ct + 1) * NTOK],
                  waits=[vend0, ffn_dep], key="xs")
        st1.close()
        C.barrier()

        st2 = ExitStack()
        ATT = C.sbuf("ATT", [128, 4 * SEQ], BF16, st2)
        sub = ExitStack()
        XNF = C.sbuf("XNF", [128, NT * SEQ], BF16, sub)
        xfl = []
        for ct in range(NT):
            for half in range(2):
                xfl.append(C.dma("gpsimd", XNF[:, ct * SEQ + half * NTOK: ct * SEQ + (half + 1) * NTOK],
                                 xn_gout[half * D + ct * 128: half * D + (ct + 1) * 128, :],
                                 key="xf", waits=[ag]))
        xnf_dep = list(xfl[-1:])
        if li == 1:
            lns2 = ExitStack()
            alloc_lns(C, lns2)
            gcol = C.sbuf("gcol", [128, NT], F32, lns2)
            bcol = C.sbuf("bcol", [128, NT], F32, lns2)
            dg = C.dma("sync", gcol[:], wins["gamma"], key="c1")
            db = C.dma("sync", bcol[:], wins["beta"], key="c2")
            emit_ln_chunks(C, lambda ct, o, n, XNF=XNF: XNF[:, ct * SEQ + o:ct * SEQ + o + n],
                           lambda ct, o, n, XNF=XNF: XNF[:, ct * SEQ + o:ct * SEQ + o + n],
                           SEQ // LCH, deps=[C.mark("vector"), dg, db] + xnf_dep,
                           affine=(lambda ct, gcol=gcol: gcol[:, ct:ct + 1],
                                   lambda ct, bcol=bcol: bcol[:, ct:ct + 1]))
            lns2.close()
            xnf_dep = [C.mark("vector")]

        def xnf(ct, t0, t1, XNF=XNF):
            return XNF[:, ct * SEQ + t0: ct * SEQ + t1]
        if li == 3:
            emit_attn_l3(C, cn, xnf,
                         lambda g, t0, t1, ATT=ATT: ATT[:, g * SEQ + t0: g * SEQ + t1],
                         wins, deps=xnf_dep + xfl)
        else:
            emit_attn(C, li, xnf,
                      lambda g, h, s, ATT=ATT: ATT[64 * h:64 * h + 64, g * SEQ + s * 512: g * SEQ + (s + 1) * 512],
                      wins, deps=xnf_dep + xfl)
        att_dep = [C.mark("vector"), C.mark("scalar")]
        sub.close()
        C.barrier()
        ust = ExitStack()
        wst = [C.sbuf("uw", [128, 4 * 128], F32, ust) for _ in range(2)]
        wbf = [C.sbuf("uwb", [128, 4 * 128], BF16, ust) for _ in range(2)]
        urow = [C.sbuf("urow", [128, 512], BF16, ust) for _ in range(2)]
        psu = [C.psum("psu", [128, 512], F32, ust) for _ in range(2)]
        wcast = [None, None]
        ud = []
        for ot in range(NT):
            b = ot % 2
            dd = C.dma("sync", wst[b][:], wins["wo_my"][ot], key="w%d" % b,
                       waits=[wcast[b]] if wcast[b] else [C.mark("tensor"), C.mark("vector")])
            wcast[b] = C.op("vector", lambda e, b=b, wbf=wbf, wst=wst: e.tensor_copy(
                wbf[b][:], wst[b][:]),
                waits=[dd, C.mark("tensor")], inc=True)
            for sg in range(NSG):
                pp = psu[sg % 2]
                for g4 in range(4):
                    C.op("tensor", lambda e, g4=g4, sg=sg, b=b, pp=pp, ATT=ATT, wbf=wbf: e.matmul(
                        pp[:, 0:512], wbf[b][:, g4 * 128:(g4 + 1) * 128],
                        ATT[:, g4 * SEQ + sg * 512:g4 * SEQ + (sg + 1) * 512],
                        start=(g4 == 0), stop=(g4 == 3)),
                        waits=[wcast[b], C.mark("scalar")] + att_dep if g4 == 0 else (),
                        inc=(g4 == 3))
                mmd = C.mark("tensor")
                uiw = (C.dsem["ui"], C.dsem["ui"].n) if "ui" in C.dsem else None
                ue = C.op("scalar", lambda e, sg=sg, pp=pp, urow=urow: e.copy(
                    urow[sg % 2][:], pp[:, 0:512]),
                          waits=[mmd, uiw], inc=True)
                half = (sg * 512) // NTOK
                tg = (sg * 512 - half * NTOK) // 512
                ud.append(C.dma("gpsimd", u_in[half * D + ot * 128: half * D + (ot + 1) * 128,
                                               tg * 512:(tg + 1) * 512],
                                urow[sg % 2][:], key="ui", waits=[ue]))
        ust.close()
        st2.close()
        C.barrier()
        rs = C.cc("ReduceScatter", ALU.add, RG_PAIRS, [u_in], [u_out],
                  waits=ud, key="rs")
        st3 = ExitStack()
        X2 = C.sbuf("X2", [128, NT * NTOK], F32, st3)
        R = C.sbuf("R", [128, NT * NTOK], BF16, st3)
        bocol = C.sbuf("bocol", [128, NT], F32, st3)
        dbo = C.dma("sync", bocol[:], wins["bo"], key="c0", waits=C.fence())
        x2l, rl = [], []
        for ct in range(NT):
            x2l.append(C.dma("gpsimd", X2[:, ct * NTOK:(ct + 1) * NTOK],
                             x_cur[ct * 128:(ct + 1) * 128, :], key="x2",
                             waits=[(C.dsem["xs"], C.dsem["xs"].n)]))
            rl.append(C.dma("gpsimd", R[:, ct * NTOK:(ct + 1) * NTOK],
                            u_out[ct * 128:(ct + 1) * 128, :], key="r", waits=[rs]))
        dst = x_out if li == 3 else x_cur
        for ct in range(NT):
            d1 = C.op("vector", lambda e, ct=ct, X2=X2, R=R, bocol=bocol: e.scalar_tensor_tensor(
                X2[:, ct * NTOK:(ct + 1) * NTOK], R[:, ct * NTOK:(ct + 1) * NTOK],
                bocol[:, ct:ct + 1], X2[:, ct * NTOK:(ct + 1) * NTOK], ALU.add, ALU.add),
                waits=[x2l[-1], rl[-1], dbo], inc=True)
            C.dma("gpsimd", dst[ct * 128:(ct + 1) * 128, :], X2[:, ct * NTOK:(ct + 1) * NTOK],
                  waits=[d1], key="out" if li == 3 else "xs")
        st3.close()
        C.barrier()
    fin = (C.dsem["out"], C.dsem["out"].n)
    C.op("gpsimd", lambda e: e.nop(), waits=[fin])
    C.replay()
    es.close()
    return nc, C



_PREP = {0: prep_l0, 1: prep_l1, 2: prep_l2, 3: prep_l3}


def kernel(**inputs):
    set_size(4096)
    consts = host_consts()
    x = np.asarray(inputs["L0_attn_x"], dtype=np.float32)
    nc, C = build_fused()
    in_maps = []
    for c in range(8):
        r, hp = c // 2, c % 2
        d = dict(consts)
        for li in range(4):
            p = _PREP[li](inputs, hp)
            if li == 1:
                p["wo_strips"] = host_tile_w(np.eye(D, dtype=np.float32))
                p["bo_cols"] = np.zeros((128, NT), np.float32)
            p["wo_my"] = np.ascontiguousarray(p["wo_strips"][:, :, hp * 512:(hp + 1) * 512])
            for k, v in p.items():
                d["L%d_%s" % (li, k)] = v
        d = {k: v for k, v in d.items() if k in C.inputs}
        d["x_own"] = np.ascontiguousarray(x[r, hp * NTOK:(hp + 1) * NTOK, :].T)
        in_maps.append(d)
    res = run_bass_kernel_spmd(nc, in_maps, core_ids=list(range(8))).results
    out = np.empty((4, SEQ, D), np.float32)
    for c in range(8):
        r, hp = c // 2, c % 2
        out[r, hp * NTOK:(hp + 1) * NTOK, :] = np.asarray(res[c]["x_next"]).T
    return out


# version 03
# speedup vs baseline: 1.8600x; 1.8600x over previous
import numpy as np
import ml_dtypes
import concourse.bass as bass
import concourse.mybir as mybir
from concourse.bass_utils import run_bass_kernel_spmd
from contextlib import ExitStack

F32 = mybir.dt.float32
BF16 = mybir.dt.bfloat16
AF = mybir.ActivationFunctionType
ALU = mybir.AluOpType
AXX = mybir.AxisListType.X
BFNP = ml_dtypes.bfloat16

D = 1024
SEQ = 4096
LCH = 512
NTOK = 2048
NT = D // 128
TG = 512
NTG = NTOK // TG
NSG = SEQ // TG
NKT = SEQ // 128
NHEAD_OWN = 8


class Sem:
    def __init__(self, handle):
        self.h = handle
        self.n = 0

    def tick(self, k=1):
        self.n += k
        return self.n


class Ctx:
    ENGINES = ("sync", "gpsimd", "scalar", "vector", "tensor")

    def __init__(self, nc, es):
        self.nc = nc
        self.es = es
        self.q = {e: [] for e in self.ENGINES}
        self.sem = {e: Sem(es.enter_context(nc.semaphore("s_" + e))) for e in self.ENGINES}
        self.hw = {e: {} for e in self.ENGINES}
        self.uid = 0
        self.inputs = {}
        self.dsem = {}

    def _emit(self, eng, fn, waits, inc):
        hw = self.hw[eng]
        best = {}
        for s, v in waits:
            if s is None:
                continue
            if v > best.get(id(s), (None, 0))[1]:
                best[id(s)] = (s, v)
        waits2 = []
        for s, v in best.values():
            if v > 0 and hw.get(id(s), 0) < v:
                waits2.append((s, v))
                hw[id(s)] = v

        def thunk(e, fn=fn, waits2=waits2, inc=inc):
            for s, v in waits2:
                e.wait_ge(s.h, v)
            ins = fn(e)
            if inc is not None and ins is not None:
                ins.then_inc(inc[0].h, inc[1])
        self.q[eng].append(thunk)

    def op(self, eng, fn, waits=(), inc=False, inc_amt=1):
        s = self.sem[eng]
        incpair = None
        ret = None
        if inc:
            v = s.tick(inc_amt)
            incpair = (s, inc_amt)
            ret = (s, v)
        self._emit(eng, fn, [w for w in waits if w], incpair)
        return ret

    def cc(self, kind, op, rg, ins, outs, waits=(), key="cc"):
        if key not in self.dsem:
            self.dsem[key] = Sem(self.es.enter_context(self.nc.semaphore("d_" + key)))
        s = self.dsem[key]
        v = s.tick(1)
        self._emit("gpsimd", lambda e: e.collective_compute(
            kind, op, replica_groups=rg, ins=[a.opt() for a in ins],
            outs=[a.opt() for a in outs]), [w for w in waits if w], (s, 1))
        return (s, v)

    def dma(self, eng, out, in_, waits=(), key="misc", **kw):
        if key not in self.dsem:
            self.dsem[key] = Sem(self.es.enter_context(self.nc.semaphore("d_" + key)))
        s = self.dsem[key]
        v = s.tick(16)
        self._emit(eng, lambda e: e.dma_start(out=out, in_=in_, **kw), [w for w in waits if w], (s, 16))
        return (s, v)

    def mark(self, eng):
        return (self.sem[eng], self.sem[eng].n)

    def all_marks(self):
        return [self.mark(e) for e in self.ENGINES if self.sem[e].n > 0]

    def fence(self):
        f = [self.mark(e) for e in self.ENGINES if self.sem[e].n > 0]
        f += [(s, s.n) for s in self.dsem.values() if s.n > 0]
        return f

    def barrier(self):
        f = self.fence()
        for e in self.ENGINES:
            self._emit(e, lambda en: en.nop(), f, None)

    def din(self, name, shape, dtype):
        name = getattr(self, "prefix", "") + name
        self.inputs[name] = (tuple(shape), dtype)
        return self.nc.dram_tensor(name, list(shape), dtype, kind="ExternalInput").ap()

    def dout(self, name, shape, dtype):
        return self.nc.dram_tensor(name, list(shape), dtype, kind="ExternalOutput").ap()

    def sbuf(self, name, shape, dtype, stack=None):
        self.uid += 1
        return (stack or self.es).enter_context(
            self.nc.sbuf_tensor(f"{name}_{self.uid}", shape, dtype))

    def psum(self, name, shape, dtype, stack=None):
        self.uid += 1
        return (stack or self.es).enter_context(
            self.nc.psum_tensor(f"{name}_{self.uid}", shape, dtype))

    def replay(self):
        with self.nc.allow_low_precision(reason="intentional bf16 matmul operand prep; fp32 accumulation kept"), \
             self.nc.Block() as block:
            for eng in self.ENGINES:
                thunks = self.q[eng]
                if not thunks:
                    continue
                getattr(block, eng)(lambda e, thunks=thunks: [t(e) for t in thunks])



class Shared:

    def __init__(self, C):
        self.C = C
        self.ones_col = C.sbuf("ones_col", [128, 1], BF16)
        self.onesb_row = C.sbuf("onesb_row", [1, 128], BF16)
        self.ident = C.sbuf("ident", [128, 128], BF16)
        self.trimask = C.sbuf("trimask", [128, 1152], BF16)
        self.sel8 = C.sbuf("sel8", [8, 8 * 128], BF16)
        self.xb = C.sbuf("ln_xb", [128, NT * SEQ // 2], BF16)
        self.sq = C.sbuf("ln_sq", [128, NT * SEQ // 2], BF16)
        self.musd = C.sbuf("ln_musd", [4, SEQ], F32)
        self.mu_bf = C.sbuf("ln_mubf", [1, SEQ], BF16)
        self.rs_bf = C.sbuf("ln_rsbf", [1, SEQ], BF16)
        self.mu_b = C.sbuf("ln_mub", [128, SEQ], BF16)
        self.rs_b = C.sbuf("ln_rsb", [128, SEQ], BF16)

    def load(self, C):
        ident = C.din("c_ident", [128, 128], BF16)
        trimask = C.din("c_trimask", [128, 1152], BF16)
        sel8 = C.din("c_sel8", [8, 8 * 128], BF16)
        d1 = C.dma("sync", self.ident[:], ident, key="const")
        d2 = C.dma("sync", self.trimask[:], trimask, key="const")
        d2b = C.dma("sync", self.sel8[:], sel8, key="const")
        d3 = C.op("vector", lambda e: e.memset(self.ones_col[:], 1.0), inc=True)
        d4 = C.op("vector", lambda e: e.memset(self.onesb_row[:], 1.0), inc=True)
        self.dep = [d1, d2, d3, d4]


def host_consts():
    tri = np.zeros((128, 1152), np.float32)
    for m in range(128):
        tri[m, m + 640:] = 1.0
    sel = np.zeros((8, 8 * 128), np.float32)
    for e in range(8):
        sel[e, e * 128:(e + 1) * 128] = 1.0
    return {
        "c_ident": np.eye(128, dtype=np.float32).astype(BFNP),
        "c_trimask": tri.astype(BFNP),
        "c_sel8": sel.astype(BFNP),
    }



def emit_layernorm(C, sh, x_tiles, xn_tiles, n, off=0, deps=(), eps=1e-5, affine=None):
    xb, sq, musd = sh.xb, sh.sq, sh.musd
    mu_bf, rs_bf, mu_b, rs_b = sh.mu_bf, sh.rs_bf, sh.mu_b, sh.rs_b
    ps_stat = sh.ps_stat
    ps_b = sh.ps_b

    for ct in range(NT):
        C.op("vector", lambda e, ct=ct: e.tensor_copy(xb[:, ct * n:(ct + 1) * n], x_tiles(ct)),
             waits=deps, inc=(ct == NT - 1))
        C.op("scalar", lambda e, ct=ct: e.activation(out=sq[:, ct * n:(ct + 1) * n],
                                                     in_=x_tiles(ct), func=AF.Square),
             waits=deps, inc=(ct == NT - 1))
    cv, cs = C.mark("vector"), C.mark("scalar")
    ngr = (n + TG - 1) // TG
    for g in range(ngr):
        w = min(TG, n - g * TG)
        for ct in range(NT):
            C.op("tensor", lambda e, ct=ct, g=g, w=w: e.matmul(
                ps_stat[0:1, g * TG:g * TG + w], sh.ones_col[:, 0:1],
                xb[:, ct * n + g * TG: ct * n + g * TG + w],
                start=(ct == 0), stop=(ct == NT - 1)),
                waits=[cv] if ct == 0 else (), inc=(ct == NT - 1))
        for ct in range(NT):
            C.op("tensor", lambda e, ct=ct, g=g, w=w: e.matmul(
                ps_stat[0:1, n + g * TG:n + g * TG + w], sh.ones_col[:, 0:1],
                sq[:, ct * n + g * TG: ct * n + g * TG + w],
                start=(ct == 0), stop=(ct == NT - 1)),
                waits=[cs] if ct == 0 else (), inc=(ct == NT - 1))
    mm = C.mark("tensor")
    o = off
    S0 = musd[0:1, 0:n]
    S1 = musd[0:1, n:2 * n]
    S2 = musd[0:1, 2 * n:3 * n]
    S3 = musd[0:1, 3 * n:4 * n]
    d = C.op("vector", lambda e: e.tensor_scalar(S2, ps_stat[0:1, 0:n], 1.0 / D, None, ALU.mult),
             waits=[mm], inc=True)
    d2 = C.op("vector", lambda e: e.tensor_scalar(S1, ps_stat[0:1, n:2 * n], 1.0 / D, None, ALU.mult),
              waits=[mm], inc=True)
    d3 = C.op("vector", lambda e: e.tensor_tensor(S0, S2, S2, ALU.mult),
              waits=[d], inc=True)
    d4 = C.op("vector", lambda e: e.tensor_tensor(S1, S1, S0, ALU.subtract),
              waits=[d2, d3], inc=True)
    d5 = C.op("vector", lambda e: e.tensor_scalar(S1, S1, float(eps), None, ALU.add),
              waits=[d4], inc=True)
    d6 = C.op("scalar", lambda e: e.activation(out=S3, in_=S1, func=AF.Sqrt),
              waits=[d5], inc=True)
    d7 = C.op("vector", lambda e: e.reciprocal(rs_bf[:, 0:n], S3), waits=[d6], inc=True)
    d8 = C.op("vector", lambda e: e.tensor_copy(mu_bf[:, 0:n], S2), waits=[d3], inc=True)
    for g in range(ngr):
        w = min(TG, n - g * TG)
        C.op("tensor", lambda e, g=g, w=w: e.matmul(ps_b[:, g * TG:g * TG + w], sh.onesb_row[0:1, :],
                                                    mu_bf[0:1, g * TG:g * TG + w], start=True, stop=True),
             waits=[d8], inc=(g == ngr - 1))
    bm = C.mark("tensor")
    bme = C.op("vector", lambda e: e.tensor_copy(mu_b[:, 0:n], ps_b[:, 0:n]), waits=[bm], inc=True)
    for g in range(ngr):
        w = min(TG, n - g * TG)
        C.op("tensor", lambda e, g=g, w=w: e.matmul(ps_b[:, g * TG:g * TG + w], sh.onesb_row[0:1, :],
                                                    rs_bf[0:1, g * TG:g * TG + w], start=True, stop=True),
             waits=[d7, bme], inc=(g == ngr - 1))
    br = C.mark("tensor")
    bre = C.op("vector", lambda e: e.tensor_copy(rs_b[:, 0:n], ps_b[:, 0:n]), waits=[br], inc=True)
    last = None
    for ct in range(NT):
        da = C.op("vector", lambda e, ct=ct: e.tensor_tensor(xb[:, ct * n:(ct + 1) * n], x_tiles(ct),
                                                             mu_b[:, 0:n], ALU.subtract),
                  waits=[bme], inc=True)
        if affine is None:
            last = C.op("vector", lambda e, ct=ct: e.tensor_tensor(xn_tiles(ct), xb[:, ct * n:(ct + 1) * n],
                                                                   rs_b[:, 0:n], ALU.mult),
                        waits=[da, bre], inc=True)
        else:
            gcol, bcol = affine
            db = C.op("vector", lambda e, ct=ct: e.tensor_tensor(sq[:, ct * n:(ct + 1) * n],
                                                                 xb[:, ct * n:(ct + 1) * n],
                                                                 rs_b[:, 0:n], ALU.mult),
                      waits=[da, bre, cs, mm], inc=True)
            last = C.op("vector", lambda e, ct=ct: e.tensor_scalar(
                xn_tiles(ct), sq[:, ct * n:(ct + 1) * n], gcol(ct), bcol(ct), ALU.mult, ALU.add),
                waits=[db], inc=True)
    return last



def emit_stream_linear(C, wst, n_ct, n_ot, rhs_fn, tok_ranges, evac, wbufs, psums,
                       rhs_deps=(), ot_list=None, w_waits=()):
    stage, wbf = wbufs
    nps = len(psums)
    free_dep = [None] * nps
    cast_done = [None, None]
    last_mm_w = [None, None]
    ots = list(range(n_ot)) if ot_list is None else ot_list
    slot = 0
    for oi, ot in enumerate(ots):
        b = oi % 2
        dd = C.dma("sync", stage[b][:], wst[ot],
                   waits=[cast_done[b]] if cast_done[b] else list(w_waits), key="w%d" % b)
        cast_done[b] = C.op("vector", lambda e, b=b: e.tensor_copy(wbf[b][:], stage[b][:]),
                            waits=[dd] + ([last_mm_w[b]] if last_mm_w[b] else []), inc=True)
        for (t0, t1) in tok_ranges:
            ps = psums[slot % nps]
            mmw = [cast_done[b]] + list(rhs_deps)
            if free_dep[slot % nps] is not None:
                mmw.append(free_dep[slot % nps])
            for ct in range(n_ct):
                C.op("tensor", lambda e, ct=ct, t0=t0, t1=t1, b=b, ps=ps: e.matmul(
                    ps[:, 0:t1 - t0], wbf[b][:, ct * 128:(ct + 1) * 128],
                    rhs_fn(ct, t0, t1),
                    start=(ct == 0), stop=(ct == n_ct - 1)),
                    waits=mmw if ct == 0 else (), inc=(ct == n_ct - 1))
            mm_dep = C.mark("tensor")
            last_mm_w[b] = mm_dep
            free_dep[slot % nps] = evac(ot, t0, t1, ps, mm_dep)
            slot += 1
    return C.mark("tensor")


def host_tile_w(wT, n_ct=None, n_ot=None):
    ci, co = wT.shape
    n_ct = n_ct or ci // 128
    n_ot = n_ot or co // 128
    t = wT.reshape(n_ct, 128, n_ot, 128).transpose(2, 1, 0, 3).reshape(n_ot, 128, n_ct * 128)
    return np.ascontiguousarray(t)



def emit_attn_group(C, sh, ab, heads, score_chunks, vtok_ap, av_rows, out_ap,
                    exp_scale, pre_exp=None, deps=()):
    nh = len(heads)
    exp_dep = {}
    probs_src = {}
    av_free = {h: None for h in heads}
    for s in range(NSG):
        nkt = 4 * s + 4
        score_dep = {}
        for kt in range(nkt):
            for h in heads:
                r = kt % 3
                w = []
                if kt >= 3:
                    w.append(exp_dep[(h, kt - 3)])
                if kt == 0 and av_free[h] is not None:
                    w.append(av_free[h])
                w += list(deps)
                chunks = score_chunks(h)
                for ci, (k_fn, q_fn, trow) in enumerate(chunks):
                    kap, qap, oap = k_fn(kt), q_fn(s), ab["sc"][h][r][:, 0:512]
                    C.op("tensor", lambda e, kap=kap, qap=qap, oap=oap, trow=trow,
                         st=(ci == 0), sp=(ci == len(chunks) - 1): e.matmul(
                             oap, kap, qap,
                             start=st, stop=sp, tile_position=(trow, 0)),
                         waits=w if ci == 0 else (), inc=(ci == len(chunks) - 1))
                score_dep[(h, kt)] = C.mark("tensor")
            if kt >= 2:
                for h in heads:
                    vap, pap = vtok_ap(h, kt - 2), probs_src[(h, kt - 2)]
                    C.op("tensor", lambda e, vap=vap, pap=pap, h=h,
                         st=(kt - 2 == 0): e.matmul(
                        ab["av"][h][0:av_rows, 0:512], vap, pap,
                        start=st, stop=False),
                        waits=[exp_dep[(h, kt - 2)]], inc=True)
            for h in heads:
                r = kt % 3
                pb = ab["probs"][h][:, r * 512:(r + 1) * 512]
                src = ab["sc"][h][r][:, 0:512]
                dep = score_dep[(h, kt)]
                if pre_exp is not None:
                    src, dep = pre_exp(h, s, kt, src, dep)
                dd = C.op("scalar", lambda e, pb=pb, src=src: e.activation(
                    out=pb, in_=src, func=AF.Exp, scale=float(exp_scale)),
                    waits=[dep], inc=True)
                if kt >= 4 * s:
                    c0 = 640 + (512 * s - 128 * kt)
                    dd = C.op("vector", lambda e, pb=pb, c0=c0: e.tensor_tensor(
                        pb, pb, sh.trimask[:, c0:c0 + 512], ALU.mult),
                        waits=[dd], inc=True)
                exp_dep[(h, kt)] = dd
                probs_src[(h, kt)] = pb
        for h in heads:
            for ktf in ([nkt - 2] if nkt >= 2 else []) + [nkt - 1]:
                vap, pap = vtok_ap(h, ktf), probs_src[(h, ktf)]
                C.op("tensor", lambda e, vap=vap, pap=pap, h=h, st=(ktf == 0),
                     sp=(ktf == nkt - 1): e.matmul(
                    ab["av"][h][0:av_rows, 0:512], vap, pap,
                    start=st, stop=sp),
                    waits=[exp_dep[(h, ktf)]], inc=True)
            avd = C.mark("tensor")
            dv = av_rows - 1
            d1 = C.op("scalar", lambda e, h=h: e.copy(ab["sums"][h][:, 0:512],
                                                      ab["av"][h][dv:dv + 1, 0:512]),
                      waits=[avd], inc=True)
            d2 = C.op("scalar", lambda e, h=h: e.copy(ab["att_sb"][h][0:dv, 0:512],
                                                      ab["av"][h][0:dv, 0:512]),
                      waits=[avd], inc=True)
            d3 = C.op("vector", lambda e, h=h: e.reciprocal(ab["recip"][h][:, 0:512],
                                                            ab["sums"][h][:, 0:512]),
                      waits=[d1], inc=True)
            d4 = C.op("tensor", lambda e, h=h, dv=dv: e.matmul(
                ab["av"][h][0:dv, 0:512], sh.onesb_row[0:1, 0:dv],
                ab["recip"][h][0:1, 0:512], start=True, stop=True),
                waits=[d2, d3], inc=True)
            oap = out_ap(h, s)
            d5 = C.op("vector", lambda e, oap=oap, h=h, dv=dv: e.tensor_tensor(
                oap, ab["att_sb"][h][0:dv, 0:512],
                ab["av"][h][0:dv, 0:512], ALU.mult),
                waits=[d4], inc=True)
            av_free[h] = d5
    return C.mark("tensor")


def make_attn_bufs(C, st, heads, av_rows):
    ab = {"sc": {}, "av": {}, "probs": {}, "att_sb": {}, "sums": {}, "recip": {}}
    for h in heads:
        ab["sc"][h] = [C.psum("sc", [128, 512], F32, st) for _ in range(3)]
        ab["av"][h] = C.psum("av", [128, 512], F32, st)
        ab["probs"][h] = C.sbuf("probs", [128, 3 * 512], BF16, st)
        ab["att_sb"][h] = C.sbuf("attsb", [av_rows - 1, 512], BF16, st)
        ab["sums"][h] = C.sbuf("sums", [1, 512], F32, st)
        ab["recip"][h] = C.sbuf("recip", [1, 512], BF16, st)
    return ab


def emit_vtok(C, sh, v_cm_rows_fn, vtok, n_heads2, deps=()):
    vst = ExitStack()
    ptr = C.psum("ptr", [128, 128], BF16, vst)
    d_ones = []
    for h in range(n_heads2):
        d_ones.append(C.op("vector", lambda e, h=h: e.memset(
            vtok[h].rearrange("p (k c) -> p k c", c=65)[:, :, 64:65], 1.0), inc=True))
    last = None
    for kt in range(NKT):
        vap = v_cm_rows_fn(kt)
        dt = C.op("tensor", lambda e, vap=vap: e.transpose(
            ptr[:], vap, sh.ident[:]),
            waits=list(deps) + ([last] if last else []), inc=True)
        cps = []
        for h in range(n_heads2):
            cps.append(C.op("scalar", lambda e, kt=kt, h=h: e.copy(
                vtok[h][:, kt * 65:kt * 65 + 64], ptr[:, h * 64:(h + 1) * 64]),
                waits=[dt], inc=True))
        last = cps[-1]
    vst.close()
    return [last] + d_ones



def emit_ffn1(C, sh, xn_own_tiles, x_tiles, win, deps=()):
    bdst = ExitStack()
    BD = [C.sbuf("BD", [128, 32 * 128], BF16, bdst) for _ in range(4)]
    bst = ExitStack()
    rm_sb = C.sbuf("rmsb", [16, 8 * 128], BF16, bst)
    rm_f = C.sbuf("rmf", [16, 8 * 128], F32, bst)
    rup_sb = C.sbuf("rupsb", [16, 8 * 64], BF16, bst)
    rup_f = C.sbuf("rupf", [16, 8 * 64], F32, bst)
    ps_bd = C.psum("psbd", [128, 128], F32, bst)
    bd_dep = []
    prev_cp = None
    c1 = c2 = None
    for h in range(4):
        for t in range(8):
            dd1 = C.dma("sync", rm_f[:], win["rm_pad"][h][:, t * 1024:(t + 1) * 1024],
                        key="w0", waits=[c1] if c1 else [])
            dd2 = C.dma("sync", rup_f[:], win["rup"][h][:, t * 512:(t + 1) * 512],
                        key="w1", waits=[c2] if c2 else [])
            c1 = C.op("vector", lambda e: e.tensor_copy(rm_sb[:], rm_f[:]),
                      waits=[dd1] + ([prev_cp] if prev_cp else []), inc=True)
            c2 = C.op("vector", lambda e: e.tensor_copy(rup_sb[:], rup_f[:]), waits=[dd2], inc=True)
            for cp in range(4):
                for wi in range(2):
                    u = 2 * cp + wi
                    C.op("tensor", lambda e, u=u, wi=wi: e.matmul(
                        ps_bd[:, wi * 64:(wi + 1) * 64],
                        rm_sb[:, u * 128:(u + 1) * 128],
                        rup_sb[:, u * 64:(u + 1) * 64],
                        start=True, stop=True),
                        waits=[c1, c2] + ([prev_cp] if prev_cp else []), inc=(wi == 1))
                mmd = C.mark("tensor")
                prev_cp = C.op("vector", lambda e, h=h, t=t, cp=cp: e.tensor_copy(
                    BD[h][:, (t * 4 + cp) * 128:(t * 4 + cp + 1) * 128], ps_bd[:]),
                    waits=[mmd], inc=True)
        bd_dep.append(prev_cp)
    bst.close()
    C.barrier()

    fst = ExitStack()
    zbuf = C.sbuf("zbuf", [128, 32 * 512], BF16, fst)
    wstage = [C.sbuf("cpw", [128, 8 * 128], F32, fst) for _ in range(2)]
    wbf = [C.sbuf("cpwb", [128, 8 * 128], BF16, fst) for _ in range(2)]
    ps_z = [C.psum("psz", [128, 512], F32, fst) for _ in range(2)]
    ps_o = [C.psum("pso", [128, 512], F32, fst) for _ in range(4)]
    out_free = [None] * 4
    wcast = [None, None]
    wlast = [None, None]
    z_free = [None, None]
    bidx = 0
    last = None
    for h in range(4):
        for tc in range(NTG):
            t0, t1 = tc * 512, (tc + 1) * 512
            for zt in range(32):
                t = zt // 4
                zp = ps_z[zt % 2]
                C.op("tensor", lambda e, zt=zt, t=t, zp=zp, h=h, t0=t0, t1=t1: e.matmul(
                    zp[:, 0:512], BD[h][:, zt * 128:(zt + 1) * 128],
                    xn_own_tiles(t, t0, t1), start=True, stop=True),
                    waits=[bd_dep[h]] + list(deps) +
                          ([z_free[zt % 2]] if z_free[zt % 2] else []), inc=True)
                zd = C.mark("tensor")
                z_free[zt % 2] = C.op("scalar", lambda e, zt=zt, zp=zp: e.activation(
                    out=zbuf[:, zt * 512:(zt + 1) * 512], in_=zp[:, 0:512], func=AF.Gelu),
                    waits=[zd], inc=True)
            zdone = z_free[31 % 2]
            for ot in range(8):
                po = ps_o[ot % 4]
                for ztg in range(4):
                    b = bidx % 2
                    bidx += 1
                    dd = C.dma("sync", wstage[b][:], win["cproj"][(h * 4 + ztg) * 8 + ot],
                               waits=[wcast[b]] if wcast[b] else [], key="w%d" % b)
                    wcast[b] = C.op("vector", lambda e, b=b: e.tensor_copy(wbf[b][:], wstage[b][:]),
                                    waits=[dd] + ([wlast[b]] if wlast[b] else []), inc=True)
                    mmw = [wcast[b], zdone]
                    if ztg == 0 and out_free[ot % 4] is not None:
                        mmw.append(out_free[ot % 4])
                    for z8 in range(8):
                        zt = ztg * 8 + z8
                        C.op("tensor", lambda e, z8=z8, zt=zt, po=po, b=b: e.matmul(
                            po[:, 0:512], wbf[b][:, z8 * 128:(z8 + 1) * 128],
                            zbuf[:, zt * 512:(zt + 1) * 512],
                            start=(zt == 0), stop=(zt == 31)),
                            waits=mmw if z8 == 0 else (), inc=(z8 == 7))
                    wlast[b] = C.mark("tensor")
                out_free[ot % 4] = C.op("vector", lambda e, ot=ot, po=po, t0=t0, t1=t1: e.tensor_tensor(
                    x_tiles(ot, t0, t1), x_tiles(ot, t0, t1), po[:, 0:512], ALU.add),
                    waits=[wlast[(bidx - 1) % 2]], inc=True)
                last = out_free[ot % 4]
    fst.close()
    bdst.close()
    return last



def emit_attn_l1(C, sh, xnf_tiles, att_tiles, win, deps=()):
    wst = ExitStack()
    wstage = [C.sbuf("aw", [128, 8 * 128], F32, wst) for _ in range(2)]
    wbf = [C.sbuf("awb", [128, 8 * 128], BF16, wst) for _ in range(2)]
    for g in range(4):
        gst = ExitStack()
        K_cm = C.sbuf("Kcm", [128, SEQ], BF16, gst)
        V_cm = C.sbuf("Vcm", [128, SEQ], BF16, gst)
        Q_cm = C.sbuf("Qcm", [128, SEQ], BF16, gst)
        vtok = [C.sbuf("vtok", [128, NKT * 65], BF16, gst) for _ in range(2)]
        pst = ExitStack()
        ps_p = [C.psum("pp", [128, 512], F32, pst) for _ in range(2)]
        dst = {0: K_cm, 1: V_cm, 2: Q_cm}
        toks = [(t0, t0 + 512) for t0 in range(0, SEQ, 512)]

        def mk_evac(dest):
            def evac(ot, t0, t1, ps, dep):
                return C.op("scalar", lambda e: e.copy(dest[:, t0:t1], ps[:, 0:t1 - t0]),
                            waits=[dep], inc=True)
            return evac
        for j, dest in dst.items():
            emit_stream_linear(C, win["qkv"], NT, 1, xnf_tiles, toks,
                               mk_evac(dest), (wstage, wbf), ps_p,
                               rhs_deps=list(deps) + [C.mark("scalar"), C.mark("vector")],
                               ot_list=[g * 3 + j],
                               w_waits=[C.mark("vector"), C.mark("tensor")])
        kvq_dep = [C.mark("scalar")]
        vt_dep = emit_vtok(C, sh,
                           lambda kt, V_cm=V_cm: V_cm[:, kt * 128:(kt + 1) * 128], vtok, 2,
                           deps=kvq_dep)
        pst.close()
        ab = make_attn_bufs(C, gst, [0, 1], 65)
        emit_attn_group(
            C, sh, ab, [0, 1],
            lambda h, K_cm=K_cm, Q_cm=Q_cm: [
                (lambda kt, h=h, K_cm=K_cm: K_cm[64 * h:64 * h + 64, kt * 128:(kt + 1) * 128],
                 lambda s, h=h, Q_cm=Q_cm: Q_cm[64 * h:64 * h + 64, s * 512:(s + 1) * 512],
                 64 * h)],
            lambda h, kt, vtok=vtok: vtok[h][:, kt * 65:kt * 65 + 65],
            65,
            lambda h, s, g=g: att_tiles(g, h, s),
            1.0 / 8.0,
            deps=kvq_dep + vt_dep)
        gst.close()
        C.barrier()
    wst.close()
    return C.mark("vector")



def alloc_lns(C, st):
    sh = C.shared
    sh.xb = C.sbuf("ln_xb", [128, NT * LCH], BF16, st)
    sh.sq = C.sbuf("ln_sq", [128, NT * LCH], BF16, st)
    sh.musd = C.sbuf("ln_musd", [1, 4 * LCH], F32, st)
    sh.mu_bf = C.sbuf("ln_mubf", [1, LCH], BF16, st)
    sh.rs_bf = C.sbuf("ln_rsbf", [1, LCH], BF16, st)
    sh.mu_b = C.sbuf("ln_mub", [128, LCH], BF16, st)
    sh.rs_b = C.sbuf("ln_rsb", [128, LCH], BF16, st)
    sh.ps_stat = C.psum("ps_stat", [1, 2 * LCH], F32, st)
    sh.ps_b = C.psum("ps_b", [128, LCH], F32, st)


class Consts:
    def __init__(self, C):
        self.ones_col = C.sbuf("ones_col", [128, 1], BF16)
        self.onesb_row = C.sbuf("onesb_row", [1, 128], BF16)
        self.ident = C.sbuf("ident", [128, 128], BF16)
        self.trimask = C.sbuf("trimask", [128, 1152], BF16)
        self.sel8 = C.sbuf("sel8", [8, 8 * 128], BF16)

    def load(self, C):
        ident = C.din("c_ident", [128, 128], BF16)
        trimask = C.din("c_trimask", [128, 1152], BF16)
        sel8 = C.din("c_sel8", [8, 8 * 128], BF16)
        d1 = C.dma("sync", self.ident[:], ident, key="const")
        d2 = C.dma("sync", self.trimask[:], trimask, key="const")
        d2b = C.dma("sync", self.sel8[:], sel8, key="const")
        d3 = C.op("vector", lambda e: e.memset(self.ones_col[:], 1.0), inc=True)
        d4 = C.op("vector", lambda e: e.memset(self.onesb_row[:], 1.0), inc=True)
        self.dep = [d1, d2, d3, d4]


def emit_ln_chunks(C, src_fn, dst_fn, nchunks, deps, affine=None, eps=1e-5):
    sh = C.shared
    last = None
    for c in range(nchunks):
        o = c * LCH
        last = emit_layernorm(C, sh,
                              lambda ct, o=o: src_fn(ct, o, LCH),
                              lambda ct, o=o: dst_fn(ct, o, LCH),
                              LCH,
                              deps=list(deps) + [C.mark("tensor"), C.mark("vector"), C.mark("scalar")],
                              affine=affine, eps=eps)
    return last


def build_A(li):
    if li == 3:
        return build_A3()
    nc = bass.Bass("TRN2", target_bir_lowering=False)
    es = ExitStack()
    C = Ctx(nc, es)
    x_own_in = C.din("x_own", [D, NTOK], F32)
    x_row_in = C.din("x_row", [D, SEQ], F32)
    x_mid = C.dout("x_mid", [D, NTOK], F32)
    att_out = C.dout("att", [4, 128, SEQ], BF16)

    cn = Consts(C)
    cn.load(C)
    C.barrier()
    C.shared = cn
    sh = cn

    xst = ExitStack()
    X = C.sbuf("X", [128, NT * NTOK], F32, xst)

    wins = host_wins_declare(C, li)

    xld = []
    for ct in range(NT):
        xld.append(C.dma("gpsimd", X[:, ct * NTOK:(ct + 1) * NTOK],
                         x_own_in[ct * 128:(ct + 1) * 128, :], key="x"))

    def x_own(ct, t0, t1):
        return X[:, ct * NTOK + t0: ct * NTOK + t1]

    ph1 = ExitStack()
    XN_OWN = C.sbuf("XN_OWN", [128, NT * NTOK], BF16, ph1)

    def xn_own(ct, t0, t1):
        return XN_OWN[:, ct * NTOK + t0: ct * NTOK + t1]

    lns1 = ExitStack()
    alloc_lns(C, lns1)
    ln_dep = emit_ln_chunks(C, lambda ct, o, n: X[:, ct * NTOK + o:ct * NTOK + o + n],
                            lambda ct, o, n: XN_OWN[:, ct * NTOK + o:ct * NTOK + o + n],
                            NTOK // LCH, deps=[cn.dep[-1]] + xld)
    lns1.close()
    C.barrier()
    ffn_dep = emit_ffn(C, li, xn_own, x_own, wins, deps=[ln_dep])
    ph1.close()
    vend0 = C.mark("vector")
    for ct in range(NT):
        C.dma("gpsimd", x_mid[ct * 128:(ct + 1) * 128, :], X[:, ct * NTOK:(ct + 1) * NTOK],
              waits=[vend0, ffn_dep], key="out")
    xst.close()
    C.barrier()

    ph2big = ExitStack()
    ATT = C.sbuf("ATT", [128, 4 * SEQ], BF16, ph2big)
    XNF = C.sbuf("XNF", [128, NT * SEQ], BF16, ph2big)
    ph2 = ExitStack()
    alloc_lns(C, ph2)
    XPART = C.sbuf("XPART", [128, NT * LCH], F32, ph2)
    for c in range(SEQ // LCH):
        o = c * LCH
        pld = [C.dma("gpsimd", XPART[:, ct * LCH:(ct + 1) * LCH],
                     x_row_in[ct * 128:(ct + 1) * 128, o:o + LCH],
                     waits=[C.mark("vector"), C.mark("scalar"), ffn_dep], key="xp")
               for ct in range(NT)]
        emit_layernorm(C, sh,
                       lambda ct: XPART[:, ct * LCH:(ct + 1) * LCH],
                       lambda ct, o=o: XNF[:, ct * SEQ + o: ct * SEQ + o + LCH],
                       LCH, deps=[cn.dep[-1], C.mark("tensor")] + pld)
    if li == 1:
        gcol = C.sbuf("gcol", [128, NT], F32, ph2)
        bcol = C.sbuf("bcol", [128, NT], F32, ph2)
        dg = C.dma("sync", gcol[:], wins["gamma"], key="c1")
        db = C.dma("sync", bcol[:], wins["beta"], key="c2")
        emit_ln_chunks(C, lambda ct, o, n: XNF[:, ct * SEQ + o:ct * SEQ + o + n],
                       lambda ct, o, n: XNF[:, ct * SEQ + o:ct * SEQ + o + n],
                       SEQ // LCH, deps=[C.mark("vector"), dg, db],
                       affine=(lambda ct: gcol[:, ct:ct + 1], lambda ct: bcol[:, ct:ct + 1]))
    xnf_dep = [C.mark("vector")]
    ph2.close()
    C.barrier()

    def xnf(ct, t0, t1):
        return XNF[:, ct * SEQ + t0: ct * SEQ + t1]


    emit_attn(C, li, xnf, lambda g, h, s: ATT[64 * h:64 * h + 64, g * SEQ + s * 512: g * SEQ + (s + 1) * 512],
              wins, deps=xnf_dep)

    vend = C.mark("vector")
    send = C.mark("scalar")
    for g4 in range(4):
        C.dma("gpsimd", att_out[g4], ATT[:, g4 * SEQ:(g4 + 1) * SEQ],
              waits=[vend, send], key="out")
    fin = (C.dsem["out"], C.dsem["out"].n)
    C.op("gpsimd", lambda e: e.nop(), waits=[fin])
    ph2big.close()
    C.replay()
    es.close()
    return nc, C


def host_wins_declare(C, li):
    wins = {}
    if li == 0:
        wins.update(_wins_l0(C))
    if li == 2:
        wins.update(_wins_l2(C))
    if li == 3:
        wins.update(_wins_l3(C))
    if li == 1:
        wins["qkv"] = C.din("w_qkv_strips", [12, 128, NT * 128], F32)
        wins["gamma"] = C.din("gamma_cols", [128, NT], F32)
        wins["beta"] = C.din("beta_cols", [128, NT], F32)
        wins["rm_pad"] = C.din("rm_pad", [4, 16, 64 * 128], F32)
        wins["rup"] = C.din("rup", [4, 16, 64 * 64], F32)
        wins["cproj"] = C.din("cproj_strips", [128, 128, NT * 128], F32)
    return wins


def emit_ffn(C, li, xn_own, x_own, wins, deps):
    if li == 0:
        return emit_ffn0(C, C.shared, xn_own, x_own, wins, deps=deps)
    if li == 1:
        return emit_ffn1(C, C.shared, xn_own, x_own, wins, deps=deps)
    if li == 2:
        return emit_ffn2(C, C.shared, xn_own, x_own, wins, deps=deps)
    if li == 3:
        return emit_ffn3(C, C.shared, xn_own, x_own, wins, deps=deps)
    raise NotImplementedError


def emit_attn(C, li, xnf, att_tiles, wins, deps):
    if li == 0:
        return emit_attn_l0(C, C.shared, xnf, att_tiles, wins, deps=deps)
    if li == 1:
        return emit_attn_l1(C, C.shared, xnf, att_tiles, wins, deps=deps)
    if li == 2:
        return emit_attn_l2(C, C.shared, xnf, att_tiles, wins, deps=deps)
    if li == 3:
        return emit_attn_l3(C, C.shared, xnf, att_tiles, wins, deps=deps)
    raise NotImplementedError



def build_B(li):
    nc = bass.Bass("TRN2", target_bir_lowering=False)
    es = ExitStack()
    C = Ctx(nc, es)
    x_mid_in = C.din("x_mid", [D, NTOK], F32)
    att_in = C.din("att_my", [D, NTOK], BF16)
    x_next = C.dout("x_next", [D, NTOK], F32)

    X = C.sbuf("X", [128, NT * NTOK], F32)
    A = C.sbuf("A", [128, NT * NTOK], BF16)
    xld, ald = [], []
    for ct in range(NT):
        xld.append(C.dma("gpsimd", X[:, ct * NTOK:(ct + 1) * NTOK],
                         x_mid_in[ct * 128:(ct + 1) * 128, :], key="x"))
        ald.append(C.dma("gpsimd", A[:, ct * NTOK:(ct + 1) * NTOK],
                         att_in[ct * 128:(ct + 1) * 128, :], key="a"))

    if li == 1:
        for ct in range(NT):
            C.op("vector", lambda e, ct=ct: e.tensor_tensor(
                X[:, ct * NTOK:(ct + 1) * NTOK], X[:, ct * NTOK:(ct + 1) * NTOK],
                A[:, ct * NTOK:(ct + 1) * NTOK], ALU.add),
                waits=[xld[-1], ald[-1]], inc=True)
    else:
        wo = C.din("wo_strips", [NT, 128, NT * 128], F32)
        bo = C.din("bo_cols", [128, NT], F32)
        bocol = C.sbuf("bocol", [128, NT], F32)
        dbo = C.dma("sync", bocol[:], bo, key="c0")
        wstage = [C.sbuf("bw", [128, NT * 128], F32) for _ in range(2)]
        wbf = [C.sbuf("bwb", [128, NT * 128], BF16) for _ in range(2)]
        psums = [C.psum("bp", [128, 512], F32) for _ in range(4)]
        sc = C.sbuf("bsc", [128, 512], F32)

        def evac(ot, t0, t1, ps, dep):
            d1 = C.op("vector", lambda e: e.tensor_scalar(ps[:, 0:t1 - t0], ps[:, 0:t1 - t0],
                                                          bocol[:, ot:ot + 1], None, ALU.add),
                      waits=[dep, dbo], inc=True)
            return C.op("vector", lambda e: e.tensor_tensor(
                X[:, ot * NTOK + t0: ot * NTOK + t1], X[:, ot * NTOK + t0: ot * NTOK + t1],
                ps[:, 0:t1 - t0], ALU.add),
                waits=[d1, xld[-1]], inc=True)
        emit_stream_linear(C, wo, NT, NT,
                           lambda ct, t0, t1: A[:, ct * NTOK + t0: ct * NTOK + t1],
                           [(i * 512, (i + 1) * 512) for i in range(NTG)],
                           evac, (wstage, wbf), psums, rhs_deps=[ald[-1]])
    vend = C.mark("vector")
    for ct in range(NT):
        C.dma("gpsimd", x_next[ct * 128:(ct + 1) * 128, :], X[:, ct * NTOK:(ct + 1) * NTOK],
              waits=[vend], key="out")
    fin = (C.dsem["out"], C.dsem["out"].n)
    C.op("gpsimd", lambda e: e.nop(), waits=[fin])
    C.replay()
    es.close()
    return nc, C


def set_size(seq):
    global SEQ, NTOK, NTG, NSG, NKT, LCH
    SEQ = seq
    NTOK = seq // 2
    NTG = max(1, NTOK // TG)
    NSG = SEQ // TG
    NKT = SEQ // 128
    LCH = min(512, NTOK)



def prep_l1(ins, hp):
    w_qkv = ins["L1_attn_w_qkv"]
    gam, bet = ins["L1_attn_gamma"], ins["L1_attn_beta"]
    rm, rup = ins["L1_ffn_rail_maker"], ins["L1_ffn_rail_up_proj"]
    wc = ins["L1_ffn_w_cproj"]
    strips = []
    for g in range(4):
        gh = 8 * hp + 2 * g
        for base in (D, 2 * D, 0):
            blk = w_qkv[base + 64 * gh: base + 64 * gh + 128, :]
            strips.append(host_tile_w(np.ascontiguousarray(blk.T))[0])
    out = {
        "w_qkv_strips": np.ascontiguousarray(np.stack(strips)),
        "gamma_cols": np.ascontiguousarray(np.asarray(gam).reshape(NT, 128).T),
        "beta_cols": np.ascontiguousarray(np.asarray(bet).reshape(NT, 128).T),
    }
    rm_pad = np.zeros((4, 16, 64, 128), np.float32)
    rup_t = np.zeros((4, 16, 64, 64), np.float32)
    for h in range(4):
        for r in range(64):
            rm_pad[h, :, r, 16 * (r % 8):16 * (r % 8) + 16] = np.asarray(rm[h, r]).T
            rup_t[h, :, r, :] = np.asarray(rup[h, r]).T
    out["rm_pad"] = rm_pad.reshape(4, 16, 64 * 128)
    out["rup"] = rup_t.reshape(4, 16, 64 * 64)
    wcT = np.ascontiguousarray(np.asarray(wc).T)
    t = wcT.reshape(16, 8, 128, 8, 128).transpose(0, 3, 2, 1, 4).reshape(128, 128, 8 * 128)
    out["cproj_strips"] = np.ascontiguousarray(t)
    return out



def emit_gating(C, sh, st, logits_cm, w_all, mode, deps=()):
    gst = ExitStack()
    lb = w_all
    ps_t = C.psum("g_pt", [128, 128], BF16, gst)
    ps_b = C.psum("g_pb", [8, 128], BF16, gst)
    L = C.sbuf("g_L", [128, 8], F32, gst)
    m1 = C.sbuf("g_m1", [128, 1], F32, gst)
    m1n = C.sbuf("g_m1n", [128, 1], F32, gst)
    m2 = C.sbuf("g_m2", [128, 1], F32, gst)
    sel = C.sbuf("g_sel", [128, 8], F32, gst)
    num = C.sbuf("g_num", [128, 8], F32, gst)
    den = C.sbuf("g_den", [128, 1], F32, gst)
    wt = C.sbuf("g_wt", [128, 8], BF16, gst)
    dl = C.op("vector", lambda e: e.tensor_copy(lb[:], logits_cm[:]), waits=deps, inc=True)
    last = None
    for tt in range(NTOK // 128):
        d1 = C.op("tensor", lambda e, tt=tt: e.transpose(
            ps_t[:, 0:8], lb[0:8, tt * 128:(tt + 1) * 128], sh.ident[0:8, 0:8]),
            waits=[dl] + ([last] if last else []), inc=True)
        d2 = C.op("vector", lambda e: e.tensor_copy(L[:], ps_t[:, 0:8]), waits=[d1], inc=True)
        d3 = C.op("vector", lambda e: e.reduce_max(m1[:], L[:], axis=AXX), waits=[d2], inc=True)
        if mode == "top1sig":
            dg = C.op("scalar", lambda e: e.activation(out=m1n[:], in_=m1[:], func=AF.Sigmoid),
                      waits=[d3], inc=True)
            d4 = C.op("vector", lambda e: e.tensor_scalar(sel[:], L[:], m1[:], None, ALU.is_ge),
                      waits=[d3], inc=True)
            d5 = C.op("vector", lambda e: e.tensor_scalar(wt[:], sel[:], m1n[:], None, ALU.mult),
                      waits=[d4, dg], inc=True)
        else:
            d3n = C.op("vector", lambda e: e.reduce_max(m1n[:], L[:], axis=AXX, negate=True),
                       waits=[d2], inc=True)
            d4 = C.op("vector", lambda e: e.tensor_scalar(sel[:], L[:], m1[:], None, ALU.is_ge),
                      waits=[d3], inc=True)
            dm = C.op("vector", lambda e: e.scalar_tensor_tensor(
                num[:], sel[:], -1e9, L[:], ALU.mult, ALU.add), waits=[d4], inc=True)
            dm2 = C.op("vector", lambda e: e.reduce_max(m2[:], num[:], axis=AXX),
                       waits=[dm], inc=True)
            de = C.op("scalar", lambda e: e.activation(out=num[:], in_=L[:], func=AF.Exp,
                                                       bias=m1n[:]),
                      waits=[d3n, dm2], inc=True)
            ds = C.op("vector", lambda e: e.tensor_scalar(sel[:], L[:], m2[:], None, ALU.is_ge),
                      waits=[dm2, de], inc=True)
            dn = C.op("vector", lambda e: e.tensor_tensor(num[:], num[:], sel[:], ALU.mult),
                      waits=[ds, de], inc=True)
            dd = C.op("vector", lambda e: e.reduce_sum(den[:], num[:], axis=AXX),
                      waits=[dn], inc=True)
            dr = C.op("vector", lambda e: e.reciprocal(den[:], den[:]), waits=[dd], inc=True)
            d5 = C.op("vector", lambda e: e.tensor_scalar(wt[:], num[:], den[:], None, ALU.mult),
                      waits=[dr], inc=True)
        d6 = C.op("tensor", lambda e, tt=tt: e.transpose(ps_b[0:8, :], wt[:, 0:8], sh.ident[:]),
                  waits=[d5], inc=True)
        last = C.op("vector", lambda e, tt=tt: e.tensor_copy(
            w_all[0:8, tt * 128:(tt + 1) * 128], ps_b[0:8, :]), waits=[d6], inc=True)
    gst.close()
    return last



def emit_ffn2(C, sh, xn_own, x_own, wins, deps):
    fst = ExitStack()
    wb0 = [C.mark("tensor"), C.mark("vector")]
    logits = C.sbuf("f2_logits", [8, NTOK], F32, fst)
    w_all = C.sbuf("f2_wall", [8, NTOK], BF16, fst)
    ps_pool = [C.psum("f2_ps", [128, 512], F32, fst) for _ in range(4)]
    wstage2 = [C.sbuf("f2_ws", [128, 11 * 128], F32, fst) for _ in range(2)]
    wbf4 = [C.sbuf("f2_wb", [128, 11 * 128], BF16, fst) for _ in range(4)]
    st_free = [None, None]
    NPT = min(NTOK, 1024)
    NHP = NPT // 512
    NPAIR = NTOK // NPT
    rin = C.sbuf("f2_rin", [128, NT * NPT], BF16, fst)
    mid = C.sbuf("f2_mid", [128, 22 * NPT], BF16, fst)
    scr2a = [C.sbuf("f2_scr", [128, 512], BF16, fst) for _ in range(2)]
    gb = C.psum("f2_gb", [128, 512], F32, fst)

    drt = C.dma("sync", wstage2[0][:, 0:NT * 8], wins["router"], key="c1", waits=wb0)
    dcb = C.op("vector", lambda e: e.tensor_copy(mid[:, 0:NT * 8], wstage2[0][:, 0:NT * 8]),
               waits=[drt] + wb0, inc=True)
    st_free[0] = dcb
    for tg in range(NTG):
        for ct in range(NT):
            C.op("tensor", lambda e, tg=tg, ct=ct: e.matmul(
                ps_pool[tg % 2][0:8, 0:512], mid[:, ct * 8:(ct + 1) * 8],
                xn_own(ct, tg * 512, (tg + 1) * 512),
                start=(ct == 0), stop=(ct == NT - 1)),
                waits=[dcb, C.mark("scalar")] + list(deps) if ct == 0 else (), inc=(ct == NT - 1))
        mmd = C.mark("tensor")
        C.op("scalar", lambda e, tg=tg: e.copy(logits[0:8, tg * 512:(tg + 1) * 512],
                                               ps_pool[tg % 2][0:8, 0:512]),
             waits=[mmd], inc=True)
    lg_dep = C.mark("scalar")
    g_dep = emit_gating(C, sh, fst, logits, w_all, "top1sig", deps=[lg_dep])

    last = None
    bank_free = [lg_dep] * 4
    wb_free = [None] * 4
    for e_i in range(-1, 8):
        w1st = wins["w1"][e_i] if e_i >= 0 else wins["ws1"]
        w3st = wins["w3"][e_i] if e_i >= 0 else wins["ws3"]
        w2st = wins["w2"][e_i] if e_i >= 0 else wins["ws2"]
        for tp in range(NPAIR):
            p0 = tp * NPT
            if e_i >= 0:
                rdeps = []
                for th in range(NHP):
                    t0 = p0 + th * 512
                    dbc = C.op("tensor", lambda e, e_i=e_i, t0=t0: e.matmul(
                        gb[:, 0:512], sh.sel8[:, e_i * 128:(e_i + 1) * 128],
                        w_all[0:8, t0:t0 + 512], start=True, stop=True),
                        waits=[g_dep, C.mark("vector")], inc=True)
                    mtw = C.mark("tensor")
                    for ct in range(NT):
                        rdeps.append(C.op("vector", lambda e, ct=ct, th=th, t0=t0: e.tensor_tensor(
                            rin[:, ct * NPT + th * 512:ct * NPT + th * 512 + 512],
                            xn_own(ct, t0, t0 + 512), gb[:, 0:512],
                            ALU.mult), waits=[dbc, mtw], inc=True))

                def rhs(ct, th):
                    return rin[:, ct * NPT + th * 512:ct * NPT + (th + 1) * 512]
                rhs_deps = [rdeps[-1]]
            else:
                def rhs(ct, th, p0=p0):
                    t0 = p0 + th * 512
                    return xn_own(ct, t0, t0 + 512)
                rhs_deps = list(deps)
            mid_dep_th = [None] * NHP
            for ot in range(22):
                wcst = {}
                for j, wst_ in ((0, w1st), (1, w3st)):
                    b = 2 * j + ot % 2
                    sb = b // 2
                    dd = C.dma("sync", wstage2[sb][:, 0:NT * 128], wst_[ot],
                               key="w%d" % (2 * sb + ot % 2),
                               waits=[st_free[sb]] if st_free[sb] else wb0)
                    wcst[j] = C.op("vector", lambda e, b=b, sb=sb: e.tensor_copy(
                        wbf4[b][:, 0:NT * 128], wstage2[sb][:, 0:NT * 128]),
                        waits=[dd] + ([wb_free[b]] if wb_free[b] else wb0), inc=True)
                    st_free[sb] = wcst[j]
                for th in range(NHP):
                    gi = NHP * ot + th
                    for j in (0, 1):
                        b = 2 * j + ot % 2
                        bank = 2 * j + gi % 2
                        ps = ps_pool[bank]
                        mmw = [wcst[j]] + rhs_deps
                        if bank_free[bank] is not None:
                            mmw.append(bank_free[bank])
                        for ct in range(NT):
                            rap = rhs(ct, th)
                            C.op("tensor", lambda e, rap=rap, ct=ct, b=b, ps=ps: e.matmul(
                                ps[:, 0:512], wbf4[b][:, ct * 128:(ct + 1) * 128],
                                rap, start=(ct == 0), stop=(ct == NT - 1)),
                                waits=mmw if ct == 0 else (), inc=(ct == NT - 1))
                    mmd = C.mark("tensor")
                    pa, pb = ps_pool[gi % 2], ps_pool[2 + gi % 2]
                    dsl = C.op("scalar", lambda e, pa=pa, gi=gi: e.activation(
                        out=scr2a[gi % 2][:], in_=pa[:, 0:512], func=AF.Silu),
                        waits=[mmd], inc=True)
                    dmid = C.op("vector", lambda e, ot=ot, th=th, gi=gi, pb=pb: e.tensor_tensor(
                        mid[:, ot * NPT + th * 512:ot * NPT + th * 512 + 512],
                        scr2a[gi % 2][:], pb[:, 0:512], ALU.mult),
                        waits=[dsl], inc=True)
                    bank_free[gi % 2] = dmid
                    bank_free[2 + gi % 2] = dmid
                    mid_dep_th[th] = dmid
                wm = C.mark("tensor")
                wb_free[ot % 2] = wm
                wb_free[2 + ot % 2] = wm
            for ot2 in range(8):
                wcst2 = {}
                for hc in range(2):
                    b = 2 * hc + ot2 % 2
                    sb = b // 2
                    dd = C.dma("sync", wstage2[sb][:, 0:11 * 128],
                               w2st[ot2][:, hc * 1408:(hc + 1) * 1408],
                               key="w%d" % (2 * sb + ot2 % 2),
                               waits=[st_free[sb]] if st_free[sb] else wb0)
                    wcst2[hc] = C.op("vector", lambda e, b=b, sb=sb: e.tensor_copy(
                        wbf4[b][:, 0:11 * 128], wstage2[sb][:, 0:11 * 128]),
                        waits=[dd] + ([wb_free[b]] if wb_free[b] else wb0), inc=True)
                    st_free[sb] = wcst2[hc]
                for th in range(NHP):
                    bank = 2 * (ot2 % 2) + th
                    ps = ps_pool[bank]
                    mmw0 = [wcst2[0], mid_dep_th[th]]
                    if bank_free[bank] is not None:
                        mmw0.append(bank_free[bank])
                    for ct2 in range(22):
                        b = (ot2 % 2) if ct2 < 11 else 2 + ot2 % 2
                        j2 = ct2 % 11
                        w = mmw0 if ct2 == 0 else ([wcst2[1]] if ct2 == 11 else ())
                        C.op("tensor", lambda e, ct2=ct2, j2=j2, b=b, ps=ps, th=th: e.matmul(
                            ps[:, 0:512], wbf4[b][:, j2 * 128:(j2 + 1) * 128],
                            mid[:, ct2 * NPT + th * 512:ct2 * NPT + th * 512 + 512],
                            start=(ct2 == 0), stop=(ct2 == 21)),
                            waits=w, inc=(ct2 == 21))
                    mmd = C.mark("tensor")
                    t0 = p0 + th * 512
                    bank_free[bank] = C.op("vector", lambda e, ot2=ot2, ps=ps, t0=t0: e.tensor_tensor(
                        x_own(ot2, t0, t0 + 512), x_own(ot2, t0, t0 + 512),
                        ps[:, 0:512], ALU.add),
                        waits=[mmd], inc=True)
                    last = bank_free[bank]
                wm = C.mark("tensor")
                wb_free[ot2 % 2] = wm
                wb_free[2 + ot2 % 2] = wm
    fst.close()
    return last


def emit_ffn0(C, sh, xn_own, x_own, wins, deps):
    fst = ExitStack()
    wb0 = [C.mark("tensor"), C.mark("vector")]
    H = C.sbuf("f0_h", [128, NT * NTOK], BF16, fst)
    logits = C.sbuf("f0_logits", [8, NTOK], F32, fst)
    w_all = C.sbuf("f0_wall", [8, NTOK], BF16, fst)
    cols = C.sbuf("f0_cols", [128, 1 + NT + 8 * 32], F32, fst)
    ps_pool = [C.psum("f0_ps", [128, 512], F32, fst) for _ in range(4)]
    gb = C.psum("f0_gb", [128, 512], F32, fst)
    gb2 = C.psum("f0_gb2", [1, 512], F32, fst)
    wstage2 = [C.sbuf("f0_ws", [128, 8 * 128], F32, fst) for _ in range(2)]
    wbf4 = [C.sbuf("f0_wb", [128, 8 * 128], BF16, fst) for _ in range(4)]
    act4 = C.sbuf("f0_act", [128, NTG * 8 * 512], BF16, fst)
    gsb = C.sbuf("f0_gsb", [128, NTG * 512], BF16, fst)
    scr = C.sbuf("f0_scr", [128, 512], BF16, fst)
    scr2 = C.sbuf("f0_scr2", [128, 512], BF16, fst)

    d0cols = C.dma("sync", cols[:], wins["f0_cols"], key="c0", waits=wb0)
    dcols = d0cols
    for e_i in range(8):
        o = 1 + NT + e_i * 32 + 16
        dcols = C.op("vector", lambda e, o=o: e.tensor_scalar(
            cols[:, o:o + 16], cols[:, o:o + 16], 1.0, None, ALU.add),
            waits=[dcols], inc=True)
    hdeps = []
    for g in range(NTOK // TG):
        for ct in range(NT):
            src = scr if ct % 2 == 0 else scr2
            sqd = C.op("scalar", lambda e, src=src, ct=ct, g=g: e.activation(
                out=src[:, 0:512], in_=xn_own(ct, g * TG, (g + 1) * TG), func=AF.Square),
                waits=list(deps) + [C.mark("tensor"), C.mark("vector")], inc=True)
            C.op("tensor", lambda e, src=src, st=(ct == 0), sp=(ct == NT - 1): e.matmul(
                gb2[0:1, 0:512], sh.ones_col[:, 0:1], src[:, 0:512],
                start=st, stop=sp),
                waits=[sqd], inc=True)
        mmd = C.mark("tensor")
        d1 = C.op("vector", lambda e: e.tensor_scalar(wstage2[1][0:1, 0:512], gb2[0:1, 0:512],
                                                      1.0 / D, 1.1920929e-07, ALU.mult, ALU.add),
                  waits=[mmd] + wb0, inc=True)
        d2 = C.op("scalar", lambda e: e.activation(out=wstage2[1][0:1, 512:1024],
                                                   in_=wstage2[1][0:1, 0:512],
                                                   func=AF.Sqrt), waits=[d1], inc=True)
        d3 = C.op("vector", lambda e: e.reciprocal(wbf4[1][0:1, 0:512], wstage2[1][0:1, 512:1024]),
                  waits=[d2], inc=True)
        dbb = C.op("tensor", lambda e, g=g: e.matmul(gb[:, 0:512], sh.onesb_row[0:1, :],
                                                     wbf4[1][0:1, 0:512],
                                                     start=True, stop=True),
                   waits=[d3, C.mark("scalar"), C.mark("vector")], inc=True)
        for ct in range(NT):
            hdeps.append(C.op("vector", lambda e, ct=ct, g=g: e.scalar_tensor_tensor(
                H[:, ct * NTOK + g * TG: ct * NTOK + (g + 1) * TG],
                xn_own(ct, g * TG, (g + 1) * TG), cols[:, 1 + ct:2 + ct],
                gb[:, 0:512], ALU.mult, ALU.mult),
                waits=[dbb, dcols], inc=True))
    h_dep = hdeps[-1]

    dgt = C.dma("sync", wstage2[0][:, 0:NT * 8], wins["gate_w"], key="c1", waits=wb0)
    dcb = C.op("vector", lambda e: e.tensor_copy(wbf4[2][:, 0:NT * 8], wstage2[0][:, 0:NT * 8]),
               waits=[dgt] + wb0, inc=True)
    for tg in range(NTG):
        for ct in range(NT):
            C.op("tensor", lambda e, tg=tg, ct=ct: e.matmul(
                ps_pool[tg % 2][0:8, 0:512], wbf4[2][:, ct * 8:(ct + 1) * 8],
                H[:, ct * NTOK + tg * 512: ct * NTOK + (tg + 1) * 512],
                start=(ct == 0), stop=(ct == NT - 1)),
                waits=[dcb, h_dep, C.mark("scalar")] if ct == 0 else (), inc=(ct == NT - 1))
        mmd = C.mark("tensor")
        C.op("scalar", lambda e, tg=tg: e.copy(logits[0:8, tg * 512:(tg + 1) * 512],
                                               ps_pool[tg % 2][0:8, 0:512]),
             waits=[mmd], inc=True)
    lg_dep = C.mark("scalar")
    dlb = C.op("vector", lambda e: e.tensor_scalar(logits[0:8, :], logits[0:8, :],
                                                   cols[0:8, 0:1], None, ALU.add),
               waits=[lg_dep, d0cols], inc=True)
    g_dep = emit_gating(C, sh, fst, logits, w_all, "top2soft", deps=[dlb])

    last = None
    ydep = C.mark("vector")
    b2b = C.sbuf("f0_b2b", [8, NT * 128], BF16, fst)
    db2 = C.dma("sync", wstage2[0][0:8, 0:NT * 128], wins["mlp2_b"], key="c3",
                waits=[g_dep, dcb, C.mark("vector"), C.mark("tensor"), C.mark("scalar")])
    db2b = C.op("vector", lambda e: e.tensor_copy(b2b[:], wstage2[0][0:8, 0:NT * 128]),
                waits=[db2], inc=True)
    bias_bl = [None] * 4
    k_b = 0
    for tc in range(NTG):
        t0 = tc * 512
        for ot in range(NT):
            bank = k_b % 4
            mmw = [db2b, g_dep, ydep]
            if bias_bl[bank] is not None:
                mmw.append(bias_bl[bank])
            C.op("tensor", lambda e, ot=ot, t0=t0, bank=bank: e.matmul(
                ps_pool[bank][:, 0:512], b2b[0:8, ot * 128:(ot + 1) * 128],
                w_all[0:8, t0:t0 + 512], start=True, stop=True),
                waits=mmw, inc=True)
            mmd = C.mark("tensor")
            bias_bl[bank] = C.op("vector", lambda e, ot=ot, t0=t0, bank=bank: e.tensor_tensor(
                x_own(ot, t0, t0 + 512), x_own(ot, t0, t0 + 512),
                ps_pool[bank][:, 0:512], ALU.add),
                waits=[mmd], inc=True)
            last = bias_bl[bank]
            k_b += 1
    bank_last = list(bias_bl)
    wb_free = [None, h_dep, g_dep, None]
    st_free = [db2b, h_dep]
    g_prev = h_dep
    prev_dsl = C.mark("scalar")
    prev_dact = C.mark("vector")
    gs_deps = [None] * NTG
    for e_i in range(8):
        act_dep = {}
        for i in range(8):
            wcst = {}
            for j, oti in ((0, i), (1, 8 + i)):
                b = 2 * j + i % 2
                sb = b // 2
                dd = C.dma("sync", wstage2[sb][:], wins["mlp1"][e_i * 16 + oti], key="w%d" % b,
                           waits=[st_free[sb]] if st_free[sb] else wb0)
                wcst[j] = C.op("vector", lambda e, b=b, sb=sb: e.tensor_copy(
                    wbf4[b][:], wstage2[sb][:]),
                    waits=[dd] + ([wb_free[b]] if wb_free[b] else wb0), inc=True)
                st_free[sb] = wcst[j]
            for tc in range(NTG):
                t0 = tc * 512
                for j in (0, 1):
                    b = 2 * j + i % 2
                    bank = 2 * j + tc % 2
                    ps = ps_pool[bank]
                    mmw = [wcst[j], h_dep]
                    if bank_last[bank] is not None:
                        mmw.append(bank_last[bank])
                    for ct in range(NT):
                        C.op("tensor", lambda e, ct=ct, b=b, ps=ps, t0=t0: e.matmul(
                            ps[:, 0:512], wbf4[b][:, ct * 128:(ct + 1) * 128],
                            H[:, ct * NTOK + t0: ct * NTOK + t0 + 512],
                            start=(ct == 0), stop=(ct == NT - 1)),
                            waits=mmw if ct == 0 else (), inc=(ct == NT - 1))
                mmd = C.mark("tensor")
                cga = cols[:, 1 + NT + e_i * 32 + i:1 + NT + e_i * 32 + i + 1]
                cgb = cols[:, 1 + NT + e_i * 32 + 16 + i:1 + NT + e_i * 32 + 16 + i + 1]
                pg, pl = ps_pool[tc % 2], ps_pool[2 + tc % 2]
                dga = C.op("vector", lambda e, cga=cga, pg=pg: e.tensor_scalar(
                    scr[:], pg[:, 0:512], cga, 7.0, ALU.add, ALU.min),
                    waits=[mmd, dcols, prev_dsl, prev_dact], inc=True)
                dsl = C.op("scalar", lambda e: e.activation(out=scr2[:], in_=scr[:],
                                                            func=AF.Silu, scale=1.702),
                           waits=[dga], inc=True)
                dgb_ = C.op("vector", lambda e, cgb=cgb, pl=pl: e.tensor_scalar(
                    scr[:], pl[:, 0:512], cgb, 8.0, ALU.add, ALU.min),
                    waits=[dsl], inc=True)
                dact = C.op("vector", lambda e, i=i, tc=tc: e.scalar_tensor_tensor(
                    act4[:, (tc * 8 + i) * 512:(tc * 8 + i + 1) * 512],
                    scr[:], -6.0, scr2[:], ALU.max, ALU.mult),
                    waits=[dgb_], inc=True)
                bank_last[tc % 2] = dga
                bank_last[2 + tc % 2] = dgb_
                prev_dsl = dsl
                prev_dact = dact
                act_dep[(i, tc)] = dact
            wm = C.mark("tensor")
            wb_free[i % 2] = wm
            wb_free[2 + i % 2] = wm
            if 4 <= i < 4 + NTG:
                tcb = i - 4
                dbcw = C.op("tensor", lambda e, e_i=e_i, tcb=tcb: e.matmul(
                    gb[:, 0:512], sh.sel8[:, e_i * 128:(e_i + 1) * 128],
                    w_all[0:8, tcb * 512:(tcb + 1) * 512], start=True, stop=True),
                    waits=[g_dep, g_prev], inc=True)
                mmdg = C.mark("tensor")
                gs_deps[tcb] = C.op("scalar", lambda e, tcb=tcb: e.activation(
                    out=gsb[:, tcb * 512:(tcb + 1) * 512], in_=gb[:, 0:512],
                    func=AF.Copy, scale=1.0 / 1.702),
                    waits=[mmdg], inc=True)
                g_prev = gs_deps[tcb]
        adep_tc = [None] * NTG
        for tc in range(NTG):
            for i in range(8):
                adep_tc[tc] = C.op("vector", lambda e, tc=tc, i=i: e.tensor_tensor(
                    act4[:, (tc * 8 + i) * 512:(tc * 8 + i + 1) * 512],
                    act4[:, (tc * 8 + i) * 512:(tc * 8 + i + 1) * 512],
                    gsb[:, tc * 512:(tc + 1) * 512], ALU.mult),
                    waits=[gs_deps[tc], act_dep[(i, tc)]], inc=True)
        for ot in range(NT):
            b = ot % 4
            sb = b // 2
            dd = C.dma("sync", wstage2[sb][:], wins["mlp2"][e_i * 8 + ot], key="w%d" % b,
                       waits=[st_free[sb]] if st_free[sb] else wb0)
            wc = C.op("vector", lambda e, b=b, sb=sb: e.tensor_copy(wbf4[b][:], wstage2[sb][:]),
                      waits=[dd] + ([wb_free[b]] if wb_free[b] else wb0), inc=True)
            st_free[sb] = wc
            for tc in range(NTG):
                t0 = tc * 512
                bank = 2 * (ot % 2) + tc % 2
                ps = ps_pool[bank]
                mmw = [wc, adep_tc[tc]]
                if bank_last[bank] is not None:
                    mmw.append(bank_last[bank])
                for ct in range(8):
                    C.op("tensor", lambda e, ct=ct, b=b, ps=ps, tc=tc: e.matmul(
                        ps[:, 0:512], wbf4[b][:, ct * 128:(ct + 1) * 128],
                        act4[:, (tc * 8 + ct) * 512:(tc * 8 + ct + 1) * 512],
                        start=(ct == 0), stop=(ct == 7)),
                        waits=mmw if ct == 0 else (), inc=(ct == 7))
                mmd = C.mark("tensor")
                bank_last[bank] = C.op("vector", lambda e, ot=ot, ps=ps, t0=t0: e.tensor_tensor(
                    x_own(ot, t0, t0 + 512), x_own(ot, t0, t0 + 512),
                    ps[:, 0:512], ALU.add), waits=[mmd], inc=True)
                last = bank_last[bank]
            wb_free[b] = C.mark("tensor")
    for ct in range(NT):
        last = C.op("vector", lambda e, ct=ct: e.tensor_tensor(
            x_own(ct, 0, NTOK), x_own(ct, 0, NTOK), xn_own(ct, 0, NTOK), ALU.add),
            waits=[last], inc=True)
    fst.close()
    return last



def emit_attn_l2(C, sh, xnf_tiles, att_tiles, wins, deps=()):
    wst = ExitStack()
    wstage = [C.sbuf("aw", [128, 8 * 128], F32, wst) for _ in range(2)]
    wbf = [C.sbuf("awb", [128, 8 * 128], BF16, wst) for _ in range(2)]
    for g in range(4):
        gst = ExitStack()
        K_cm = C.sbuf("Kcm", [128, SEQ], BF16, gst)
        V_cm = C.sbuf("Vcm", [128, SEQ], BF16, gst)
        Q_cm = C.sbuf("Qcm", [128, SEQ], BF16, gst)
        vtok = [C.sbuf("vtok", [128, NKT * 65], BF16, gst) for _ in range(2)]
        rrow = C.sbuf("rrow", [1, 2 * SEQ], F32, gst)
        rcol = C.sbuf("rcol", [128, 2 * NKT], F32, gst)
        rbf = C.sbuf("rbf", [1, SEQ], BF16, gst)
        qsq = C.sbuf("qsq", [128, 512], BF16, gst)
        pst = ExitStack()
        ps_p = [C.psum("pp", [128, 512], F32, pst) for _ in range(2)]
        ps_s = C.psum("pqs", [1, 2048], F32, pst)
        dst = {0: K_cm, 1: V_cm, 2: Q_cm}
        toks = [(t0, t0 + 512) for t0 in range(0, SEQ, 512)]

        def mk_evac(dest):
            def evac(ot, t0, t1, ps, dep):
                return C.op("scalar", lambda e, dest=dest, t0=t0, t1=t1, ps=ps: e.copy(
                    dest[:, t0:t1], ps[:, 0:t1 - t0]), waits=[dep], inc=True)
            return evac
        for j, dest in dst.items():
            emit_stream_linear(C, wins["qkv"], NT, 1, xnf_tiles, toks,
                               mk_evac(dest), (wstage, wbf), ps_p,
                               rhs_deps=list(deps) + [C.mark("scalar"), C.mark("vector")],
                               ot_list=[g * 3 + j],
                               w_waits=[C.mark("vector"), C.mark("tensor")])
        kvq_dep = [C.mark("scalar")]
        def rms_block(SRC, scale_dst):
            for sgi in range(NSG):
                t0 = sgi * 512
                dq = C.op("scalar", lambda e, SRC=SRC, qsq=qsq, t0=t0: e.activation(
                    out=qsq[:, 0:512], in_=SRC[:, t0:t0 + 512], func=AF.Square),
                    waits=kvq_dep + [C.mark("tensor"), C.mark("vector")], inc=True)
                for h in range(2):
                    C.op("tensor", lambda e, qsq=qsq, ps_s=ps_s, h=h: e.matmul(
                        ps_s[0:1, h * 512:(h + 1) * 512], sh.ones_col[64 * h:64 * h + 64, 0:1],
                        qsq[64 * h:64 * h + 64, 0:512],
                        start=True, stop=True, tile_position=(64 * h, 0)),
                        waits=[dq, C.mark("vector")], inc=True)
                    mmd = C.mark("tensor")
                    C.op("vector", lambda e, rrow=rrow, ps_s=ps_s, h=h, t0=t0: e.tensor_scalar(
                        rrow[0:1, h * SEQ + t0: h * SEQ + t0 + 512], ps_s[0:1, h * 512:(h + 1) * 512],
                        1.0 / 64.0, 1.1920929e-07, ALU.mult, ALU.add), waits=[mmd], inc=True)
        rms_block(K_cm, None)
        for h in range(2):
            dsq = C.op("scalar", lambda e, rrow=rrow, h=h: e.activation(
                out=rrow[0:1, h * SEQ:(h + 1) * SEQ], in_=rrow[0:1, h * SEQ:(h + 1) * SEQ],
                func=AF.Sqrt), waits=[C.mark("vector")], inc=True)
            drc = C.op("vector", lambda e, rbf=rbf, rrow=rrow, h=h: e.reciprocal(
                rbf[0:1, 0:SEQ], rrow[0:1, h * SEQ:(h + 1) * SEQ]), waits=[dsq], inc=True)
            for kt in range(NKT):
                C.op("tensor", lambda e, rbf=rbf, ps_p=ps_p, kt=kt: e.matmul(
                    ps_p[0][:, 0:1], rbf[0:1, kt * 128:(kt + 1) * 128],
                    sh.onesb_row[0:1, 0:1], start=True, stop=True),
                    waits=[drc, C.mark("vector")], inc=True)
                mmd = C.mark("tensor")
                C.op("vector", lambda e, rcol=rcol, ps_p=ps_p, h=h, kt=kt: e.tensor_scalar(
                    rcol[:, h * NKT + kt: h * NKT + kt + 1], ps_p[0][:, 0:1],
                    1.0 / 400.0, None, ALU.mult), waits=[mmd], inc=True)
        for sgi in range(NSG):
            t0 = sgi * 512
            dq = C.op("scalar", lambda e, Q_cm=Q_cm, qsq=qsq, t0=t0: e.activation(
                out=qsq[:, 0:512], in_=Q_cm[:, t0:t0 + 512], func=AF.Square),
                waits=[C.mark("vector"), C.mark("tensor")], inc=True)
            for h in range(2):
                C.op("tensor", lambda e, qsq=qsq, ps_s=ps_s, h=h: e.matmul(
                    ps_s[0:1, 1024 + h * 512:1024 + (h + 1) * 512], sh.ones_col[64 * h:64 * h + 64, 0:1],
                    qsq[64 * h:64 * h + 64, 0:512],
                    start=True, stop=True, tile_position=(64 * h, 0)),
                    waits=[dq, C.mark("vector")], inc=True)
                mmd = C.mark("tensor")
                dt1 = C.op("vector", lambda e, rrow=rrow, ps_s=ps_s, h=h: e.tensor_scalar(
                    rrow[0:1, 0:512], ps_s[0:1, 1024 + h * 512:1024 + (h + 1) * 512],
                    1.0 / 64.0, 1.1920929e-07, ALU.mult, ALU.add), waits=[mmd], inc=True)
                dsq = C.op("scalar", lambda e, rrow=rrow: e.activation(
                    out=rrow[0:1, 0:512], in_=rrow[0:1, 0:512], func=AF.Sqrt),
                    waits=[dt1], inc=True)
                drc = C.op("vector", lambda e, rbf=rbf, rrow=rrow: e.reciprocal(
                    rbf[0:1, 0:512], rrow[0:1, 0:512]), waits=[dsq], inc=True)
                dbb = C.op("tensor", lambda e, rbf=rbf, ps_p=ps_p: e.matmul(
                    ps_p[0][0:64, 0:512], sh.onesb_row[0:1, 0:64], rbf[0:1, 0:512],
                    start=True, stop=True), waits=[drc, C.mark("vector")], inc=True)
                mm2 = C.mark("tensor")
                C.op("vector", lambda e, Q_cm=Q_cm, ps_p=ps_p, h=h, t0=t0: e.tensor_tensor(
                    Q_cm[64 * h:64 * h + 64, t0:t0 + 512],
                    Q_cm[64 * h:64 * h + 64, t0:t0 + 512],
                    ps_p[0][0:64, 0:512], ALU.mult), waits=[mm2], inc=True)
        rms_dep = [C.mark("vector"), C.mark("scalar")]
        vt_dep = emit_vtok(C, sh,
                           lambda kt, V_cm=V_cm: V_cm[:, kt * 128:(kt + 1) * 128], vtok, 2,
                           deps=kvq_dep + [C.mark("vector")])
        pst.close()
        ab = make_attn_bufs(C, gst, [0, 1], 65)
        capt = {h: C.sbuf("capt", [128, 512], F32, gst) for h in (0, 1)}

        def pre_exp(h, s_, kt, src, dep, capt=capt, rcol=rcol):
            dd = C.op("scalar", lambda e, h=h, kt=kt, src=src, capt=capt, rcol=rcol: e.activation(
                out=capt[h][:], in_=src, func=AF.Tanh,
                scale=rcol[:, h * NKT + kt: h * NKT + kt + 1]),
                waits=[dep], inc=True)
            return capt[h][:], dd
        emit_attn_group(
            C, sh, ab, [0, 1],
            lambda h, K_cm=K_cm, Q_cm=Q_cm: [
                (lambda kt, h=h, K_cm=K_cm: K_cm[64 * h:64 * h + 64, kt * 128:(kt + 1) * 128],
                 lambda s_, h=h, Q_cm=Q_cm: Q_cm[64 * h:64 * h + 64, s_ * 512:(s_ + 1) * 512],
                 64 * h)],
            lambda h, kt, vtok=vtok: vtok[h][:, kt * 65:kt * 65 + 65],
            65,
            lambda h, s_, g=g: att_tiles(g, h, s_),
            50.0, pre_exp=pre_exp,
            deps=kvq_dep + vt_dep + rms_dep)
        gst.close()
        C.barrier()
    wst.close()
    return C.mark("vector")



def emit_attn_l0(C, sh, xnf_tiles, att_tiles, wins, deps=()):
    wst = ExitStack()
    wstage = [C.sbuf("aw", [128, 8 * 128], F32, wst) for _ in range(2)]
    wbf = [C.sbuf("awb", [128, 8 * 128], BF16, wst) for _ in range(2)]
    K = [C.sbuf("K0", [128, SEQ], BF16, wst) for _ in range(4)]
    Q = [C.sbuf("Q0", [128, SEQ], BF16, wst) for _ in range(4)]
    bcols = C.sbuf("bcols", [128, 12], F32, wst)
    rst = ExitStack()
    cosT = C.sbuf("cosT", [128, SEQ], BF16, rst)
    sinT = C.sbuf("sinT", [128, SEQ], BF16, rst)
    S1 = C.sbuf("ropeS1", [128, 2048], BF16, rst)
    S2 = C.sbuf("ropeS2", [128, 2048], BF16, rst)
    dc1 = C.dma("sync", cosT[:], wins["cos"], key="c0")
    dc2 = C.dma("sync", sinT[:], wins["sin"], key="c1")
    dbc = C.dma("sync", bcols[:], wins["bcols"], key="c2")
    pst = ExitStack()
    ps_p = [C.psum("pp", [128, 512], F32, pst) for _ in range(2)]
    toks = [(t0, t0 + 512) for t0 in range(0, SEQ, 512)]

    def mk_evac(dest, bi):
        def evac(ot, t0, t1, ps, dep):
            return C.op("vector", lambda e, dest=dest, t0=t0, t1=t1, ps=ps, bi=bi: e.tensor_scalar(
                dest[:, t0:t1], ps[:, 0:t1 - t0], bcols[:, bi:bi + 1], None, ALU.add),
                waits=[dep, dbc], inc=True)
        return evac
    for rp in range(4):
        emit_stream_linear(C, wins["qkv"], NT, 1, xnf_tiles, toks,
                           mk_evac(K[rp], rp), (wstage, wbf), ps_p,
                           rhs_deps=list(deps) + [C.mark("vector")],
                           ot_list=[rp], w_waits=[C.mark("vector"), C.mark("tensor")])
    for rp in range(4):
        emit_stream_linear(C, wins["qkv"], NT, 1, xnf_tiles, toks,
                           mk_evac(Q[rp], 4 + rp), (wstage, wbf), ps_p,
                           rhs_deps=list(deps) + [C.mark("vector")],
                           ot_list=[4 + rp], w_waits=[C.mark("vector"), C.mark("tensor")])
    kq_dep = [C.mark("vector")]

    def rope_pair(ebuf, obuf, dep0):
        dd = dep0
        for c0 in range(0, SEQ, 2048):
            c1 = min(SEQ, c0 + 2048)
            w = c1 - c0
            eb, ob = ebuf[:, c0:c1], obuf[:, c0:c1]
            cs, sn = cosT[:, c0:c1], sinT[:, c0:c1]
            d1 = C.op("vector", lambda e, ob=ob, sn=sn, w=w: e.tensor_tensor(
                S1[:, 0:w], ob, sn, ALU.mult), waits=[dd, dc1, dc2], inc=True)
            d2 = C.op("vector", lambda e, ob=ob, cs=cs, w=w: e.tensor_tensor(
                S2[:, 0:w], ob, cs, ALU.mult), waits=[d1], inc=True)
            d3 = C.op("vector", lambda e, eb=eb, ob=ob, sn=sn: e.tensor_tensor(
                ob, eb, sn, ALU.mult), waits=[d2], inc=True)
            d4 = C.op("vector", lambda e, ob=ob, w=w: e.tensor_tensor(
                ob, ob, S2[:, 0:w], ALU.add), waits=[d3], inc=True)
            d5 = C.op("vector", lambda e, eb=eb, cs=cs: e.tensor_tensor(
                eb, eb, cs, ALU.mult), waits=[d4], inc=True)
            dd = C.op("vector", lambda e, eb=eb, w=w: e.tensor_tensor(
                eb, eb, S1[:, 0:w], ALU.subtract), waits=[d5], inc=True)
        return dd
    rd = kq_dep[0]
    for i in range(2):
        rd = rope_pair(K[i], K[2 + i], rd)
        rd = rope_pair(Q[i], Q[2 + i], rd)
    rope_dep = [rd]
    rst.close()
    pst.close()
    C.barrier()

    for g in range(4):
        gst = ExitStack()
        V_cm = C.sbuf("Vcm", [128, SEQ], BF16, gst)
        vtok = [C.sbuf("vtok", [128, NKT * 65], BF16, gst) for _ in range(2)]
        pst2 = ExitStack()
        ps_p2 = [C.psum("pp2", [128, 512], F32, pst2) for _ in range(2)]
        emit_stream_linear(C, wins["qkv"], NT, 1, xnf_tiles, toks,
                           mk_evac(V_cm, 8 + g), (wstage, wbf), ps_p2,
                           rhs_deps=list(deps) + [C.mark("vector")],
                           ot_list=[8 + g], w_waits=[C.mark("vector"), C.mark("tensor")])
        v_dep = [C.mark("vector")]
        vt_dep = emit_vtok(C, sh,
                           lambda kt, V_cm=V_cm: V_cm[:, kt * 128:(kt + 1) * 128], vtok, 2,
                           deps=v_dep)
        pst2.close()
        ab = make_attn_bufs(C, gst, [0, 1], 65)

        def chunks_l0(h, g=g):
            hh = 2 * g + h
            pi, rb = hh // 4, 32 * (hh % 4)
            return [
                (lambda kt, pi=pi, rb=rb: K[pi][rb:rb + 32, kt * 128:(kt + 1) * 128],
                 lambda s_, pi=pi, rb=rb: Q[pi][rb:rb + 32, s_ * 512:(s_ + 1) * 512], rb),
                (lambda kt, pi=pi, rb=rb: K[2 + pi][rb:rb + 32, kt * 128:(kt + 1) * 128],
                 lambda s_, pi=pi, rb=rb: Q[2 + pi][rb:rb + 32, s_ * 512:(s_ + 1) * 512], rb),
            ]
        emit_attn_group(
            C, sh, ab, [0, 1], chunks_l0,
            lambda h, kt, vtok=vtok: vtok[h][:, kt * 65:kt * 65 + 65],
            65,
            lambda h, s_, g=g: att_tiles(g, h, s_),
            1.0 / 8.0,
            deps=rope_dep + vt_dep)
        gst.close()
        C.barrier()
    wst.close()
    return C.mark("vector")



def _wins_l2(C):
    return {
        "qkv": C.din("w_qkv_strips", [12, 128, NT * 128], F32),
        "router": C.din("router_tiles", [128, NT * 8], F32),
        "w1": C.din("w1_strips", [8, 22, 128, NT * 128], F32),
        "w3": C.din("w3_strips", [8, 22, 128, NT * 128], F32),
        "w2": C.din("w2_strips", [8, 8, 128, 22 * 128], F32),
        "ws1": C.din("ws1_strips", [22, 128, NT * 128], F32),
        "ws3": C.din("ws3_strips", [22, 128, NT * 128], F32),
        "ws2": C.din("ws2_strips", [8, 128, 22 * 128], F32),
    }


def _wins_l0(C):
    return {
        "qkv": C.din("w_qkv_strips", [12, 128, NT * 128], F32),
        "cos": C.din("rope_cos", [128, SEQ], BF16),
        "sin": C.din("rope_sin", [128, SEQ], BF16),
        "bcols": C.din("attn_bcols", [128, 12], F32),
        "f0_cols": C.din("f0_cols", [128, 1 + NT + 8 * 32], F32),
        "gate_w": C.din("gate_w_tiles", [128, NT * 8], F32),
        "gate_b": C.din("gate_b_row", [1, 8], F32),
        "gate_b_col": C.din("gate_b_col", [8, 1], F32),
        "mlp1": C.din("mlp1_strips", [8 * 16, 128, NT * 128], F32),
        "mlp2": C.din("mlp2_strips", [8 * 8, 128, NT * 128], F32),
        "mlp2_b": C.din("mlp2_b_rows", [8, NT * 128], F32),
    }


def tiles_cm(w):
    ci, k = w.shape
    return np.ascontiguousarray(w.reshape(ci // 128, 128, k).transpose(1, 0, 2).reshape(128, -1))


def prep_l2(ins, hp):
    w_q, w_k, w_v = (np.asarray(ins["L2_attn_w_q"]), np.asarray(ins["L2_attn_w_k"]),
                     np.asarray(ins["L2_attn_w_v"]))
    strips = []
    for g in range(4):
        gh = 8 * hp + 2 * g
        for w in (w_k, w_v, w_q):
            blk = w[64 * gh: 64 * gh + 128, :]
            strips.append(host_tile_w(np.ascontiguousarray(blk.T))[0])
    out = {"w_qkv_strips": np.ascontiguousarray(np.stack(strips)),
           "router_tiles": tiles_cm(np.asarray(ins["L2_ffn_router_DE"]))}
    out["w1_strips"] = np.stack([host_tile_w(np.asarray(ins["L2_ffn_w1_eDF"][e])) for e in range(8)])
    out["w3_strips"] = np.stack([host_tile_w(np.asarray(ins["L2_ffn_w3_eDF"][e])) for e in range(8)])
    out["w2_strips"] = np.stack([host_tile_w(np.asarray(ins["L2_ffn_w2_eFD"][e])) for e in range(8)])
    out["ws1_strips"] = host_tile_w(np.asarray(ins["L2_ffn_ws1_DF"]))
    out["ws3_strips"] = host_tile_w(np.asarray(ins["L2_ffn_ws3_DF"]))
    out["ws2_strips"] = host_tile_w(np.asarray(ins["L2_ffn_ws2_FD"]))
    out["wo_strips"] = host_tile_w(np.ascontiguousarray(np.asarray(ins["L2_attn_w_o"]).T))
    out["bo_cols"] = np.zeros((128, NT), np.float32)
    return out


def prep_l0(ins, hp):
    Wq, Wk, Wv = (np.asarray(ins["L0_attn_Wq"]), np.asarray(ins["L0_attn_Wk"]),
                  np.asarray(ins["L0_attn_Wv"]))
    bq, bk, bv = (np.asarray(ins["L0_attn_bq"]), np.asarray(ins["L0_attn_bk"]),
                  np.asarray(ins["L0_attn_bv"]))
    strips = []
    bcols = np.zeros((128, 12), np.float32)
    for j, (w, b) in enumerate(((Wk, bk), (Wq, bq))):
        for par in (0, 1):
            for pi in range(2):
                rows = []
                for hh in range(4 * pi, 4 * pi + 4):
                    gh = 8 * hp + hh
                    rows += [64 * gh + 2 * jj + par for jj in range(32)]
                strips_idx = j * 4 + par * 2 + pi if False else None
                strips.append((j * 4 + par * 2 + pi, host_tile_w(np.ascontiguousarray(w[rows, :].T))[0]))
                bcols[:, j * 4 + par * 2 + pi] = b[rows]
    for g in range(4):
        gh = 8 * hp + 2 * g
        rows = list(range(64 * gh, 64 * gh + 128))
        strips.append((8 + g, host_tile_w(np.ascontiguousarray(Wv[rows, :].T))[0]))
        bcols[:, 8 + g] = bv[rows]
    strips.sort(key=lambda t: t[0])
    out = {"w_qkv_strips": np.ascontiguousarray(np.stack([t[1] for t in strips])),
           "attn_bcols": bcols}
    p = np.arange(32)
    invf = 10000.0 ** (-2.0 * p / 64.0)
    t = np.arange(SEQ)
    ang = t[None, :] * invf[:, None]
    out["rope_cos"] = np.ascontiguousarray(np.tile(np.cos(ang), (4, 1))).astype(BFNP)
    out["rope_sin"] = np.ascontiguousarray(np.tile(np.sin(ang), (4, 1))).astype(BFNP)
    cols = np.zeros((128, 1 + NT + 8 * 32), np.float32)
    cols[:, 1:1 + NT] = np.asarray(ins["L0_ffn_norm_w"]).reshape(NT, 128).T
    cols[0:8, 0] = np.asarray(ins["L0_ffn_gate_b"])
    m1b = np.asarray(ins["L0_ffn_mlp1_b"])
    for e in range(8):
        glu_b, lin_b = m1b[e][0::2], m1b[e][1::2]
        for i in range(8):
            cols[:, 1 + NT + e * 32 + i] = glu_b[i * 128:(i + 1) * 128]
            cols[:, 1 + NT + e * 32 + 16 + i] = lin_b[i * 128:(i + 1) * 128]
    out["f0_cols"] = cols
    out["gate_w_tiles"] = tiles_cm(np.ascontiguousarray(np.asarray(ins["L0_ffn_gate_w"]).T))
    out["gate_b_row"] = np.asarray(ins["L0_ffn_gate_b"]).reshape(1, 8)
    out["gate_b_col"] = np.asarray(ins["L0_ffn_gate_b"]).reshape(8, 1)
    m1w = np.asarray(ins["L0_ffn_mlp1_w"])
    mlp1 = []
    for e in range(8):
        perm = np.concatenate([m1w[e][0::2], m1w[e][1::2]], axis=0)
        mlp1.append(host_tile_w(np.ascontiguousarray(perm.T)))
    out["mlp1_strips"] = np.ascontiguousarray(np.concatenate(mlp1))
    m2w = np.asarray(ins["L0_ffn_mlp2_w"])
    mlp2 = [host_tile_w(np.ascontiguousarray(m2w[e].T)) for e in range(8)]
    out["mlp2_strips"] = np.ascontiguousarray(np.concatenate(mlp2))
    out["mlp2_b_rows"] = np.asarray(ins["L0_ffn_mlp2_b"]).reshape(8, NT * 128)
    out["wo_strips"] = host_tile_w(np.ascontiguousarray(np.asarray(ins["L0_attn_Wo"]).T))
    out["bo_cols"] = np.asarray(ins["L0_attn_bo"]).reshape(NT, 128).T.copy()
    return out



def _wins_l3(C):
    return {
        "qkv": C.din("w_qkv_strips", [12, 128, NT * 128], F32),
        "qkv_b": C.din("qkv_bcols", [128, 12], F32),
        "lat": C.din("lat_bd", [8, 128, 128], F32),
        "v2l": C.din("v2l_bd", [128, 64], F32),
        "l2o": C.din("l2o_bd", [64, 128], F32),
        "lat_b": C.din("lat_bcols", [128, 8], F32),
        "l2o_b": C.din("l2o_bcols", [128, 1], F32),
        "cos": C.din("rope_cos", [128, SEQ], BF16),
        "sin": C.din("rope_sin", [128, SEQ], BF16),
        "Wr": C.din("wr_tiles", [128, NT * 8], F32),
        "A": C.din("a_tiles", [128, 8 * NT * 64], F32),
        "Bw": C.din("bw_strips", [8, 64, 8 * 128], F32),
    }


def emit_ffn3_pre(C, sh, xn_own, x_own, MIDs, wins, deps, dram):
    mid_dram, _ = dram
    fst = ExitStack()
    Asf = C.sbuf("f3_Af", [128, NT * 64], F32, fst)
    Asb = C.sbuf("f3_A", [128, NT * 64], BF16, fst)
    ps = [C.psum("f3_ps", [128, 512], F32, fst) for _ in range(2)]
    dc = None
    for e_i in range(8):
        dd = C.dma("sync", Asf[:], wins["A"][:, e_i * NT * 64:(e_i + 1) * NT * 64],
                   key="w0", waits=[dc] if dc else [])
        dc = C.op("vector", lambda e: e.tensor_copy(Asb[:], Asf[:]),
                  waits=[dd, C.mark("tensor")], inc=True)
        sdone = []
        for tg in range(NTG):
            pp = ps[tg % 2]
            for ct in range(NT):
                C.op("tensor", lambda e, ct=ct, tg=tg, pp=pp: e.matmul(
                    pp[0:64, 0:512], Asb[:, ct * 64:(ct + 1) * 64],
                    xn_own(ct, tg * 512, (tg + 1) * 512),
                    start=(ct == 0), stop=(ct == NT - 1)),
                    waits=[dc, C.mark("scalar"), (C.dsem.get("mb"), C.dsem["mb"].n) if "mb" in C.dsem else None] + list(deps) if ct == 0 else (),
                    inc=(ct == NT - 1))
            mmd = C.mark("tensor")
            sdone.append(C.op("scalar", lambda e, tg=tg, pp=pp: e.copy(
                MIDs[0:64, tg * 512:(tg + 1) * 512], pp[0:64, 0:512]),
                waits=[mmd], inc=True))
        C.dma("gpsimd", mid_dram[0:64, e_i * NTOK:(e_i + 1) * NTOK], MIDs[0:64, :],
              waits=[sdone[-1]], key="mb")
    last2 = None
    for ct in range(NT):
        last2 = C.op("vector", lambda e, ct=ct: e.tensor_tensor(
            x_own(ct, 0, NTOK), x_own(ct, 0, NTOK), xn_own(ct, 0, NTOK), ALU.add),
            waits=list(deps), inc=True)
    fst.close()
    return [last2, (C.dsem["mb"], C.dsem["mb"].n)]


def emit_ffn3_post(C, sh, cls_buf, x_own, mid_dram, wins, deps):
    fst = ExitStack()
    wr = C.sbuf("f3_wr", [128, NT * 8], F32, fst)
    wrb = C.sbuf("f3_wrb", [128, NT * 8], BF16, fst)
    L8 = C.sbuf("f3_L8", [1, 48], F32, fst)
    g8b = C.sbuf("f3_g8b", [1, 8], BF16, fst)
    gcolf = C.sbuf("f3_gcol", [64, 8], F32, fst)
    MID = C.sbuf("f3_mid", [64, 8 * NTOK], BF16, fst)
    Bf = C.sbuf("f3_Bf", [64, 8 * 128], F32, fst)
    Bb = C.sbuf("f3_B", [64, 8 * 128], BF16, fst)
    ps = [C.psum("f3_ps2", [128, 512], F32, fst) for _ in range(2)]
    mld = [C.dma("gpsimd", MID[0:64, :], mid_dram[0:64, :], waits=list(deps), key="mr")]
    dwr = C.dma("sync", wr[:], wins["Wr"], key="c1", waits=list(deps))
    dwrb = C.op("vector", lambda e: e.tensor_copy(wrb[:], wr[:]), waits=[dwr], inc=True)
    for ct in range(NT):
        C.op("tensor", lambda e, ct=ct: e.matmul(
            ps[0][0:1, 0:8], cls_buf[:, ct:ct + 1], wrb[:, ct * 8:(ct + 1) * 8],
            start=(ct == 0), stop=(ct == NT - 1)),
            waits=[dwrb] + list(deps), inc=(ct == NT - 1))
    mmd = C.mark("tensor")
    l = L8[0:1, 0:8]; m1 = L8[0:1, 8:9]; m1n = L8[0:1, 9:10]; m2 = L8[0:1, 10:11]
    sel = L8[0:1, 16:24]; num = L8[0:1, 24:32]; den = L8[0:1, 32:33]
    d0 = C.op("vector", lambda e: e.tensor_copy(l, ps[0][0:1, 0:8]), waits=[mmd], inc=True)
    d1 = C.op("vector", lambda e: e.reduce_max(m1, l, axis=AXX), waits=[d0], inc=True)
    d1n = C.op("vector", lambda e: e.reduce_max(m1n, l, axis=AXX, negate=True), waits=[d1], inc=True)
    d2 = C.op("vector", lambda e: e.tensor_scalar(sel, l, m1, None, ALU.is_ge), waits=[d1], inc=True)
    d3 = C.op("vector", lambda e: e.scalar_tensor_tensor(num, sel, -1e9, l, ALU.mult, ALU.add),
              waits=[d2], inc=True)
    d4 = C.op("vector", lambda e: e.reduce_max(m2, num, axis=AXX), waits=[d3], inc=True)
    d5 = C.op("scalar", lambda e: e.activation(out=num, in_=l, func=AF.Exp, bias=m1n),
              waits=[d4, d1n], inc=True)
    d6 = C.op("vector", lambda e: e.tensor_scalar(sel, l, m2, None, ALU.is_ge), waits=[d4, d5], inc=True)
    d7 = C.op("vector", lambda e: e.tensor_tensor(num, num, sel, ALU.mult), waits=[d6], inc=True)
    d8 = C.op("vector", lambda e: e.reduce_sum(den, num, axis=AXX), waits=[d7], inc=True)
    d9 = C.op("vector", lambda e: e.reciprocal(den, den), waits=[d8], inc=True)
    d10 = C.op("vector", lambda e: e.tensor_scalar(g8b[0:1, 0:8], num, den, None, ALU.mult), waits=[d9], inc=True)
    d11 = C.op("tensor", lambda e: e.matmul(ps[0][0:64, 0:8], sh.onesb_row[0:1, 0:64],
                                            g8b[0:1, 0:8], start=True, stop=True),
               waits=[d10, C.mark("vector")], inc=True)
    d12 = C.op("vector", lambda e: e.tensor_copy(gcolf[:], ps[0][0:64, 0:8]), waits=[d11], inc=True)
    sdeps = []
    for e_i in range(8):
        sdeps.append(C.op("vector", lambda e, e_i=e_i: e.tensor_scalar(
            MID[0:64, e_i * NTOK:(e_i + 1) * NTOK], MID[0:64, e_i * NTOK:(e_i + 1) * NTOK],
            gcolf[:, e_i:e_i + 1], None, ALU.mult), waits=[d12] + mld, inc=True))
    last = None
    dc = None
    for ot in range(NT):
        dd = C.dma("sync", Bf[:], wins["Bw"][ot], key="w1", waits=[dc] if dc else [])
        dc = C.op("vector", lambda e: e.tensor_copy(Bb[:], Bf[:]),
                  waits=[dd, C.mark("tensor")], inc=True)
        for tg in range(NTG):
            pp = ps[(ot * NTG + tg) % 2]
            for e_i in range(8):
                C.op("tensor", lambda e, e_i=e_i, ot=ot, tg=tg, pp=pp: e.matmul(
                    pp[:, 0:512], Bb[0:64, e_i * 128:(e_i + 1) * 128],
                    MID[0:64, e_i * NTOK + tg * 512: e_i * NTOK + (tg + 1) * 512],
                    start=(e_i == 0), stop=(e_i == 7)),
                    waits=[dc, sdeps[-1], C.mark("vector")] if e_i == 0 else (),
                    inc=(e_i == 7))
            mmd = C.mark("tensor")
            last = C.op("vector", lambda e, ot=ot, tg=tg, pp=pp: e.tensor_tensor(
                x_own(ot, tg * 512, (tg + 1) * 512), x_own(ot, tg * 512, (tg + 1) * 512),
                pp[:, 0:512], ALU.add), waits=[mmd], inc=True)
    fst.close()
    return last


def emit_attn_l3(C, sh, xnf_tiles, att_tiles, wins, deps=()):
    wst = ExitStack()
    wstage = [C.sbuf("aw", [128, 8 * 128], F32, wst) for _ in range(2)]
    wbf = [C.sbuf("awb", [128, 8 * 128], BF16, wst) for _ in range(2)]
    bcols = C.sbuf("bcols", [128, 12], F32, wst)
    latw_f = C.sbuf("latwf", [128, 128], F32, wst)
    LW = {}
    for nm in ("q2l_e", "q2l_o", "kv2l_e", "kv2l_o", "wqk_ee", "wqk_oe", "wqk_eo", "wqk_oo"):
        LW[nm] = C.sbuf(nm, [128, 128], BF16, wst)
    v2l = C.sbuf("v2l", [128, 64], BF16, wst)
    l2o = C.sbuf("l2o", [64, 128], BF16, wst)
    latb = C.sbuf("latb", [128, 8], F32, wst)
    l2ob = C.sbuf("l2ob", [128, 1], F32, wst)
    cosT = C.sbuf("cosT", [128, SEQ], BF16, wst)
    sinT = C.sbuf("sinT", [128, SEQ], BF16, wst)
    dl = C.dma("sync", bcols[:], wins["qkv_b"], key="c2")
    cv = None
    for i, nm in enumerate(("q2l_e", "q2l_o", "kv2l_e", "kv2l_o", "wqk_ee", "wqk_oe", "wqk_eo", "wqk_oo")):
        dd = C.dma("sync", latw_f[:], wins["lat"][i], key="c0", waits=[cv] if cv else [])
        cv = C.op("vector", lambda e, nm=nm: e.tensor_copy(LW[nm][:], latw_f[:]), waits=[dd], inc=True)
    dd = C.dma("sync", latw_f[:, 0:64], wins["v2l"], key="c0", waits=[cv])
    cv = C.op("vector", lambda e: e.tensor_copy(v2l[:], latw_f[:, 0:64]), waits=[dd], inc=True)
    dd = C.dma("sync", latw_f[0:64, 0:128], wins["l2o"], key="c0", waits=[cv])
    cv = C.op("vector", lambda e: e.tensor_copy(l2o[:], latw_f[0:64, 0:128]), waits=[dd], inc=True)
    d_lb = C.dma("sync", latb[:], wins["lat_b"], key="c1")
    d_ob = C.dma("sync", l2ob[:], wins["l2o_b"], key="c3")
    d_c1 = C.dma("sync", cosT[:], wins["cos"], key="c4")
    d_c2 = C.dma("sync", sinT[:], wins["sin"], key="c5")
    wdep = [cv, dl, d_lb, d_ob, d_c1, d_c2]
    toks = [(t0, t0 + 512) for t0 in range(0, SEQ, 512)]

    for g in range(4):
        gst = ExitStack()
        QLe = C.sbuf("QLe", [128, SEQ], BF16, gst)
        QLo = C.sbuf("QLo", [128, SEQ], BF16, gst)
        KLe = C.sbuf("KLe", [128, SEQ], BF16, gst)
        KLo = C.sbuf("KLo", [128, SEQ], BF16, gst)
        VL = C.sbuf("VL", [64, SEQ], BF16, gst)
        YL = C.sbuf("YL", [64, SEQ], BF16, gst)
        vtok = [C.sbuf("vtok", [128, NKT * 33], BF16, gst) for _ in range(2)]
        rS = C.sbuf("rS", [128, 1024], BF16, gst)
        HC = C.sbuf("HC", [128, 512], BF16, gst)
        pst = ExitStack()
        ps_p = [C.psum("pp", [128, 512], F32, pst) for _ in range(2)]
        ps_l = [C.psum("pl", [128, 512], F32, pst) for _ in range(2)]

        plan = ((0, ((LW["kv2l_e"], KLe, 2, 128), (LW["kv2l_o"], KLo, 3, 128))),
                (1, ((v2l, VL, 4, 64),)),
                (2, ((LW["q2l_e"], QLe, 0, 128), (LW["q2l_o"], QLo, 1, 128))))
        for j, lats in plan:
            b = j % 2
            dd = C.dma("sync", wstage[b][:], wins["qkv"][g * 3 + j], key="w%d" % b,
                       waits=[C.mark("vector"), C.mark("tensor")])
            wc = C.op("vector", lambda e, b=b: e.tensor_copy(wbf[b][:], wstage[b][:]),
                      waits=[dd], inc=True)
            for t0, t1 in toks:
                pc = ps_p[(t0 // 512) % 2]
                for ct in range(NT):
                    rap = xnf_tiles(ct, t0, t1)
                    C.op("tensor", lambda e, rap=rap, ct=ct, b=b, pc=pc: e.matmul(
                        pc[:, 0:512], wbf[b][:, ct * 128:(ct + 1) * 128], rap,
                        start=(ct == 0), stop=(ct == NT - 1)),
                        waits=[wc, C.mark("vector")] + list(deps) if ct == 0 else (),
                        inc=(ct == NT - 1))
                mmd = C.mark("tensor")
                hcd = C.op("vector", lambda e, HC=HC, pc=pc, j=j, g=g: e.tensor_scalar(
                    HC[:], pc[:, 0:512], bcols[:, g * 3 + j:g * 3 + j + 1], None, ALU.add),
                    waits=[mmd, dl, C.mark("tensor")], inc=True)
                for (W, DST, bi, rows) in lats:
                    pl = ps_l[bi % 2]
                    C.op("tensor", lambda e, W=W, pl=pl, HC=HC, rows=rows: e.matmul(
                        pl[0:rows, 0:512], W[:, 0:rows], HC[:],
                        start=True, stop=True),
                        waits=[hcd, cv, C.mark("vector")], inc=True)
                    mm2 = C.mark("tensor")
                    C.op("vector", lambda e, DST=DST, pl=pl, t0=t0, t1=t1, bi=bi, rows=rows: e.tensor_scalar(
                        DST[0:rows, t0:t1], pl[0:rows, 0:512], latb[0:rows, bi:bi + 1],
                        None, ALU.add),
                        waits=[mm2, d_lb], inc=True)
        kl_dep = [C.mark("vector")]

        def rope_pair(EB, OB, dep0):
            dd = dep0
            for c0 in range(0, SEQ, 512):
                w = 512
                for bb in (0, 64):
                    eb, ob = EB[bb:bb + 16, c0:c0 + w], OB[bb:bb + 16, c0:c0 + w]
                    cs, sn = cosT[bb:bb + 16, c0:c0 + w], sinT[bb:bb + 16, c0:c0 + w]
                    s1 = rS[bb:bb + 16, 0:w]
                    s2 = rS[bb:bb + 16, 512:512 + w]
                    d1 = C.op("vector", lambda e, s1=s1, ob=ob, sn=sn: e.tensor_tensor(s1, ob, sn, ALU.mult),
                              waits=[dd, d_c1, d_c2], inc=True)
                    d2 = C.op("vector", lambda e, s2=s2, ob=ob, cs=cs: e.tensor_tensor(s2, ob, cs, ALU.mult),
                              waits=[d1], inc=True)
                    d3 = C.op("vector", lambda e, ob=ob, eb=eb, sn=sn: e.tensor_tensor(ob, eb, sn, ALU.mult),
                              waits=[d2], inc=True)
                    d4 = C.op("vector", lambda e, ob=ob, s2=s2: e.tensor_tensor(ob, ob, s2, ALU.add),
                              waits=[d3], inc=True)
                    d5 = C.op("vector", lambda e, eb=eb, cs=cs: e.tensor_tensor(eb, eb, cs, ALU.mult),
                              waits=[d4], inc=True)
                    dd = C.op("vector", lambda e, eb=eb, s1=s1: e.tensor_tensor(eb, eb, s1, ALU.subtract),
                              waits=[d5], inc=True)
            return dd
        rd = C.mark("vector")
        rd = rope_pair(QLe, QLo, rd)
        rd = rope_pair(KLe, KLo, rd)
        qfk_dep = rd
        for t0, t1 in toks:
            m1w = [qfk_dep, cv, C.mark("vector")]
            C.op("tensor", lambda e, QLe=QLe, pl=ps_l[0], t0=t0, t1=t1: e.matmul(
                pl[:, 0:512], LW["wqk_ee"][:, 0:128], QLe[:, t0:t1],
                start=True, stop=False), waits=m1w, inc=False)
            C.op("tensor", lambda e, QLo=QLo, pl=ps_l[0], t0=t0, t1=t1: e.matmul(
                pl[:, 0:512], LW["wqk_oe"][:, 0:128], QLo[:, t0:t1],
                start=False, stop=True), waits=(), inc=True)
            C.op("tensor", lambda e, QLe=QLe, pl=ps_l[1], t0=t0, t1=t1: e.matmul(
                pl[:, 0:512], LW["wqk_eo"][:, 0:128], QLe[:, t0:t1],
                start=True, stop=False), waits=(), inc=False)
            C.op("tensor", lambda e, QLo=QLo, pl=ps_l[1], t0=t0, t1=t1: e.matmul(
                pl[:, 0:512], LW["wqk_oo"][:, 0:128], QLo[:, t0:t1],
                start=False, stop=True), waits=(), inc=True)
            mmq = C.mark("tensor")
            C.op("vector", lambda e, QLe=QLe, pl=ps_l[0], t0=t0, t1=t1: e.tensor_scalar(
                QLe[:, t0:t1], pl[:, 0:512], latb[:, 5:6], None, ALU.add),
                waits=[mmq, d_lb], inc=True)
            qfk_dep = C.op("vector", lambda e, QLo=QLo, pl=ps_l[1], t0=t0, t1=t1: e.tensor_scalar(
                QLo[:, t0:t1], pl[:, 0:512], latb[:, 6:7], None, ALU.add),
                waits=[mmq], inc=True)
        lat_dep = [C.mark("vector")]
        vt_dep = emit_vtok33(C, sh, VL, vtok, deps=lat_dep)
        pst.close()
        ab = make_attn_bufs(C, gst, [0, 1], 33)

        def chunks_l3(h):
            b = 64 * h
            return [
                (lambda kt, b=b, KLe=KLe: KLe[b:b + 16, kt * 128:(kt + 1) * 128],
                 lambda s_, b=b, QLe=QLe: QLe[b:b + 16, s_ * 512:(s_ + 1) * 512], b),
                (lambda kt, b=b, KLo=KLo: KLo[b:b + 16, kt * 128:(kt + 1) * 128],
                 lambda s_, b=b, QLo=QLo: QLo[b:b + 16, s_ * 512:(s_ + 1) * 512], b),
            ]
        emit_attn_group(
            C, sh, ab, [0, 1], chunks_l3,
            lambda h, kt, vtok=vtok: vtok[h][:, kt * 33:kt * 33 + 33],
            33,
            lambda h, s_, YL=YL: YL[32 * h:32 * h + 32, s_ * 512:(s_ + 1) * 512],
            float(32 ** -0.5),
            deps=lat_dep + vt_dep)
        yd = C.mark("vector")
        for s_, (t0, t1) in enumerate(toks):
            pv = ab["av"][0]
            C.op("tensor", lambda e, pv=pv, YL=YL, t0=t0, t1=t1: e.matmul(
                pv[:, 0:512], l2o[:, 0:128], YL[0:64, t0:t1],
                start=True, stop=True),
                waits=[yd, C.mark("scalar"), C.mark("vector")], inc=True)
            mmd = C.mark("tensor")
            oap = att_tiles(g, t0, t1)
            C.op("vector", lambda e, oap=oap, pv=pv: e.tensor_scalar(
                oap, pv[:, 0:512], l2ob[:, 0:1], None, ALU.add),
                waits=[mmd, d_ob], inc=True)
        gst.close()
        C.barrier()
    wst.close()
    return C.mark("vector")


def emit_vtok33(C, sh, VL, vtok, deps=()):
    vst = ExitStack()
    ptr = C.psum("ptr", [128, 64], BF16, vst)
    d_ones = []
    for h in range(2):
        d_ones.append(C.op("vector", lambda e, h=h: e.memset(
            vtok[h].rearrange("p (k c) -> p k c", c=33)[:, :, 32:33], 1.0), inc=True))
    last = None
    for kt in range(NKT):
        vap = VL[0:64, kt * 128:(kt + 1) * 128]
        dt = C.op("tensor", lambda e, vap=vap: e.transpose(ptr[:, 0:64], vap, sh.ident[0:64, 0:64]),
                  waits=list(deps) + ([last] if last else []), inc=True)
        cps = []
        for h in range(2):
            cps.append(C.op("scalar", lambda e, kt=kt, h=h: e.copy(
                vtok[h][:, kt * 33:kt * 33 + 32], ptr[:, h * 32:(h + 1) * 32]),
                waits=[dt], inc=True))
        last = cps[-1]
    vst.close()
    return [last] + d_ones


def build_A3():
    li = 3
    nc = bass.Bass("TRN2", target_bir_lowering=False)
    es = ExitStack()
    C = Ctx(nc, es)
    x_own_in = C.din("x_own", [D, NTOK], F32)
    x_row_in = C.din("x_row", [D, SEQ], F32)
    x_mid = C.dout("x_mid", [D, NTOK], F32)
    att_out = C.dout("att", [4, 128, SEQ], BF16)
    mid_dram = nc.dram_tensor("mid_bounce", [64, 8 * NTOK], BF16, kind="Internal").ap()
    xnf_dram = nc.dram_tensor("xnf_bounce", [D, SEQ], BF16, kind="Internal").ap()

    cn = Consts(C)
    cn.load(C)
    C.barrier()
    C.shared = cn
    sh = cn
    wins = host_wins_declare(C, li)
    cls_buf = C.sbuf("cls_keep", [128, NT], BF16)

    xst = ExitStack()
    X = C.sbuf("X", [128, NT * NTOK], F32, xst)
    xld = []
    for ct in range(NT):
        xld.append(C.dma("gpsimd", X[:, ct * NTOK:(ct + 1) * NTOK],
                         x_own_in[ct * 128:(ct + 1) * 128, :], key="x"))

    def x_own(ct, t0, t1):
        return X[:, ct * NTOK + t0: ct * NTOK + t1]

    ph1 = ExitStack()
    XN_OWN = C.sbuf("XN_OWN", [128, NT * NTOK], BF16, ph1)

    def xn_own(ct, t0, t1):
        return XN_OWN[:, ct * NTOK + t0: ct * NTOK + t1]

    lns1 = ExitStack()
    alloc_lns(C, lns1)
    ln_dep = emit_ln_chunks(C, lambda ct, o, n: X[:, ct * NTOK + o:ct * NTOK + o + n],
                            lambda ct, o, n: XN_OWN[:, ct * NTOK + o:ct * NTOK + o + n],
                            NTOK // LCH, deps=[cn.dep[-1]] + xld)
    lns1.close()
    C.barrier()
    MIDs = C.sbuf("MIDstage", [64, NTOK], BF16, ph1)
    pre_dep = emit_ffn3_pre(C, cn, xn_own, x_own, MIDs, wins, deps=[ln_dep],
                            dram=(mid_dram, C))
    ph1.close()
    C.barrier()

    ph2big = ExitStack()
    XNF = C.sbuf("XNF", [128, NT * SEQ], BF16, ph2big)
    ph2 = ExitStack()
    alloc_lns(C, ph2)
    XPART = C.sbuf("XPART", [128, NT * LCH], F32, ph2)
    for c in range(SEQ // LCH):
        o = c * LCH
        pld = [C.dma("gpsimd", XPART[:, ct * LCH:(ct + 1) * LCH],
                     x_row_in[ct * 128:(ct + 1) * 128, o:o + LCH],
                     waits=[C.mark("vector"), C.mark("scalar")], key="xp")
               for ct in range(NT)]
        emit_layernorm(C, sh,
                       lambda ct: XPART[:, ct * LCH:(ct + 1) * LCH],
                       lambda ct, o=o: XNF[:, ct * SEQ + o: ct * SEQ + o + LCH],
                       LCH, deps=[cn.dep[-1], C.mark("tensor")] + pld)
    xnf_dep = [C.mark("vector")]
    ph2.close()
    for ct in range(NT):
        C.op("vector", lambda e, ct=ct: e.tensor_copy(cls_buf[:, ct:ct + 1],
                                                      XNF[:, ct * SEQ:ct * SEQ + 1]),
             waits=xnf_dep, inc=True)
        C.dma("gpsimd", xnf_dram[ct * 128:(ct + 1) * 128, :], XNF[:, ct * SEQ:(ct + 1) * SEQ],
              waits=xnf_dep, key="xb")
    cls_dep = [C.mark("vector"), (C.dsem["xb"], C.dsem["xb"].n)]
    ph2big.close()
    C.barrier()

    fp = emit_ffn3_post(C, cn, cls_buf, x_own, mid_dram, wins, deps=cls_dep)
    vend0 = C.mark("vector")
    for ct in range(NT):
        C.dma("gpsimd", x_mid[ct * 128:(ct + 1) * 128, :], X[:, ct * NTOK:(ct + 1) * NTOK],
              waits=[vend0, fp], key="out")
    xst.close()
    C.barrier()

    ast = ExitStack()
    ATT = C.sbuf("ATT", [128, 4 * SEQ], BF16, ast)
    XNF2 = C.sbuf("XNF2", [128, NT * SEQ], BF16, ast)
    xrl = []
    for ct in range(NT):
        xrl.append(C.dma("gpsimd", XNF2[:, ct * SEQ:(ct + 1) * SEQ],
                         xnf_dram[ct * 128:(ct + 1) * 128, :], key="xr"))

    def xnf2(ct, t0, t1):
        return XNF2[:, ct * SEQ + t0: ct * SEQ + t1]
    emit_attn_l3(C, cn, xnf2,
                 lambda g, t0, t1: ATT[:, g * SEQ + t0: g * SEQ + t1],
                 wins, deps=xrl)
    vend = C.mark("vector")
    send = C.mark("scalar")
    for g4 in range(4):
        C.dma("gpsimd", att_out[g4], ATT[:, g4 * SEQ:(g4 + 1) * SEQ],
              waits=[vend, send], key="out")
    fin = (C.dsem["out"], C.dsem["out"].n)
    C.op("gpsimd", lambda e: e.nop(), waits=[fin])
    ast.close()
    C.replay()
    es.close()
    return nc, C


def prep_l3(ins, hp):
    caw = np.asarray(ins["L3_attn_c_attn_w"])
    cab = np.asarray(ins["L3_attn_c_attn_b"])
    strips = []
    bcols = np.zeros((128, 12), np.float32)
    for g in range(4):
        gh = 8 * hp + 2 * g
        for j, base in enumerate((D, 2 * D, 0)):
            rows = slice(base + 64 * gh, base + 64 * gh + 128)
            strips.append(host_tile_w(np.ascontiguousarray(caw[rows, :].T))[0])
            bcols[:, g * 3 + j] = cab[rows]
    out = {"w_qkv_strips": np.ascontiguousarray(np.stack(strips)), "qkv_bcols": bcols}
    q2l = np.asarray(ins["L3_attn_q2l_w"])
    kv2l = np.asarray(ins["L3_attn_kv2l_w"])
    wqk = np.asarray(ins["L3_attn_wqk_w"])
    l2ow = np.asarray(ins["L3_attn_l2o_w"])
    lat = np.zeros((8, 128, 128), np.float32)
    for i, (w, par) in enumerate(((q2l, 0), (q2l, 1), (kv2l, 0), (kv2l, 1))):
        for hb in (0, 1):
            for j in range(16):
                lat[i, hb * 64:(hb + 1) * 64, hb * 64 + j] = w[2 * j + par, :]
    for idx, (pin, pout) in enumerate(((0, 0), (1, 0), (0, 1), (1, 1))):
        for hb in (0, 1):
            for jin in range(16):
                for jout in range(16):
                    lat[4 + idx, hb * 64 + jin, hb * 64 + jout] = wqk[2 * jout + pout, 2 * jin + pin]
    out["lat_bd"] = lat
    v2l = np.zeros((128, 64), np.float32)
    for hb in (0, 1):
        for j in range(32):
            v2l[hb * 64:(hb + 1) * 64, hb * 32 + j] = kv2l[j, :]
    out["v2l_bd"] = v2l
    l2o_bd = np.zeros((64, 128), np.float32)
    for hb in (0, 1):
        for l in range(32):
            l2o_bd[hb * 32 + l, hb * 64:(hb + 1) * 64] = l2ow[:, l]
    out["l2o_bd"] = l2o_bd
    latb = np.zeros((128, 8), np.float32)
    q2lb = np.asarray(ins["L3_attn_q2l_b"])
    kv2lb = np.asarray(ins["L3_attn_kv2l_b"])
    wqkb = np.asarray(ins["L3_attn_wqk_b"])
    for hb in (0, 1):
        for j in range(16):
            latb[hb * 64 + j, 0] = q2lb[2 * j]
            latb[hb * 64 + j, 1] = q2lb[2 * j + 1]
            latb[hb * 64 + j, 2] = kv2lb[2 * j]
            latb[hb * 64 + j, 3] = kv2lb[2 * j + 1]
            latb[hb * 64 + j, 5] = wqkb[2 * j]
            latb[hb * 64 + j, 6] = wqkb[2 * j + 1]
        for j in range(32):
            latb[hb * 64 + j, 4] = kv2lb[j]
    latb[:, 4] = 0.0
    vb = np.zeros(128, np.float32)
    vb[0:32] = kv2lb
    vb[32:64] = kv2lb
    latb[:, 4] = vb
    out["lat_bcols"] = latb
    l2obc = np.zeros((128, 1), np.float32)
    l2obc[0:64, 0] = np.asarray(ins["L3_attn_l2o_b"])
    l2obc[64:128, 0] = np.asarray(ins["L3_attn_l2o_b"])
    out["l2o_bcols"] = l2obc
    i = np.arange(16)
    invf = 10000.0 ** (-2.0 * i / 32.0)
    t = np.arange(SEQ)
    ang = t[None, :] * invf[:, None]
    blk = np.zeros((32, SEQ), np.float32)
    cs = blk.copy(); cs[0:16] = np.cos(ang)
    sn = blk.copy(); sn[0:16] = np.sin(ang)
    out["rope_cos"] = np.ascontiguousarray(np.tile(cs, (4, 1))).astype(BFNP)
    out["rope_sin"] = np.ascontiguousarray(np.tile(sn, (4, 1))).astype(BFNP)
    out["wr_tiles"] = tiles_cm(np.ascontiguousarray(np.asarray(ins["L3_ffn_Wr"]).T))
    A = np.asarray(ins["L3_ffn_A"])
    a_tiles = np.concatenate([tiles_cm(np.ascontiguousarray(A[e].T)) for e in range(8)], axis=1)
    out["a_tiles"] = np.ascontiguousarray(a_tiles)
    Bw = np.asarray(ins["L3_ffn_Bw"])
    bw = np.zeros((8, 64, 8 * 128), np.float32)
    for ot in range(8):
        for e in range(8):
            bw[ot, :, e * 128:(e + 1) * 128] = Bw[e, ot * 128:(ot + 1) * 128, :].T
    out["bw_strips"] = bw
    out["wo_strips"] = host_tile_w(np.ascontiguousarray(np.asarray(ins["L3_attn_cproj_w"]).T))
    out["bo_cols"] = np.asarray(ins["L3_attn_cproj_b"]).reshape(NT, 128).T.copy()
    return out



RG_PAIRS = [[0, 1], [2, 3], [4, 5], [6, 7]]


def build_fused():
    nc = bass.Bass("TRN2", target_bir_lowering=False)
    es = ExitStack()
    C = Ctx(nc, es)
    x_in = C.din("x_own", [D, NTOK], F32)
    x_out = C.dout("x_next", [D, NTOK], F32)
    x_cur = nc.dram_tensor("x_cur", [D, NTOK], F32, kind="Internal").ap()
    xn_gin = nc.dram_tensor("xn_gin", [D, NTOK], BF16, kind="Internal").ap()
    xn_gout = nc.dram_tensor("xn_gout", [2 * D, NTOK], BF16, kind="Internal").ap()
    u_in = nc.dram_tensor("u_in", [2 * D, NTOK], BF16, kind="Internal").ap()
    u_out = nc.dram_tensor("u_out", [D, NTOK], BF16, kind="Internal").ap()
    mid_dram = nc.dram_tensor("mid_bounce", [64, 8 * NTOK], BF16, kind="Internal").ap()

    cn = Consts(C)
    cn.load(C)
    C.barrier()
    C.shared = cn
    sh = cn
    WINS = {}
    for li in range(4):
        C.prefix = "L%d_" % li
        WINS[li] = host_wins_declare(C, li)
        WINS[li]["wo_my"] = C.din("wo_my", [NT, 128, 4 * 128], F32)
        WINS[li]["bo"] = C.din("bo_cols", [128, NT], F32)
    C.prefix = ""

    for li in range(4):
        wins = WINS[li]
        st1 = ExitStack()
        X = C.sbuf("X", [128, NT * NTOK], F32, st1)
        XN_OWN = C.sbuf("XN_OWN", [128, NT * NTOK], BF16, st1)
        xld = []
        src = x_in if li == 0 else x_cur
        for ct in range(NT):
            xld.append(C.dma("gpsimd", X[:, ct * NTOK:(ct + 1) * NTOK],
                             src[ct * 128:(ct + 1) * 128, :], key="x",
                             waits=C.fence() if li > 0 else []))

        def x_own(ct, t0, t1, X=X):
            return X[:, ct * NTOK + t0: ct * NTOK + t1]

        def xn_own(ct, t0, t1, XN_OWN=XN_OWN):
            return XN_OWN[:, ct * NTOK + t0: ct * NTOK + t1]

        lns1 = ExitStack()
        alloc_lns(C, lns1)
        ln_dep = emit_ln_chunks(C, lambda ct, o, n, X=X: X[:, ct * NTOK + o:ct * NTOK + o + n],
                                lambda ct, o, n, XN_OWN=XN_OWN: XN_OWN[:, ct * NTOK + o:ct * NTOK + o + n],
                                NTOK // LCH, deps=[cn.dep[-1]] + xld)
        lns1.close()
        C.barrier()
        gi = []
        for ct in range(NT):
            gi.append(C.dma("gpsimd", xn_gin[ct * 128:(ct + 1) * 128, :],
                            XN_OWN[:, ct * NTOK:(ct + 1) * NTOK], key="gi",
                            waits=[ln_dep]))
        ag = C.cc("AllGather", ALU.bypass, RG_PAIRS, [xn_gin], [xn_gout],
                  waits=gi, key="ag")
        if li == 3:
            m3 = ExitStack()
            MIDs = C.sbuf("MIDstage", [64, NTOK], BF16, m3)
            pre = emit_ffn3_pre(C, cn, xn_own, x_own, MIDs, wins, deps=[ln_dep],
                                dram=(mid_dram, C))
            m3.close()
            p3 = ExitStack()
            cls_buf = C.sbuf("cls3", [128, NT], BF16, p3)
            cl = []
            for ct in range(NT):
                cl.append(C.dma("gpsimd", cls_buf[:, ct:ct + 1],
                                xn_gout[ct * 128:ct * 128 + 128, 0:1], key="cl",
                                waits=[ag] + pre, allow_slow_non_contiguous=True))
            fp = emit_ffn3_post(C, cn, cls_buf, x_own, mid_dram, wins,
                                deps=cl + pre)
            p3.close()
            ffn_dep = fp
        else:
            ffn_dep = emit_ffn(C, li, xn_own, x_own, wins, deps=[ln_dep])
        vend0 = C.mark("vector")
        for ct in range(NT):
            C.dma("gpsimd", x_cur[ct * 128:(ct + 1) * 128, :], X[:, ct * NTOK:(ct + 1) * NTOK],
                  waits=[vend0, ffn_dep], key="xs")
        st1.close()
        C.barrier()

        st2 = ExitStack()
        ATT = C.sbuf("ATT", [128, 4 * SEQ], BF16, st2)
        sub = ExitStack()
        XNF = C.sbuf("XNF", [128, NT * SEQ], BF16, sub)
        xfl = []
        for ct in range(NT):
            for half in range(2):
                xfl.append(C.dma("gpsimd", XNF[:, ct * SEQ + half * NTOK: ct * SEQ + (half + 1) * NTOK],
                                 xn_gout[half * D + ct * 128: half * D + (ct + 1) * 128, :],
                                 key="xf", waits=[ag]))
        xnf_dep = list(xfl[-1:])
        if li == 1:
            lns2 = ExitStack()
            alloc_lns(C, lns2)
            gcol = C.sbuf("gcol", [128, NT], F32, lns2)
            bcol = C.sbuf("bcol", [128, NT], F32, lns2)
            dg = C.dma("sync", gcol[:], wins["gamma"], key="c1")
            db = C.dma("sync", bcol[:], wins["beta"], key="c2")
            emit_ln_chunks(C, lambda ct, o, n, XNF=XNF: XNF[:, ct * SEQ + o:ct * SEQ + o + n],
                           lambda ct, o, n, XNF=XNF: XNF[:, ct * SEQ + o:ct * SEQ + o + n],
                           SEQ // LCH, deps=[C.mark("vector"), dg, db] + xnf_dep,
                           affine=(lambda ct, gcol=gcol: gcol[:, ct:ct + 1],
                                   lambda ct, bcol=bcol: bcol[:, ct:ct + 1]))
            lns2.close()
            xnf_dep = [C.mark("vector")]

        def xnf(ct, t0, t1, XNF=XNF):
            return XNF[:, ct * SEQ + t0: ct * SEQ + t1]
        if li == 3:
            emit_attn_l3(C, cn, xnf,
                         lambda g, t0, t1, ATT=ATT: ATT[:, g * SEQ + t0: g * SEQ + t1],
                         wins, deps=xnf_dep + xfl)
        else:
            emit_attn(C, li, xnf,
                      lambda g, h, s, ATT=ATT: ATT[64 * h:64 * h + 64, g * SEQ + s * 512: g * SEQ + (s + 1) * 512],
                      wins, deps=xnf_dep + xfl)
        att_dep = [C.mark("vector"), C.mark("scalar")]
        sub.close()
        C.barrier()
        ust = ExitStack()
        wst = [C.sbuf("uw", [128, 4 * 128], F32, ust) for _ in range(2)]
        wbf = [C.sbuf("uwb", [128, 4 * 128], BF16, ust) for _ in range(2)]
        urow = [C.sbuf("urow", [128, 512], BF16, ust) for _ in range(2)]
        psu = [C.psum("psu", [128, 512], F32, ust) for _ in range(2)]
        wcast = [None, None]
        ud = []
        for ot in range(NT):
            b = ot % 2
            dd = C.dma("sync", wst[b][:], wins["wo_my"][ot], key="w%d" % b,
                       waits=[wcast[b]] if wcast[b] else [C.mark("tensor"), C.mark("vector")])
            wcast[b] = C.op("vector", lambda e, b=b, wbf=wbf, wst=wst: e.tensor_copy(
                wbf[b][:], wst[b][:]),
                waits=[dd, C.mark("tensor")], inc=True)
            for sg in range(NSG):
                pp = psu[sg % 2]
                for g4 in range(4):
                    C.op("tensor", lambda e, g4=g4, sg=sg, b=b, pp=pp, ATT=ATT, wbf=wbf: e.matmul(
                        pp[:, 0:512], wbf[b][:, g4 * 128:(g4 + 1) * 128],
                        ATT[:, g4 * SEQ + sg * 512:g4 * SEQ + (sg + 1) * 512],
                        start=(g4 == 0), stop=(g4 == 3)),
                        waits=[wcast[b], C.mark("scalar")] + att_dep if g4 == 0 else (),
                        inc=(g4 == 3))
                mmd = C.mark("tensor")
                uiw = (C.dsem["ui"], C.dsem["ui"].n) if "ui" in C.dsem else None
                ue = C.op("scalar", lambda e, sg=sg, pp=pp, urow=urow: e.copy(
                    urow[sg % 2][:], pp[:, 0:512]),
                          waits=[mmd, uiw], inc=True)
                half = (sg * 512) // NTOK
                tg = (sg * 512 - half * NTOK) // 512
                ud.append(C.dma("gpsimd", u_in[half * D + ot * 128: half * D + (ot + 1) * 128,
                                               tg * 512:(tg + 1) * 512],
                                urow[sg % 2][:], key="ui", waits=[ue]))
        ust.close()
        st2.close()
        C.barrier()
        rs = C.cc("ReduceScatter", ALU.add, RG_PAIRS, [u_in], [u_out],
                  waits=ud, key="rs")
        st3 = ExitStack()
        X2 = C.sbuf("X2", [128, NT * NTOK], F32, st3)
        R = C.sbuf("R", [128, NT * NTOK], BF16, st3)
        bocol = C.sbuf("bocol", [128, NT], F32, st3)
        dbo = C.dma("sync", bocol[:], wins["bo"], key="c0", waits=C.fence())
        x2l, rl = [], []
        for ct in range(NT):
            x2l.append(C.dma("gpsimd", X2[:, ct * NTOK:(ct + 1) * NTOK],
                             x_cur[ct * 128:(ct + 1) * 128, :], key="x2",
                             waits=[(C.dsem["xs"], C.dsem["xs"].n)]))
            rl.append(C.dma("gpsimd", R[:, ct * NTOK:(ct + 1) * NTOK],
                            u_out[ct * 128:(ct + 1) * 128, :], key="r", waits=[rs]))
        dst = x_out if li == 3 else x_cur
        for ct in range(NT):
            d1 = C.op("vector", lambda e, ct=ct, X2=X2, R=R, bocol=bocol: e.scalar_tensor_tensor(
                X2[:, ct * NTOK:(ct + 1) * NTOK], R[:, ct * NTOK:(ct + 1) * NTOK],
                bocol[:, ct:ct + 1], X2[:, ct * NTOK:(ct + 1) * NTOK], ALU.add, ALU.add),
                waits=[x2l[-1], rl[-1], dbo], inc=True)
            C.dma("gpsimd", dst[ct * 128:(ct + 1) * 128, :], X2[:, ct * NTOK:(ct + 1) * NTOK],
                  waits=[d1], key="out" if li == 3 else "xs")
        st3.close()
        C.barrier()
    fin = (C.dsem["out"], C.dsem["out"].n)
    C.op("gpsimd", lambda e: e.nop(), waits=[fin])
    C.replay()
    es.close()
    return nc, C



_PREP = {0: prep_l0, 1: prep_l1, 2: prep_l2, 3: prep_l3}


def kernel(**inputs):
    set_size(4096)
    consts = host_consts()
    x = np.asarray(inputs["L0_attn_x"], dtype=np.float32)
    nc, C = build_fused()
    in_maps = []
    for c in range(8):
        r, hp = c // 2, c % 2
        d = dict(consts)
        for li in range(4):
            p = _PREP[li](inputs, hp)
            if li == 1:
                p["wo_strips"] = host_tile_w(np.eye(D, dtype=np.float32))
                p["bo_cols"] = np.zeros((128, NT), np.float32)
            p["wo_my"] = np.ascontiguousarray(p["wo_strips"][:, :, hp * 512:(hp + 1) * 512])
            for k, v in p.items():
                d["L%d_%s" % (li, k)] = v
        d = {k: v for k, v in d.items() if k in C.inputs}
        d["x_own"] = np.ascontiguousarray(x[r, hp * NTOK:(hp + 1) * NTOK, :].T)
        in_maps.append(d)
    res = run_bass_kernel_spmd(nc, in_maps, core_ids=list(range(8))).results
    out = np.empty((4, SEQ, D), np.float32)
    for c in range(8):
        r, hp = c // 2, c % 2
        out[r, hp * NTOK:(hp + 1) * NTOK, :] = np.asarray(res[c]["x_next"]).T
    return out


# version 04
# speedup vs baseline: 2.0112x; 1.0813x over previous
import numpy as np
import ml_dtypes
import concourse.bass as bass
import concourse.mybir as mybir
from concourse.bass_utils import run_bass_kernel_spmd
from contextlib import ExitStack

F32 = mybir.dt.float32
BF16 = mybir.dt.bfloat16
AF = mybir.ActivationFunctionType
ALU = mybir.AluOpType
AXX = mybir.AxisListType.X
BFNP = ml_dtypes.bfloat16

D = 1024
SEQ = 4096
LCH = 512
NTOK = 2048
NT = D // 128
TG = 512
NTG = NTOK // TG
NSG = SEQ // TG
NKT = SEQ // 128
NHEAD_OWN = 8


class Sem:
    def __init__(self, handle):
        self.h = handle
        self.n = 0

    def tick(self, k=1):
        self.n += k
        return self.n


class Ctx:
    ENGINES = ("sync", "gpsimd", "scalar", "vector", "tensor")

    def __init__(self, nc, es):
        self.nc = nc
        self.es = es
        self.q = {e: [] for e in self.ENGINES}
        self.sem = {e: Sem(es.enter_context(nc.semaphore("s_" + e))) for e in self.ENGINES}
        self.hw = {e: {} for e in self.ENGINES}
        self.uid = 0
        self.inputs = {}
        self.dsem = {}

    def _emit(self, eng, fn, waits, inc):
        hw = self.hw[eng]
        best = {}
        for s, v in waits:
            if s is None:
                continue
            if v > best.get(id(s), (None, 0))[1]:
                best[id(s)] = (s, v)
        waits2 = []
        for s, v in best.values():
            if v > 0 and hw.get(id(s), 0) < v:
                waits2.append((s, v))
                hw[id(s)] = v

        def thunk(e, fn=fn, waits2=waits2, inc=inc):
            for s, v in waits2:
                e.wait_ge(s.h, v)
            ins = fn(e)
            if inc is not None and ins is not None:
                ins.then_inc(inc[0].h, inc[1])
        self.q[eng].append(thunk)

    def op(self, eng, fn, waits=(), inc=False, inc_amt=1):
        s = self.sem[eng]
        incpair = None
        ret = None
        if inc:
            v = s.tick(inc_amt)
            incpair = (s, inc_amt)
            ret = (s, v)
        self._emit(eng, fn, [w for w in waits if w], incpair)
        return ret

    def cc(self, kind, op, rg, ins, outs, waits=(), key="cc"):
        if key not in self.dsem:
            self.dsem[key] = Sem(self.es.enter_context(self.nc.semaphore("d_" + key)))
        s = self.dsem[key]
        v = s.tick(1)
        self._emit("gpsimd", lambda e: e.collective_compute(
            kind, op, replica_groups=rg, ins=[a.opt() for a in ins],
            outs=[a.opt() for a in outs]), [w for w in waits if w], (s, 1))
        return (s, v)

    def dma(self, eng, out, in_, waits=(), key="misc", **kw):
        if key not in self.dsem:
            self.dsem[key] = Sem(self.es.enter_context(self.nc.semaphore("d_" + key)))
        s = self.dsem[key]
        v = s.tick(16)
        self._emit(eng, lambda e: e.dma_start(out=out, in_=in_, **kw), [w for w in waits if w], (s, 16))
        return (s, v)

    def mark(self, eng):
        return (self.sem[eng], self.sem[eng].n)

    def all_marks(self):
        return [self.mark(e) for e in self.ENGINES if self.sem[e].n > 0]

    def fence(self):
        f = [self.mark(e) for e in self.ENGINES if self.sem[e].n > 0]
        f += [(s, s.n) for s in self.dsem.values() if s.n > 0]
        return f

    def barrier(self):
        f = self.fence()
        for e in self.ENGINES:
            self._emit(e, lambda en: en.nop(), f, None)

    def din(self, name, shape, dtype):
        name = getattr(self, "prefix", "") + name
        self.inputs[name] = (tuple(shape), dtype)
        return self.nc.dram_tensor(name, list(shape), dtype, kind="ExternalInput").ap()

    def dout(self, name, shape, dtype):
        return self.nc.dram_tensor(name, list(shape), dtype, kind="ExternalOutput").ap()

    def sbuf(self, name, shape, dtype, stack=None):
        self.uid += 1
        return (stack or self.es).enter_context(
            self.nc.sbuf_tensor(f"{name}_{self.uid}", shape, dtype))

    def psum(self, name, shape, dtype, stack=None):
        self.uid += 1
        return (stack or self.es).enter_context(
            self.nc.psum_tensor(f"{name}_{self.uid}", shape, dtype))

    def replay(self):
        with self.nc.allow_low_precision(reason="intentional bf16 matmul operand prep; fp32 accumulation kept"), \
             self.nc.Block() as block:
            for eng in self.ENGINES:
                thunks = self.q[eng]
                if not thunks:
                    continue
                getattr(block, eng)(lambda e, thunks=thunks: [t(e) for t in thunks])



class Shared:

    def __init__(self, C):
        self.C = C
        self.ones_col = C.sbuf("ones_col", [128, 1], BF16)
        self.onesb_row = C.sbuf("onesb_row", [1, 128], BF16)
        self.ident = C.sbuf("ident", [128, 128], BF16)
        self.trimask = C.sbuf("trimask", [128, 1152], BF16)
        self.sel8 = C.sbuf("sel8", [8, 8 * 128], BF16)
        self.xb = C.sbuf("ln_xb", [128, NT * SEQ // 2], BF16)
        self.sq = C.sbuf("ln_sq", [128, NT * SEQ // 2], BF16)
        self.musd = C.sbuf("ln_musd", [4, SEQ], F32)
        self.mu_bf = C.sbuf("ln_mubf", [1, SEQ], BF16)
        self.rs_bf = C.sbuf("ln_rsbf", [1, SEQ], BF16)
        self.mu_b = C.sbuf("ln_mub", [128, SEQ], BF16)
        self.rs_b = C.sbuf("ln_rsb", [128, SEQ], BF16)

    def load(self, C):
        ident = C.din("c_ident", [128, 128], BF16)
        trimask = C.din("c_trimask", [128, 1152], BF16)
        sel8 = C.din("c_sel8", [8, 8 * 128], BF16)
        d1 = C.dma("sync", self.ident[:], ident, key="const")
        d2 = C.dma("sync", self.trimask[:], trimask, key="const")
        d2b = C.dma("sync", self.sel8[:], sel8, key="const")
        d3 = C.op("vector", lambda e: e.memset(self.ones_col[:], 1.0), inc=True)
        d4 = C.op("vector", lambda e: e.memset(self.onesb_row[:], 1.0), inc=True)
        self.dep = [d1, d2, d3, d4]


def host_consts():
    tri = np.zeros((128, 1152), np.float32)
    for m in range(128):
        tri[m, m + 640:] = 1.0
    sel = np.zeros((8, 8 * 128), np.float32)
    for e in range(8):
        sel[e, e * 128:(e + 1) * 128] = 1.0
    return {
        "c_ident": np.eye(128, dtype=np.float32).astype(BFNP),
        "c_trimask": tri.astype(BFNP),
        "c_sel8": sel.astype(BFNP),
    }



def emit_layernorm(C, sh, x_tiles, xn_tiles, n, off=0, deps=(), eps=1e-5, affine=None):
    xb, sq, musd = sh.xb, sh.sq, sh.musd
    mu_bf, rs_bf, mu_b, rs_b = sh.mu_bf, sh.rs_bf, sh.mu_b, sh.rs_b
    ps_stat = sh.ps_stat
    ps_b = sh.ps_b

    for ct in range(NT):
        C.op("vector", lambda e, ct=ct: e.tensor_copy(xb[:, ct * n:(ct + 1) * n], x_tiles(ct)),
             waits=deps, inc=(ct == NT - 1))
        C.op("scalar", lambda e, ct=ct: e.activation(out=sq[:, ct * n:(ct + 1) * n],
                                                     in_=x_tiles(ct), func=AF.Square),
             waits=deps, inc=(ct == NT - 1))
    cv, cs = C.mark("vector"), C.mark("scalar")
    ngr = (n + TG - 1) // TG
    for g in range(ngr):
        w = min(TG, n - g * TG)
        for ct in range(NT):
            C.op("tensor", lambda e, ct=ct, g=g, w=w: e.matmul(
                ps_stat[0:1, g * TG:g * TG + w], sh.ones_col[:, 0:1],
                xb[:, ct * n + g * TG: ct * n + g * TG + w],
                start=(ct == 0), stop=(ct == NT - 1)),
                waits=[cv] if ct == 0 else (), inc=(ct == NT - 1))
        for ct in range(NT):
            C.op("tensor", lambda e, ct=ct, g=g, w=w: e.matmul(
                ps_stat[0:1, n + g * TG:n + g * TG + w], sh.ones_col[:, 0:1],
                sq[:, ct * n + g * TG: ct * n + g * TG + w],
                start=(ct == 0), stop=(ct == NT - 1)),
                waits=[cs] if ct == 0 else (), inc=(ct == NT - 1))
    mm = C.mark("tensor")
    o = off
    S0 = musd[0:1, 0:n]
    S1 = musd[0:1, n:2 * n]
    S2 = musd[0:1, 2 * n:3 * n]
    S3 = musd[0:1, 3 * n:4 * n]
    d = C.op("vector", lambda e: e.tensor_scalar(S2, ps_stat[0:1, 0:n], 1.0 / D, None, ALU.mult),
             waits=[mm], inc=True)
    d2 = C.op("vector", lambda e: e.tensor_scalar(S1, ps_stat[0:1, n:2 * n], 1.0 / D, None, ALU.mult),
              waits=[mm], inc=True)
    d3 = C.op("vector", lambda e: e.tensor_tensor(S0, S2, S2, ALU.mult),
              waits=[d], inc=True)
    d4 = C.op("vector", lambda e: e.tensor_tensor(S1, S1, S0, ALU.subtract),
              waits=[d2, d3], inc=True)
    d5 = C.op("vector", lambda e: e.tensor_scalar(S1, S1, float(eps), None, ALU.add),
              waits=[d4], inc=True)
    d6 = C.op("scalar", lambda e: e.activation(out=S3, in_=S1, func=AF.Sqrt),
              waits=[d5], inc=True)
    d7 = C.op("vector", lambda e: e.reciprocal(rs_bf[:, 0:n], S3), waits=[d6], inc=True)
    d8 = C.op("vector", lambda e: e.tensor_copy(mu_bf[:, 0:n], S2), waits=[d3], inc=True)
    for g in range(ngr):
        w = min(TG, n - g * TG)
        C.op("tensor", lambda e, g=g, w=w: e.matmul(ps_b[:, g * TG:g * TG + w], sh.onesb_row[0:1, :],
                                                    mu_bf[0:1, g * TG:g * TG + w], start=True, stop=True),
             waits=[d8], inc=(g == ngr - 1))
    bm = C.mark("tensor")
    bme = C.op("vector", lambda e: e.tensor_copy(mu_b[:, 0:n], ps_b[:, 0:n]), waits=[bm], inc=True)
    for g in range(ngr):
        w = min(TG, n - g * TG)
        C.op("tensor", lambda e, g=g, w=w: e.matmul(ps_b[:, g * TG:g * TG + w], sh.onesb_row[0:1, :],
                                                    rs_bf[0:1, g * TG:g * TG + w], start=True, stop=True),
             waits=[d7, bme], inc=(g == ngr - 1))
    br = C.mark("tensor")
    bre = C.op("vector", lambda e: e.tensor_copy(rs_b[:, 0:n], ps_b[:, 0:n]), waits=[br], inc=True)
    last = None
    for ct in range(NT):
        da = C.op("vector", lambda e, ct=ct: e.tensor_tensor(xb[:, ct * n:(ct + 1) * n], x_tiles(ct),
                                                             mu_b[:, 0:n], ALU.subtract),
                  waits=[bme], inc=True)
        if affine is None:
            last = C.op("vector", lambda e, ct=ct: e.tensor_tensor(xn_tiles(ct), xb[:, ct * n:(ct + 1) * n],
                                                                   rs_b[:, 0:n], ALU.mult),
                        waits=[da, bre], inc=True)
        else:
            gcol, bcol = affine
            db = C.op("vector", lambda e, ct=ct: e.tensor_tensor(sq[:, ct * n:(ct + 1) * n],
                                                                 xb[:, ct * n:(ct + 1) * n],
                                                                 rs_b[:, 0:n], ALU.mult),
                      waits=[da, bre, cs, mm], inc=True)
            last = C.op("vector", lambda e, ct=ct: e.tensor_scalar(
                xn_tiles(ct), sq[:, ct * n:(ct + 1) * n], gcol(ct), bcol(ct), ALU.mult, ALU.add),
                waits=[db], inc=True)
    return last



def emit_stream_linear(C, wst, n_ct, n_ot, rhs_fn, tok_ranges, evac, wbufs, psums,
                       rhs_deps=(), ot_list=None, w_waits=()):
    stage, wbf = wbufs
    nps = len(psums)
    free_dep = [None] * nps
    cast_done = [None, None]
    last_mm_w = [None, None]
    ots = list(range(n_ot)) if ot_list is None else ot_list
    slot = 0
    for oi, ot in enumerate(ots):
        b = oi % 2
        cast_done[b] = C.dma("gpsimd", wbf[b][:], wst[ot],
                             waits=[last_mm_w[b]] if last_mm_w[b] else list(w_waits),
                             key="sw%d" % b)
        for (t0, t1) in tok_ranges:
            ps = psums[slot % nps]
            mmw = [cast_done[b]] + list(rhs_deps)
            if free_dep[slot % nps] is not None:
                mmw.append(free_dep[slot % nps])
            for ct in range(n_ct):
                C.op("tensor", lambda e, ct=ct, t0=t0, t1=t1, b=b, ps=ps: e.matmul(
                    ps[:, 0:t1 - t0], wbf[b][:, ct * 128:(ct + 1) * 128],
                    rhs_fn(ct, t0, t1),
                    start=(ct == 0), stop=(ct == n_ct - 1)),
                    waits=mmw if ct == 0 else (), inc=(ct == n_ct - 1))
            mm_dep = C.mark("tensor")
            last_mm_w[b] = mm_dep
            free_dep[slot % nps] = evac(ot, t0, t1, ps, mm_dep)
            slot += 1
    return C.mark("tensor")


def host_tile_w(wT, n_ct=None, n_ot=None):
    ci, co = wT.shape
    n_ct = n_ct or ci // 128
    n_ot = n_ot or co // 128
    t = wT.reshape(n_ct, 128, n_ot, 128).transpose(2, 1, 0, 3).reshape(n_ot, 128, n_ct * 128)
    return np.ascontiguousarray(t)



def emit_attn_group(C, sh, ab, heads, score_chunks, vtok_ap, av_rows, out_ap,
                    exp_scale, pre_exp=None, deps=()):
    nh = len(heads)
    exp_dep = {}
    probs_src = {}
    av_free = {h: None for h in heads}
    for s in range(NSG):
        nkt = 4 * s + 4
        score_dep = {}
        for kt in range(nkt):
            for h in heads:
                r = kt % 3
                w = []
                if kt >= 3:
                    w.append(exp_dep[(h, kt - 3)])
                if kt == 0 and av_free[h] is not None:
                    w.append(av_free[h])
                w += list(deps)
                chunks = score_chunks(h)
                for ci, (k_fn, q_fn, trow) in enumerate(chunks):
                    kap, qap, oap = k_fn(kt), q_fn(s), ab["sc"][h][r][:, 0:512]
                    C.op("tensor", lambda e, kap=kap, qap=qap, oap=oap, trow=trow,
                         st=(ci == 0), sp=(ci == len(chunks) - 1): e.matmul(
                             oap, kap, qap,
                             start=st, stop=sp, tile_position=(trow, 0)),
                         waits=w if ci == 0 else (), inc=(ci == len(chunks) - 1))
                score_dep[(h, kt)] = C.mark("tensor")
            if kt >= 2:
                for h in heads:
                    vap, pap = vtok_ap(h, kt - 2), probs_src[(h, kt - 2)]
                    C.op("tensor", lambda e, vap=vap, pap=pap, h=h,
                         st=(kt - 2 == 0): e.matmul(
                        ab["av"][h][0:av_rows, 0:512], vap, pap,
                        start=st, stop=False),
                        waits=[exp_dep[(h, kt - 2)]], inc=True)
            for h in heads:
                r = kt % 3
                pb = ab["probs"][h][:, r * 512:(r + 1) * 512]
                src = ab["sc"][h][r][:, 0:512]
                dep = score_dep[(h, kt)]
                if pre_exp is not None:
                    src, dep = pre_exp(h, s, kt, src, dep)
                dd = C.op("scalar", lambda e, pb=pb, src=src: e.activation(
                    out=pb, in_=src, func=AF.Exp, scale=float(exp_scale)),
                    waits=[dep], inc=True)
                if kt >= 4 * s:
                    c0 = 640 + (512 * s - 128 * kt)
                    dd = C.op("vector", lambda e, pb=pb, c0=c0: e.tensor_tensor(
                        pb, pb, sh.trimask[:, c0:c0 + 512], ALU.mult),
                        waits=[dd], inc=True)
                exp_dep[(h, kt)] = dd
                probs_src[(h, kt)] = pb
        for h in heads:
            for ktf in ([nkt - 2] if nkt >= 2 else []) + [nkt - 1]:
                vap, pap = vtok_ap(h, ktf), probs_src[(h, ktf)]
                C.op("tensor", lambda e, vap=vap, pap=pap, h=h, st=(ktf == 0),
                     sp=(ktf == nkt - 1): e.matmul(
                    ab["av"][h][0:av_rows, 0:512], vap, pap,
                    start=st, stop=sp),
                    waits=[exp_dep[(h, ktf)]], inc=True)
            avd = C.mark("tensor")
            dv = av_rows - 1
            d1 = C.op("scalar", lambda e, h=h: e.copy(ab["sums"][h][:, 0:512],
                                                      ab["av"][h][dv:dv + 1, 0:512]),
                      waits=[avd], inc=True)
            d2 = C.op("scalar", lambda e, h=h: e.copy(ab["att_sb"][h][0:dv, 0:512],
                                                      ab["av"][h][0:dv, 0:512]),
                      waits=[avd], inc=True)
            d3 = C.op("vector", lambda e, h=h: e.reciprocal(ab["recip"][h][:, 0:512],
                                                            ab["sums"][h][:, 0:512]),
                      waits=[d1], inc=True)
            d4 = C.op("tensor", lambda e, h=h, dv=dv: e.matmul(
                ab["av"][h][0:dv, 0:512], sh.onesb_row[0:1, 0:dv],
                ab["recip"][h][0:1, 0:512], start=True, stop=True),
                waits=[d2, d3], inc=True)
            oap = out_ap(h, s)
            d5 = C.op("vector", lambda e, oap=oap, h=h, dv=dv: e.tensor_tensor(
                oap, ab["att_sb"][h][0:dv, 0:512],
                ab["av"][h][0:dv, 0:512], ALU.mult),
                waits=[d4], inc=True)
            av_free[h] = d5
    return C.mark("tensor")


def make_attn_bufs(C, st, heads, av_rows):
    ab = {"sc": {}, "av": {}, "probs": {}, "att_sb": {}, "sums": {}, "recip": {}}
    for h in heads:
        ab["sc"][h] = [C.psum("sc", [128, 512], F32, st) for _ in range(3)]
        ab["av"][h] = C.psum("av", [128, 512], F32, st)
        ab["probs"][h] = C.sbuf("probs", [128, 3 * 512], BF16, st)
        ab["att_sb"][h] = C.sbuf("attsb", [av_rows - 1, 512], BF16, st)
        ab["sums"][h] = C.sbuf("sums", [1, 512], F32, st)
        ab["recip"][h] = C.sbuf("recip", [1, 512], BF16, st)
    return ab


def emit_vtok(C, sh, v_cm_rows_fn, vtok, n_heads2, deps=()):
    vst = ExitStack()
    ptr = C.psum("ptr", [128, 128], BF16, vst)
    d_ones = []
    for h in range(n_heads2):
        d_ones.append(C.op("vector", lambda e, h=h: e.memset(
            vtok[h].rearrange("p (k c) -> p k c", c=65)[:, :, 64:65], 1.0), inc=True))
    last = None
    for kt in range(NKT):
        vap = v_cm_rows_fn(kt)
        dt = C.op("tensor", lambda e, vap=vap: e.transpose(
            ptr[:], vap, sh.ident[:]),
            waits=list(deps) + ([last] if last else []), inc=True)
        cps = []
        for h in range(n_heads2):
            cps.append(C.op("scalar", lambda e, kt=kt, h=h: e.copy(
                vtok[h][:, kt * 65:kt * 65 + 64], ptr[:, h * 64:(h + 1) * 64]),
                waits=[dt], inc=True))
        last = cps[-1]
    vst.close()
    return [last] + d_ones



def emit_ffn1(C, sh, xn_own_tiles, x_tiles, win, deps=()):
    bdst = ExitStack()
    BD = [C.sbuf("BD", [128, 32 * 128], BF16, bdst) for _ in range(4)]
    bst = ExitStack()
    rm_sb = C.sbuf("rmsb", [16, 8 * 128], BF16, bst)
    rm_f = C.sbuf("rmf", [16, 8 * 128], F32, bst)
    rup_sb = C.sbuf("rupsb", [16, 8 * 64], BF16, bst)
    rup_f = C.sbuf("rupf", [16, 8 * 64], F32, bst)
    ps_bd = C.psum("psbd", [128, 128], F32, bst)
    bd_dep = []
    prev_cp = None
    c1 = c2 = None
    for h in range(4):
        for t in range(8):
            dd1 = C.dma("sync", rm_f[:], win["rm_pad"][h][:, t * 1024:(t + 1) * 1024],
                        key="w0", waits=[c1] if c1 else [])
            dd2 = C.dma("sync", rup_f[:], win["rup"][h][:, t * 512:(t + 1) * 512],
                        key="w1", waits=[c2] if c2 else [])
            c1 = C.op("vector", lambda e: e.tensor_copy(rm_sb[:], rm_f[:]),
                      waits=[dd1] + ([prev_cp] if prev_cp else []), inc=True)
            c2 = C.op("vector", lambda e: e.tensor_copy(rup_sb[:], rup_f[:]), waits=[dd2], inc=True)
            for cp in range(4):
                for wi in range(2):
                    u = 2 * cp + wi
                    C.op("tensor", lambda e, u=u, wi=wi: e.matmul(
                        ps_bd[:, wi * 64:(wi + 1) * 64],
                        rm_sb[:, u * 128:(u + 1) * 128],
                        rup_sb[:, u * 64:(u + 1) * 64],
                        start=True, stop=True),
                        waits=[c1, c2] + ([prev_cp] if prev_cp else []), inc=(wi == 1))
                mmd = C.mark("tensor")
                prev_cp = C.op("vector", lambda e, h=h, t=t, cp=cp: e.tensor_copy(
                    BD[h][:, (t * 4 + cp) * 128:(t * 4 + cp + 1) * 128], ps_bd[:]),
                    waits=[mmd], inc=True)
        bd_dep.append(prev_cp)
    bst.close()
    C.barrier()

    fst = ExitStack()
    zbuf = C.sbuf("zbuf", [128, 32 * 512], BF16, fst)
    wbf = [C.sbuf("cpwb", [128, 8 * 128], BF16, fst) for _ in range(2)]
    ps_z = [C.psum("psz", [128, 512], F32, fst) for _ in range(2)]
    ps_o = [C.psum("pso", [128, 512], F32, fst) for _ in range(4)]
    out_free = [None] * 4
    wcast = [None, None]
    wlast = [None, None]
    z_free = [None, None]
    bidx = 0
    last = None
    for h in range(4):
        for tc in range(NTG):
            t0, t1 = tc * 512, (tc + 1) * 512
            for zt in range(32):
                t = zt // 4
                zp = ps_z[zt % 2]
                C.op("tensor", lambda e, zt=zt, t=t, zp=zp, h=h, t0=t0, t1=t1: e.matmul(
                    zp[:, 0:512], BD[h][:, zt * 128:(zt + 1) * 128],
                    xn_own_tiles(t, t0, t1), start=True, stop=True),
                    waits=[bd_dep[h]] + list(deps) +
                          ([z_free[zt % 2]] if z_free[zt % 2] else []), inc=True)
                zd = C.mark("tensor")
                z_free[zt % 2] = C.op("scalar", lambda e, zt=zt, zp=zp: e.activation(
                    out=zbuf[:, zt * 512:(zt + 1) * 512], in_=zp[:, 0:512], func=AF.Gelu),
                    waits=[zd], inc=True)
            zdone = z_free[31 % 2]
            for ot in range(8):
                po = ps_o[ot % 4]
                for ztg in range(4):
                    b = bidx % 2
                    bidx += 1
                    wcast[b] = C.dma("gpsimd", wbf[b][:], win["cproj"][(h * 4 + ztg) * 8 + ot],
                                     waits=[wlast[b]] if wlast[b] else [], key="sw%d" % b)
                    mmw = [wcast[b], zdone]
                    if ztg == 0 and out_free[ot % 4] is not None:
                        mmw.append(out_free[ot % 4])
                    for z8 in range(8):
                        zt = ztg * 8 + z8
                        C.op("tensor", lambda e, z8=z8, zt=zt, po=po, b=b: e.matmul(
                            po[:, 0:512], wbf[b][:, z8 * 128:(z8 + 1) * 128],
                            zbuf[:, zt * 512:(zt + 1) * 512],
                            start=(zt == 0), stop=(zt == 31)),
                            waits=mmw if z8 == 0 else (), inc=(z8 == 7))
                    wlast[b] = C.mark("tensor")
                out_free[ot % 4] = C.op("vector", lambda e, ot=ot, po=po, t0=t0, t1=t1: e.tensor_tensor(
                    x_tiles(ot, t0, t1), x_tiles(ot, t0, t1), po[:, 0:512], ALU.add),
                    waits=[wlast[(bidx - 1) % 2]], inc=True)
                last = out_free[ot % 4]
    fst.close()
    bdst.close()
    return last



def emit_attn_l1(C, sh, xnf_tiles, att_tiles, win, deps=()):
    wst = ExitStack()
    wbf = [C.sbuf("awb", [128, 8 * 128], BF16, wst) for _ in range(2)]
    for g in range(4):
        gst = ExitStack()
        K_cm = C.sbuf("Kcm", [128, SEQ], BF16, gst)
        V_cm = C.sbuf("Vcm", [128, SEQ], BF16, gst)
        Q_cm = C.sbuf("Qcm", [128, SEQ], BF16, gst)
        vtok = [C.sbuf("vtok", [128, NKT * 65], BF16, gst) for _ in range(2)]
        pst = ExitStack()
        ps_p = [C.psum("pp", [128, 512], F32, pst) for _ in range(2)]
        dst = {0: K_cm, 1: V_cm, 2: Q_cm}
        toks = [(t0, t0 + 512) for t0 in range(0, SEQ, 512)]

        def mk_evac(dest):
            def evac(ot, t0, t1, ps, dep):
                return C.op("scalar", lambda e: e.copy(dest[:, t0:t1], ps[:, 0:t1 - t0]),
                            waits=[dep], inc=True)
            return evac
        for j, dest in dst.items():
            emit_stream_linear(C, win["qkv"], NT, 1, xnf_tiles, toks,
                               mk_evac(dest), (None, wbf), ps_p,
                               rhs_deps=list(deps) + [C.mark("scalar"), C.mark("vector")],
                               ot_list=[g * 3 + j],
                               w_waits=[C.mark("vector"), C.mark("tensor")])
        kvq_dep = [C.mark("scalar")]
        vt_dep = emit_vtok(C, sh,
                           lambda kt, V_cm=V_cm: V_cm[:, kt * 128:(kt + 1) * 128], vtok, 2,
                           deps=kvq_dep)
        pst.close()
        ab = make_attn_bufs(C, gst, [0, 1], 65)
        emit_attn_group(
            C, sh, ab, [0, 1],
            lambda h, K_cm=K_cm, Q_cm=Q_cm: [
                (lambda kt, h=h, K_cm=K_cm: K_cm[64 * h:64 * h + 64, kt * 128:(kt + 1) * 128],
                 lambda s, h=h, Q_cm=Q_cm: Q_cm[64 * h:64 * h + 64, s * 512:(s + 1) * 512],
                 64 * h)],
            lambda h, kt, vtok=vtok: vtok[h][:, kt * 65:kt * 65 + 65],
            65,
            lambda h, s, g=g: att_tiles(g, h, s),
            1.0 / 8.0,
            deps=kvq_dep + vt_dep)
        gst.close()
        C.barrier()
    wst.close()
    return C.mark("vector")



def alloc_lns(C, st):
    sh = C.shared
    sh.xb = C.sbuf("ln_xb", [128, NT * LCH], BF16, st)
    sh.sq = C.sbuf("ln_sq", [128, NT * LCH], BF16, st)
    sh.musd = C.sbuf("ln_musd", [1, 4 * LCH], F32, st)
    sh.mu_bf = C.sbuf("ln_mubf", [1, LCH], BF16, st)
    sh.rs_bf = C.sbuf("ln_rsbf", [1, LCH], BF16, st)
    sh.mu_b = C.sbuf("ln_mub", [128, LCH], BF16, st)
    sh.rs_b = C.sbuf("ln_rsb", [128, LCH], BF16, st)
    sh.ps_stat = C.psum("ps_stat", [1, 2 * LCH], F32, st)
    sh.ps_b = C.psum("ps_b", [128, LCH], F32, st)


class Consts:
    def __init__(self, C):
        self.ones_col = C.sbuf("ones_col", [128, 1], BF16)
        self.onesb_row = C.sbuf("onesb_row", [1, 128], BF16)
        self.ident = C.sbuf("ident", [128, 128], BF16)
        self.trimask = C.sbuf("trimask", [128, 1152], BF16)
        self.sel8 = C.sbuf("sel8", [8, 8 * 128], BF16)

    def load(self, C):
        ident = C.din("c_ident", [128, 128], BF16)
        trimask = C.din("c_trimask", [128, 1152], BF16)
        sel8 = C.din("c_sel8", [8, 8 * 128], BF16)
        d1 = C.dma("sync", self.ident[:], ident, key="const")
        d2 = C.dma("sync", self.trimask[:], trimask, key="const")
        d2b = C.dma("sync", self.sel8[:], sel8, key="const")
        d3 = C.op("vector", lambda e: e.memset(self.ones_col[:], 1.0), inc=True)
        d4 = C.op("vector", lambda e: e.memset(self.onesb_row[:], 1.0), inc=True)
        self.dep = [d1, d2, d3, d4]


def emit_ln_chunks(C, src_fn, dst_fn, nchunks, deps, affine=None, eps=1e-5):
    sh = C.shared
    last = None
    for c in range(nchunks):
        o = c * LCH
        last = emit_layernorm(C, sh,
                              lambda ct, o=o: src_fn(ct, o, LCH),
                              lambda ct, o=o: dst_fn(ct, o, LCH),
                              LCH,
                              deps=list(deps) + [C.mark("tensor"), C.mark("vector"), C.mark("scalar")],
                              affine=affine, eps=eps)
    return last


def build_A(li):
    if li == 3:
        return build_A3()
    nc = bass.Bass("TRN2", target_bir_lowering=False)
    es = ExitStack()
    C = Ctx(nc, es)
    x_own_in = C.din("x_own", [D, NTOK], F32)
    x_row_in = C.din("x_row", [D, SEQ], F32)
    x_mid = C.dout("x_mid", [D, NTOK], F32)
    att_out = C.dout("att", [4, 128, SEQ], BF16)

    cn = Consts(C)
    cn.load(C)
    C.barrier()
    C.shared = cn
    sh = cn

    xst = ExitStack()
    X = C.sbuf("X", [128, NT * NTOK], F32, xst)

    wins = host_wins_declare(C, li)

    xld = []
    for ct in range(NT):
        xld.append(C.dma("gpsimd", X[:, ct * NTOK:(ct + 1) * NTOK],
                         x_own_in[ct * 128:(ct + 1) * 128, :], key="x"))

    def x_own(ct, t0, t1):
        return X[:, ct * NTOK + t0: ct * NTOK + t1]

    ph1 = ExitStack()
    XN_OWN = C.sbuf("XN_OWN", [128, NT * NTOK], BF16, ph1)

    def xn_own(ct, t0, t1):
        return XN_OWN[:, ct * NTOK + t0: ct * NTOK + t1]

    lns1 = ExitStack()
    alloc_lns(C, lns1)
    ln_dep = emit_ln_chunks(C, lambda ct, o, n: X[:, ct * NTOK + o:ct * NTOK + o + n],
                            lambda ct, o, n: XN_OWN[:, ct * NTOK + o:ct * NTOK + o + n],
                            NTOK // LCH, deps=[cn.dep[-1]] + xld)
    lns1.close()
    C.barrier()
    ffn_dep = emit_ffn(C, li, xn_own, x_own, wins, deps=[ln_dep])
    ph1.close()
    vend0 = C.mark("vector")
    for ct in range(NT):
        C.dma("gpsimd", x_mid[ct * 128:(ct + 1) * 128, :], X[:, ct * NTOK:(ct + 1) * NTOK],
              waits=[vend0, ffn_dep], key="out")
    xst.close()
    C.barrier()

    ph2big = ExitStack()
    ATT = C.sbuf("ATT", [128, 4 * SEQ], BF16, ph2big)
    XNF = C.sbuf("XNF", [128, NT * SEQ], BF16, ph2big)
    ph2 = ExitStack()
    alloc_lns(C, ph2)
    XPART = C.sbuf("XPART", [128, NT * LCH], F32, ph2)
    for c in range(SEQ // LCH):
        o = c * LCH
        pld = [C.dma("gpsimd", XPART[:, ct * LCH:(ct + 1) * LCH],
                     x_row_in[ct * 128:(ct + 1) * 128, o:o + LCH],
                     waits=[C.mark("vector"), C.mark("scalar"), ffn_dep], key="xp")
               for ct in range(NT)]
        emit_layernorm(C, sh,
                       lambda ct: XPART[:, ct * LCH:(ct + 1) * LCH],
                       lambda ct, o=o: XNF[:, ct * SEQ + o: ct * SEQ + o + LCH],
                       LCH, deps=[cn.dep[-1], C.mark("tensor")] + pld)
    if li == 1:
        gcol = C.sbuf("gcol", [128, NT], F32, ph2)
        bcol = C.sbuf("bcol", [128, NT], F32, ph2)
        dg = C.dma("sync", gcol[:], wins["gamma"], key="c1")
        db = C.dma("sync", bcol[:], wins["beta"], key="c2")
        emit_ln_chunks(C, lambda ct, o, n: XNF[:, ct * SEQ + o:ct * SEQ + o + n],
                       lambda ct, o, n: XNF[:, ct * SEQ + o:ct * SEQ + o + n],
                       SEQ // LCH, deps=[C.mark("vector"), dg, db],
                       affine=(lambda ct: gcol[:, ct:ct + 1], lambda ct: bcol[:, ct:ct + 1]))
    xnf_dep = [C.mark("vector")]
    ph2.close()
    C.barrier()

    def xnf(ct, t0, t1):
        return XNF[:, ct * SEQ + t0: ct * SEQ + t1]


    emit_attn(C, li, xnf, lambda g, h, s: ATT[64 * h:64 * h + 64, g * SEQ + s * 512: g * SEQ + (s + 1) * 512],
              wins, deps=xnf_dep)

    vend = C.mark("vector")
    send = C.mark("scalar")
    for g4 in range(4):
        C.dma("gpsimd", att_out[g4], ATT[:, g4 * SEQ:(g4 + 1) * SEQ],
              waits=[vend, send], key="out")
    fin = (C.dsem["out"], C.dsem["out"].n)
    C.op("gpsimd", lambda e: e.nop(), waits=[fin])
    ph2big.close()
    C.replay()
    es.close()
    return nc, C


def host_wins_declare(C, li):
    wins = {}
    if li == 0:
        wins.update(_wins_l0(C))
    if li == 2:
        wins.update(_wins_l2(C))
    if li == 3:
        wins.update(_wins_l3(C))
    if li == 1:
        wins["qkv"] = C.din("w_qkv_strips", [12, 128, NT * 128], F32)
        wins["gamma"] = C.din("gamma_cols", [128, NT], F32)
        wins["beta"] = C.din("beta_cols", [128, NT], F32)
        wins["rm_pad"] = C.din("rm_pad", [4, 16, 64 * 128], F32)
        wins["rup"] = C.din("rup", [4, 16, 64 * 64], F32)
        wins["cproj"] = C.din("cproj_strips", [128, 128, NT * 128], F32)
    return wins


def emit_ffn(C, li, xn_own, x_own, wins, deps):
    if li == 0:
        return emit_ffn0(C, C.shared, xn_own, x_own, wins, deps=deps)
    if li == 1:
        return emit_ffn1(C, C.shared, xn_own, x_own, wins, deps=deps)
    if li == 2:
        return emit_ffn2(C, C.shared, xn_own, x_own, wins, deps=deps)
    if li == 3:
        return emit_ffn3(C, C.shared, xn_own, x_own, wins, deps=deps)
    raise NotImplementedError


def emit_attn(C, li, xnf, att_tiles, wins, deps):
    if li == 0:
        return emit_attn_l0(C, C.shared, xnf, att_tiles, wins, deps=deps)
    if li == 1:
        return emit_attn_l1(C, C.shared, xnf, att_tiles, wins, deps=deps)
    if li == 2:
        return emit_attn_l2(C, C.shared, xnf, att_tiles, wins, deps=deps)
    if li == 3:
        return emit_attn_l3(C, C.shared, xnf, att_tiles, wins, deps=deps)
    raise NotImplementedError



def build_B(li):
    nc = bass.Bass("TRN2", target_bir_lowering=False)
    es = ExitStack()
    C = Ctx(nc, es)
    x_mid_in = C.din("x_mid", [D, NTOK], F32)
    att_in = C.din("att_my", [D, NTOK], BF16)
    x_next = C.dout("x_next", [D, NTOK], F32)

    X = C.sbuf("X", [128, NT * NTOK], F32)
    A = C.sbuf("A", [128, NT * NTOK], BF16)
    xld, ald = [], []
    for ct in range(NT):
        xld.append(C.dma("gpsimd", X[:, ct * NTOK:(ct + 1) * NTOK],
                         x_mid_in[ct * 128:(ct + 1) * 128, :], key="x"))
        ald.append(C.dma("gpsimd", A[:, ct * NTOK:(ct + 1) * NTOK],
                         att_in[ct * 128:(ct + 1) * 128, :], key="a"))

    if li == 1:
        for ct in range(NT):
            C.op("vector", lambda e, ct=ct: e.tensor_tensor(
                X[:, ct * NTOK:(ct + 1) * NTOK], X[:, ct * NTOK:(ct + 1) * NTOK],
                A[:, ct * NTOK:(ct + 1) * NTOK], ALU.add),
                waits=[xld[-1], ald[-1]], inc=True)
    else:
        wo = C.din("wo_strips", [NT, 128, NT * 128], F32)
        bo = C.din("bo_cols", [128, NT], F32)
        bocol = C.sbuf("bocol", [128, NT], F32)
        dbo = C.dma("sync", bocol[:], bo, key="c0")
        wstage = [C.sbuf("bw", [128, NT * 128], F32) for _ in range(2)]
        wbf = [C.sbuf("bwb", [128, NT * 128], BF16) for _ in range(2)]
        psums = [C.psum("bp", [128, 512], F32) for _ in range(4)]
        sc = C.sbuf("bsc", [128, 512], F32)

        def evac(ot, t0, t1, ps, dep):
            d1 = C.op("vector", lambda e: e.tensor_scalar(ps[:, 0:t1 - t0], ps[:, 0:t1 - t0],
                                                          bocol[:, ot:ot + 1], None, ALU.add),
                      waits=[dep, dbo], inc=True)
            return C.op("vector", lambda e: e.tensor_tensor(
                X[:, ot * NTOK + t0: ot * NTOK + t1], X[:, ot * NTOK + t0: ot * NTOK + t1],
                ps[:, 0:t1 - t0], ALU.add),
                waits=[d1, xld[-1]], inc=True)
        emit_stream_linear(C, wo, NT, NT,
                           lambda ct, t0, t1: A[:, ct * NTOK + t0: ct * NTOK + t1],
                           [(i * 512, (i + 1) * 512) for i in range(NTG)],
                           evac, (wstage, wbf), psums, rhs_deps=[ald[-1]])
    vend = C.mark("vector")
    for ct in range(NT):
        C.dma("gpsimd", x_next[ct * 128:(ct + 1) * 128, :], X[:, ct * NTOK:(ct + 1) * NTOK],
              waits=[vend], key="out")
    fin = (C.dsem["out"], C.dsem["out"].n)
    C.op("gpsimd", lambda e: e.nop(), waits=[fin])
    C.replay()
    es.close()
    return nc, C


def set_size(seq):
    global SEQ, NTOK, NTG, NSG, NKT, LCH
    SEQ = seq
    NTOK = seq // 2
    NTG = max(1, NTOK // TG)
    NSG = SEQ // TG
    NKT = SEQ // 128
    LCH = min(512, NTOK)



def prep_l1(ins, hp):
    w_qkv = ins["L1_attn_w_qkv"]
    gam, bet = ins["L1_attn_gamma"], ins["L1_attn_beta"]
    rm, rup = ins["L1_ffn_rail_maker"], ins["L1_ffn_rail_up_proj"]
    wc = ins["L1_ffn_w_cproj"]
    strips = []
    for g in range(4):
        gh = 8 * hp + 2 * g
        for base in (D, 2 * D, 0):
            blk = w_qkv[base + 64 * gh: base + 64 * gh + 128, :]
            strips.append(host_tile_w(np.ascontiguousarray(blk.T))[0])
    out = {
        "w_qkv_strips": np.ascontiguousarray(np.stack(strips)),
        "gamma_cols": np.ascontiguousarray(np.asarray(gam).reshape(NT, 128).T),
        "beta_cols": np.ascontiguousarray(np.asarray(bet).reshape(NT, 128).T),
    }
    rm_pad = np.zeros((4, 16, 64, 128), np.float32)
    rup_t = np.zeros((4, 16, 64, 64), np.float32)
    for h in range(4):
        for r in range(64):
            rm_pad[h, :, r, 16 * (r % 8):16 * (r % 8) + 16] = np.asarray(rm[h, r]).T
            rup_t[h, :, r, :] = np.asarray(rup[h, r]).T
    out["rm_pad"] = rm_pad.reshape(4, 16, 64 * 128)
    out["rup"] = rup_t.reshape(4, 16, 64 * 64)
    wcT = np.ascontiguousarray(np.asarray(wc).T)
    t = wcT.reshape(16, 8, 128, 8, 128).transpose(0, 3, 2, 1, 4).reshape(128, 128, 8 * 128)
    out["cproj_strips"] = np.ascontiguousarray(t)
    return out



def emit_gating(C, sh, st, logits_cm, w_all, mode, deps=()):
    gst = ExitStack()
    lb = w_all
    ps_t = C.psum("g_pt", [128, 128], BF16, gst)
    ps_b = C.psum("g_pb", [8, 128], BF16, gst)
    L = C.sbuf("g_L", [128, 8], F32, gst)
    m1 = C.sbuf("g_m1", [128, 1], F32, gst)
    m1n = C.sbuf("g_m1n", [128, 1], F32, gst)
    m2 = C.sbuf("g_m2", [128, 1], F32, gst)
    sel = C.sbuf("g_sel", [128, 8], F32, gst)
    num = C.sbuf("g_num", [128, 8], F32, gst)
    den = C.sbuf("g_den", [128, 1], F32, gst)
    wt = C.sbuf("g_wt", [128, 8], BF16, gst)
    dl = C.op("vector", lambda e: e.tensor_copy(lb[:], logits_cm[:]), waits=deps, inc=True)
    last = None
    for tt in range(NTOK // 128):
        d1 = C.op("tensor", lambda e, tt=tt: e.transpose(
            ps_t[:, 0:8], lb[0:8, tt * 128:(tt + 1) * 128], sh.ident[0:8, 0:8]),
            waits=[dl] + ([last] if last else []), inc=True)
        d2 = C.op("vector", lambda e: e.tensor_copy(L[:], ps_t[:, 0:8]), waits=[d1], inc=True)
        d3 = C.op("vector", lambda e: e.reduce_max(m1[:], L[:], axis=AXX), waits=[d2], inc=True)
        if mode == "top1sig":
            dg = C.op("scalar", lambda e: e.activation(out=m1n[:], in_=m1[:], func=AF.Sigmoid),
                      waits=[d3], inc=True)
            d4 = C.op("vector", lambda e: e.tensor_scalar(sel[:], L[:], m1[:], None, ALU.is_ge),
                      waits=[d3], inc=True)
            d5 = C.op("vector", lambda e: e.tensor_scalar(wt[:], sel[:], m1n[:], None, ALU.mult),
                      waits=[d4, dg], inc=True)
        else:
            d3n = C.op("vector", lambda e: e.reduce_max(m1n[:], L[:], axis=AXX, negate=True),
                       waits=[d2], inc=True)
            d4 = C.op("vector", lambda e: e.tensor_scalar(sel[:], L[:], m1[:], None, ALU.is_ge),
                      waits=[d3], inc=True)
            dm = C.op("vector", lambda e: e.scalar_tensor_tensor(
                num[:], sel[:], -1e9, L[:], ALU.mult, ALU.add), waits=[d4], inc=True)
            dm2 = C.op("vector", lambda e: e.reduce_max(m2[:], num[:], axis=AXX),
                       waits=[dm], inc=True)
            de = C.op("scalar", lambda e: e.activation(out=num[:], in_=L[:], func=AF.Exp,
                                                       bias=m1n[:]),
                      waits=[d3n, dm2], inc=True)
            ds = C.op("vector", lambda e: e.tensor_scalar(sel[:], L[:], m2[:], None, ALU.is_ge),
                      waits=[dm2, de], inc=True)
            dn = C.op("vector", lambda e: e.tensor_tensor(num[:], num[:], sel[:], ALU.mult),
                      waits=[ds, de], inc=True)
            dd = C.op("vector", lambda e: e.reduce_sum(den[:], num[:], axis=AXX),
                      waits=[dn], inc=True)
            dr = C.op("vector", lambda e: e.reciprocal(den[:], den[:]), waits=[dd], inc=True)
            d5 = C.op("vector", lambda e: e.tensor_scalar(wt[:], num[:], den[:], None, ALU.mult),
                      waits=[dr], inc=True)
        d6 = C.op("tensor", lambda e, tt=tt: e.transpose(ps_b[0:8, :], wt[:, 0:8], sh.ident[:]),
                  waits=[d5], inc=True)
        last = C.op("vector", lambda e, tt=tt: e.tensor_copy(
            w_all[0:8, tt * 128:(tt + 1) * 128], ps_b[0:8, :]), waits=[d6], inc=True)
    gst.close()
    return last



def emit_ffn2(C, sh, xn_own, x_own, wins, deps):
    fst = ExitStack()
    wb0 = [C.mark("tensor"), C.mark("vector")]
    logits = C.sbuf("f2_logits", [8, NTOK], F32, fst)
    w_all = C.sbuf("f2_wall", [8, NTOK], BF16, fst)
    ps_pool = [C.psum("f2_ps", [128, 512], F32, fst) for _ in range(4)]
    wbf4 = [C.sbuf("f2_wb", [128, 11 * 128], BF16, fst) for _ in range(4)]
    NPT = min(NTOK, 1024)
    NHP = NPT // 512
    NPAIR = NTOK // NPT
    rin = C.sbuf("f2_rin", [128, NT * NPT], BF16, fst)
    mid = C.sbuf("f2_mid", [128, 22 * NPT], BF16, fst)
    scr2a = [C.sbuf("f2_scr", [128, 512], BF16, fst) for _ in range(2)]
    gb = C.psum("f2_gb", [128, 512], F32, fst)

    dcb = C.dma("gpsimd", mid[:, 0:NT * 8], wins["router"], key="sw4", waits=wb0)
    for tg in range(NTG):
        for ct in range(NT):
            C.op("tensor", lambda e, tg=tg, ct=ct: e.matmul(
                ps_pool[tg % 2][0:8, 0:512], mid[:, ct * 8:(ct + 1) * 8],
                xn_own(ct, tg * 512, (tg + 1) * 512),
                start=(ct == 0), stop=(ct == NT - 1)),
                waits=[dcb, C.mark("scalar")] + list(deps) if ct == 0 else (), inc=(ct == NT - 1))
        mmd = C.mark("tensor")
        C.op("scalar", lambda e, tg=tg: e.copy(logits[0:8, tg * 512:(tg + 1) * 512],
                                               ps_pool[tg % 2][0:8, 0:512]),
             waits=[mmd], inc=True)
    lg_dep = C.mark("scalar")
    g_dep = emit_gating(C, sh, fst, logits, w_all, "top1sig", deps=[lg_dep])

    last = None
    bank_free = [lg_dep] * 4
    wb_free = [None] * 4
    for e_i in range(-1, 8):
        w1st = wins["w1"][e_i] if e_i >= 0 else wins["ws1"]
        w3st = wins["w3"][e_i] if e_i >= 0 else wins["ws3"]
        w2st = wins["w2"][e_i] if e_i >= 0 else wins["ws2"]
        for tp in range(NPAIR):
            p0 = tp * NPT
            if e_i >= 0:
                rdeps = []
                for th in range(NHP):
                    t0 = p0 + th * 512
                    dbc = C.op("tensor", lambda e, e_i=e_i, t0=t0: e.matmul(
                        gb[:, 0:512], sh.sel8[:, e_i * 128:(e_i + 1) * 128],
                        w_all[0:8, t0:t0 + 512], start=True, stop=True),
                        waits=[g_dep, C.mark("vector")], inc=True)
                    mtw = C.mark("tensor")
                    for ct in range(NT):
                        rdeps.append(C.op("vector", lambda e, ct=ct, th=th, t0=t0: e.tensor_tensor(
                            rin[:, ct * NPT + th * 512:ct * NPT + th * 512 + 512],
                            xn_own(ct, t0, t0 + 512), gb[:, 0:512],
                            ALU.mult), waits=[dbc, mtw], inc=True))

                def rhs(ct, th):
                    return rin[:, ct * NPT + th * 512:ct * NPT + (th + 1) * 512]
                rhs_deps = [rdeps[-1]]
            else:
                def rhs(ct, th, p0=p0):
                    t0 = p0 + th * 512
                    return xn_own(ct, t0, t0 + 512)
                rhs_deps = list(deps)
            mid_dep_th = [None] * NHP
            for ot in range(22):
                wcst = {}
                for j, wst_ in ((0, w1st), (1, w3st)):
                    b = 2 * j + ot % 2
                    wcst[j] = C.dma("gpsimd", wbf4[b][:, 0:NT * 128], wst_[ot],
                                    key="sw%d" % b,
                                    waits=[wb_free[b]] if wb_free[b] else wb0)
                for th in range(NHP):
                    gi = NHP * ot + th
                    for j in (0, 1):
                        b = 2 * j + ot % 2
                        bank = 2 * j + gi % 2
                        ps = ps_pool[bank]
                        mmw = [wcst[j]] + rhs_deps
                        if bank_free[bank] is not None:
                            mmw.append(bank_free[bank])
                        for ct in range(NT):
                            rap = rhs(ct, th)
                            C.op("tensor", lambda e, rap=rap, ct=ct, b=b, ps=ps: e.matmul(
                                ps[:, 0:512], wbf4[b][:, ct * 128:(ct + 1) * 128],
                                rap, start=(ct == 0), stop=(ct == NT - 1)),
                                waits=mmw if ct == 0 else (), inc=(ct == NT - 1))
                    mmd = C.mark("tensor")
                    pa, pb = ps_pool[gi % 2], ps_pool[2 + gi % 2]
                    dsl = C.op("scalar", lambda e, pa=pa, gi=gi: e.activation(
                        out=scr2a[gi % 2][:], in_=pa[:, 0:512], func=AF.Silu),
                        waits=[mmd], inc=True)
                    dmid = C.op("vector", lambda e, ot=ot, th=th, gi=gi, pb=pb: e.tensor_tensor(
                        mid[:, ot * NPT + th * 512:ot * NPT + th * 512 + 512],
                        scr2a[gi % 2][:], pb[:, 0:512], ALU.mult),
                        waits=[dsl], inc=True)
                    bank_free[gi % 2] = dmid
                    bank_free[2 + gi % 2] = dmid
                    mid_dep_th[th] = dmid
                wm = C.mark("tensor")
                wb_free[ot % 2] = wm
                wb_free[2 + ot % 2] = wm
            for ot2 in range(8):
                wcst2 = {}
                for hc in range(2):
                    b = 2 * hc + ot2 % 2
                    wcst2[hc] = C.dma("gpsimd", wbf4[b][:, 0:11 * 128],
                                      w2st[ot2][:, hc * 1408:(hc + 1) * 1408],
                                      key="sw%d" % b,
                                      waits=[wb_free[b]] if wb_free[b] else wb0)
                for th in range(NHP):
                    bank = 2 * (ot2 % 2) + th
                    ps = ps_pool[bank]
                    mmw0 = [wcst2[0], mid_dep_th[th]]
                    if bank_free[bank] is not None:
                        mmw0.append(bank_free[bank])
                    for ct2 in range(22):
                        b = (ot2 % 2) if ct2 < 11 else 2 + ot2 % 2
                        j2 = ct2 % 11
                        w = mmw0 if ct2 == 0 else ([wcst2[1]] if ct2 == 11 else ())
                        C.op("tensor", lambda e, ct2=ct2, j2=j2, b=b, ps=ps, th=th: e.matmul(
                            ps[:, 0:512], wbf4[b][:, j2 * 128:(j2 + 1) * 128],
                            mid[:, ct2 * NPT + th * 512:ct2 * NPT + th * 512 + 512],
                            start=(ct2 == 0), stop=(ct2 == 21)),
                            waits=w, inc=(ct2 == 21))
                    mmd = C.mark("tensor")
                    t0 = p0 + th * 512
                    bank_free[bank] = C.op("vector", lambda e, ot2=ot2, ps=ps, t0=t0: e.tensor_tensor(
                        x_own(ot2, t0, t0 + 512), x_own(ot2, t0, t0 + 512),
                        ps[:, 0:512], ALU.add),
                        waits=[mmd], inc=True)
                    last = bank_free[bank]
                wm = C.mark("tensor")
                wb_free[ot2 % 2] = wm
                wb_free[2 + ot2 % 2] = wm
    fst.close()
    return last


def emit_ffn0(C, sh, xn_own, x_own, wins, deps):
    fst = ExitStack()
    wb0 = [C.mark("tensor"), C.mark("vector")]
    H = C.sbuf("f0_h", [128, NT * NTOK], BF16, fst)
    logits = C.sbuf("f0_logits", [8, NTOK], F32, fst)
    w_all = C.sbuf("f0_wall", [8, NTOK], BF16, fst)
    cols = C.sbuf("f0_cols", [128, 1 + NT + 8 * 32], F32, fst)
    ps_pool = [C.psum("f0_ps", [128, 512], F32, fst) for _ in range(4)]
    gb = C.psum("f0_gb", [128, 512], F32, fst)
    gb2 = C.psum("f0_gb2", [1, 512], F32, fst)
    wbf4 = [C.sbuf("f0_wb", [128, 8 * 128], BF16, fst) for _ in range(4)]
    row = C.sbuf("f0_row", [1, 1024], F32, fst)
    act4 = C.sbuf("f0_act", [128, NTG * 8 * 512], BF16, fst)
    gsb = C.sbuf("f0_gsb", [128, NTG * 512], BF16, fst)
    scr = C.sbuf("f0_scr", [128, 512], BF16, fst)
    scr2 = C.sbuf("f0_scr2", [128, 512], BF16, fst)

    d0cols = C.dma("sync", cols[:], wins["f0_cols"], key="c0", waits=wb0)
    dcols = d0cols
    for e_i in range(8):
        o = 1 + NT + e_i * 32 + 16
        dcols = C.op("vector", lambda e, o=o: e.tensor_scalar(
            cols[:, o:o + 16], cols[:, o:o + 16], 1.0, None, ALU.add),
            waits=[dcols], inc=True)
    hdeps = []
    for g in range(NTOK // TG):
        for ct in range(NT):
            src = scr if ct % 2 == 0 else scr2
            sqd = C.op("scalar", lambda e, src=src, ct=ct, g=g: e.activation(
                out=src[:, 0:512], in_=xn_own(ct, g * TG, (g + 1) * TG), func=AF.Square),
                waits=list(deps) + [C.mark("tensor"), C.mark("vector")], inc=True)
            C.op("tensor", lambda e, src=src, st=(ct == 0), sp=(ct == NT - 1): e.matmul(
                gb2[0:1, 0:512], sh.ones_col[:, 0:1], src[:, 0:512],
                start=st, stop=sp),
                waits=[sqd], inc=True)
        mmd = C.mark("tensor")
        d1 = C.op("vector", lambda e: e.tensor_scalar(row[0:1, 0:512], gb2[0:1, 0:512],
                                                      1.0 / D, 1.1920929e-07, ALU.mult, ALU.add),
                  waits=[mmd] + wb0, inc=True)
        d2 = C.op("scalar", lambda e: e.activation(out=row[0:1, 512:1024],
                                                   in_=row[0:1, 0:512],
                                                   func=AF.Sqrt), waits=[d1], inc=True)
        d3 = C.op("vector", lambda e: e.reciprocal(wbf4[1][0:1, 0:512], row[0:1, 512:1024]),
                  waits=[d2], inc=True)
        dbb = C.op("tensor", lambda e, g=g: e.matmul(gb[:, 0:512], sh.onesb_row[0:1, :],
                                                     wbf4[1][0:1, 0:512],
                                                     start=True, stop=True),
                   waits=[d3, C.mark("scalar"), C.mark("vector")], inc=True)
        for ct in range(NT):
            hdeps.append(C.op("vector", lambda e, ct=ct, g=g: e.scalar_tensor_tensor(
                H[:, ct * NTOK + g * TG: ct * NTOK + (g + 1) * TG],
                xn_own(ct, g * TG, (g + 1) * TG), cols[:, 1 + ct:2 + ct],
                gb[:, 0:512], ALU.mult, ALU.mult),
                waits=[dbb, dcols], inc=True))
    h_dep = hdeps[-1]

    dcb = C.dma("gpsimd", wbf4[2][:, 0:NT * 8], wins["gate_w"], key="sw4", waits=wb0)
    for tg in range(NTG):
        for ct in range(NT):
            C.op("tensor", lambda e, tg=tg, ct=ct: e.matmul(
                ps_pool[tg % 2][0:8, 0:512], wbf4[2][:, ct * 8:(ct + 1) * 8],
                H[:, ct * NTOK + tg * 512: ct * NTOK + (tg + 1) * 512],
                start=(ct == 0), stop=(ct == NT - 1)),
                waits=[dcb, h_dep, C.mark("scalar")] if ct == 0 else (), inc=(ct == NT - 1))
        mmd = C.mark("tensor")
        C.op("scalar", lambda e, tg=tg: e.copy(logits[0:8, tg * 512:(tg + 1) * 512],
                                               ps_pool[tg % 2][0:8, 0:512]),
             waits=[mmd], inc=True)
    lg_dep = C.mark("scalar")
    dlb = C.op("vector", lambda e: e.tensor_scalar(logits[0:8, :], logits[0:8, :],
                                                   cols[0:8, 0:1], None, ALU.add),
               waits=[lg_dep, d0cols], inc=True)
    g_dep = emit_gating(C, sh, fst, logits, w_all, "top2soft", deps=[dlb])

    last = None
    ydep = C.mark("vector")
    b2b = C.sbuf("f0_b2b", [8, NT * 128], BF16, fst)
    db2b = C.dma("gpsimd", b2b[:], wins["mlp2_b"], key="sw5",
                 waits=[C.mark("vector"), C.mark("tensor"), C.mark("scalar")])
    bias_bl = [None] * 4
    k_b = 0
    for tc in range(NTG):
        t0 = tc * 512
        for ot in range(NT):
            bank = k_b % 4
            mmw = [db2b, g_dep, ydep]
            if bias_bl[bank] is not None:
                mmw.append(bias_bl[bank])
            C.op("tensor", lambda e, ot=ot, t0=t0, bank=bank: e.matmul(
                ps_pool[bank][:, 0:512], b2b[0:8, ot * 128:(ot + 1) * 128],
                w_all[0:8, t0:t0 + 512], start=True, stop=True),
                waits=mmw, inc=True)
            mmd = C.mark("tensor")
            bias_bl[bank] = C.op("vector", lambda e, ot=ot, t0=t0, bank=bank: e.tensor_tensor(
                x_own(ot, t0, t0 + 512), x_own(ot, t0, t0 + 512),
                ps_pool[bank][:, 0:512], ALU.add),
                waits=[mmd], inc=True)
            last = bias_bl[bank]
            k_b += 1
    bank_last = list(bias_bl)
    wb_free = [None, h_dep, g_dep, None]
    g_prev = h_dep
    prev_dsl = C.mark("scalar")
    prev_dact = C.mark("vector")
    gs_deps = [None] * NTG
    for e_i in range(8):
        act_dep = {}
        for i in range(8):
            wcst = {}
            for j, oti in ((0, i), (1, 8 + i)):
                b = 2 * j + i % 2
                wcst[j] = C.dma("gpsimd", wbf4[b][:], wins["mlp1"][e_i * 16 + oti],
                                key="sw%d" % b,
                                waits=[wb_free[b]] if wb_free[b] else wb0)
            for tc in range(NTG):
                t0 = tc * 512
                for j in (0, 1):
                    b = 2 * j + i % 2
                    bank = 2 * j + tc % 2
                    ps = ps_pool[bank]
                    mmw = [wcst[j], h_dep]
                    if bank_last[bank] is not None:
                        mmw.append(bank_last[bank])
                    for ct in range(NT):
                        C.op("tensor", lambda e, ct=ct, b=b, ps=ps, t0=t0: e.matmul(
                            ps[:, 0:512], wbf4[b][:, ct * 128:(ct + 1) * 128],
                            H[:, ct * NTOK + t0: ct * NTOK + t0 + 512],
                            start=(ct == 0), stop=(ct == NT - 1)),
                            waits=mmw if ct == 0 else (), inc=(ct == NT - 1))
                mmd = C.mark("tensor")
                cga = cols[:, 1 + NT + e_i * 32 + i:1 + NT + e_i * 32 + i + 1]
                cgb = cols[:, 1 + NT + e_i * 32 + 16 + i:1 + NT + e_i * 32 + 16 + i + 1]
                pg, pl = ps_pool[tc % 2], ps_pool[2 + tc % 2]
                dga = C.op("vector", lambda e, cga=cga, pg=pg: e.tensor_scalar(
                    scr[:], pg[:, 0:512], cga, 7.0, ALU.add, ALU.min),
                    waits=[mmd, dcols, prev_dsl, prev_dact], inc=True)
                dsl = C.op("scalar", lambda e: e.activation(out=scr2[:], in_=scr[:],
                                                            func=AF.Silu, scale=1.702),
                           waits=[dga], inc=True)
                dgb_ = C.op("vector", lambda e, cgb=cgb, pl=pl: e.tensor_scalar(
                    scr[:], pl[:, 0:512], cgb, 8.0, ALU.add, ALU.min),
                    waits=[dsl], inc=True)
                dact = C.op("vector", lambda e, i=i, tc=tc: e.scalar_tensor_tensor(
                    act4[:, (tc * 8 + i) * 512:(tc * 8 + i + 1) * 512],
                    scr[:], -6.0, scr2[:], ALU.max, ALU.mult),
                    waits=[dgb_], inc=True)
                bank_last[tc % 2] = dga
                bank_last[2 + tc % 2] = dgb_
                prev_dsl = dsl
                prev_dact = dact
                act_dep[(i, tc)] = dact
            wm = C.mark("tensor")
            wb_free[i % 2] = wm
            wb_free[2 + i % 2] = wm
            if 4 <= i < 4 + NTG:
                tcb = i - 4
                dbcw = C.op("tensor", lambda e, e_i=e_i, tcb=tcb: e.matmul(
                    gb[:, 0:512], sh.sel8[:, e_i * 128:(e_i + 1) * 128],
                    w_all[0:8, tcb * 512:(tcb + 1) * 512], start=True, stop=True),
                    waits=[g_dep, g_prev], inc=True)
                mmdg = C.mark("tensor")
                gs_deps[tcb] = C.op("scalar", lambda e, tcb=tcb: e.activation(
                    out=gsb[:, tcb * 512:(tcb + 1) * 512], in_=gb[:, 0:512],
                    func=AF.Copy, scale=1.0 / 1.702),
                    waits=[mmdg], inc=True)
                g_prev = gs_deps[tcb]
        adep_tc = [None] * NTG
        for tc in range(NTG):
            for i in range(8):
                adep_tc[tc] = C.op("vector", lambda e, tc=tc, i=i: e.tensor_tensor(
                    act4[:, (tc * 8 + i) * 512:(tc * 8 + i + 1) * 512],
                    act4[:, (tc * 8 + i) * 512:(tc * 8 + i + 1) * 512],
                    gsb[:, tc * 512:(tc + 1) * 512], ALU.mult),
                    waits=[gs_deps[tc], act_dep[(i, tc)]], inc=True)
        for ot in range(NT):
            b = ot % 4
            wc = C.dma("gpsimd", wbf4[b][:], wins["mlp2"][e_i * 8 + ot], key="sw%d" % b,
                       waits=[wb_free[b]] if wb_free[b] else wb0)
            for tc in range(NTG):
                t0 = tc * 512
                bank = 2 * (ot % 2) + tc % 2
                ps = ps_pool[bank]
                mmw = [wc, adep_tc[tc]]
                if bank_last[bank] is not None:
                    mmw.append(bank_last[bank])
                for ct in range(8):
                    C.op("tensor", lambda e, ct=ct, b=b, ps=ps, tc=tc: e.matmul(
                        ps[:, 0:512], wbf4[b][:, ct * 128:(ct + 1) * 128],
                        act4[:, (tc * 8 + ct) * 512:(tc * 8 + ct + 1) * 512],
                        start=(ct == 0), stop=(ct == 7)),
                        waits=mmw if ct == 0 else (), inc=(ct == 7))
                mmd = C.mark("tensor")
                bank_last[bank] = C.op("vector", lambda e, ot=ot, ps=ps, t0=t0: e.tensor_tensor(
                    x_own(ot, t0, t0 + 512), x_own(ot, t0, t0 + 512),
                    ps[:, 0:512], ALU.add), waits=[mmd], inc=True)
                last = bank_last[bank]
            wb_free[b] = C.mark("tensor")
    for ct in range(NT):
        last = C.op("vector", lambda e, ct=ct: e.tensor_tensor(
            x_own(ct, 0, NTOK), x_own(ct, 0, NTOK), xn_own(ct, 0, NTOK), ALU.add),
            waits=[last], inc=True)
    fst.close()
    return last



def emit_attn_l2(C, sh, xnf_tiles, att_tiles, wins, deps=()):
    wst = ExitStack()
    wbf = [C.sbuf("awb", [128, 8 * 128], BF16, wst) for _ in range(2)]
    for g in range(4):
        gst = ExitStack()
        K_cm = C.sbuf("Kcm", [128, SEQ], BF16, gst)
        V_cm = C.sbuf("Vcm", [128, SEQ], BF16, gst)
        Q_cm = C.sbuf("Qcm", [128, SEQ], BF16, gst)
        vtok = [C.sbuf("vtok", [128, NKT * 65], BF16, gst) for _ in range(2)]
        rrow = C.sbuf("rrow", [1, 2 * SEQ], F32, gst)
        rcol = C.sbuf("rcol", [128, 2 * NKT], F32, gst)
        rbf = C.sbuf("rbf", [1, SEQ], BF16, gst)
        qsq = C.sbuf("qsq", [128, 512], BF16, gst)
        pst = ExitStack()
        ps_p = [C.psum("pp", [128, 512], F32, pst) for _ in range(2)]
        ps_s = C.psum("pqs", [1, 2048], F32, pst)
        dst = {0: K_cm, 1: V_cm, 2: Q_cm}
        toks = [(t0, t0 + 512) for t0 in range(0, SEQ, 512)]

        def mk_evac(dest):
            def evac(ot, t0, t1, ps, dep):
                return C.op("scalar", lambda e, dest=dest, t0=t0, t1=t1, ps=ps: e.copy(
                    dest[:, t0:t1], ps[:, 0:t1 - t0]), waits=[dep], inc=True)
            return evac
        for j, dest in dst.items():
            emit_stream_linear(C, wins["qkv"], NT, 1, xnf_tiles, toks,
                               mk_evac(dest), (None, wbf), ps_p,
                               rhs_deps=list(deps) + [C.mark("scalar"), C.mark("vector")],
                               ot_list=[g * 3 + j],
                               w_waits=[C.mark("vector"), C.mark("tensor")])
        kvq_dep = [C.mark("scalar")]
        def rms_block(SRC, scale_dst):
            for sgi in range(NSG):
                t0 = sgi * 512
                dq = C.op("scalar", lambda e, SRC=SRC, qsq=qsq, t0=t0: e.activation(
                    out=qsq[:, 0:512], in_=SRC[:, t0:t0 + 512], func=AF.Square),
                    waits=kvq_dep + [C.mark("tensor"), C.mark("vector")], inc=True)
                for h in range(2):
                    C.op("tensor", lambda e, qsq=qsq, ps_s=ps_s, h=h: e.matmul(
                        ps_s[0:1, h * 512:(h + 1) * 512], sh.ones_col[64 * h:64 * h + 64, 0:1],
                        qsq[64 * h:64 * h + 64, 0:512],
                        start=True, stop=True, tile_position=(64 * h, 0)),
                        waits=[dq, C.mark("vector")], inc=True)
                    mmd = C.mark("tensor")
                    C.op("vector", lambda e, rrow=rrow, ps_s=ps_s, h=h, t0=t0: e.tensor_scalar(
                        rrow[0:1, h * SEQ + t0: h * SEQ + t0 + 512], ps_s[0:1, h * 512:(h + 1) * 512],
                        1.0 / 64.0, 1.1920929e-07, ALU.mult, ALU.add), waits=[mmd], inc=True)
        rms_block(K_cm, None)
        for h in range(2):
            dsq = C.op("scalar", lambda e, rrow=rrow, h=h: e.activation(
                out=rrow[0:1, h * SEQ:(h + 1) * SEQ], in_=rrow[0:1, h * SEQ:(h + 1) * SEQ],
                func=AF.Sqrt), waits=[C.mark("vector")], inc=True)
            drc = C.op("vector", lambda e, rbf=rbf, rrow=rrow, h=h: e.reciprocal(
                rbf[0:1, 0:SEQ], rrow[0:1, h * SEQ:(h + 1) * SEQ]), waits=[dsq], inc=True)
            for kt in range(NKT):
                C.op("tensor", lambda e, rbf=rbf, ps_p=ps_p, kt=kt: e.matmul(
                    ps_p[0][:, 0:1], rbf[0:1, kt * 128:(kt + 1) * 128],
                    sh.onesb_row[0:1, 0:1], start=True, stop=True),
                    waits=[drc, C.mark("vector")], inc=True)
                mmd = C.mark("tensor")
                C.op("vector", lambda e, rcol=rcol, ps_p=ps_p, h=h, kt=kt: e.tensor_scalar(
                    rcol[:, h * NKT + kt: h * NKT + kt + 1], ps_p[0][:, 0:1],
                    1.0 / 400.0, None, ALU.mult), waits=[mmd], inc=True)
        for sgi in range(NSG):
            t0 = sgi * 512
            dq = C.op("scalar", lambda e, Q_cm=Q_cm, qsq=qsq, t0=t0: e.activation(
                out=qsq[:, 0:512], in_=Q_cm[:, t0:t0 + 512], func=AF.Square),
                waits=[C.mark("vector"), C.mark("tensor")], inc=True)
            for h in range(2):
                C.op("tensor", lambda e, qsq=qsq, ps_s=ps_s, h=h: e.matmul(
                    ps_s[0:1, 1024 + h * 512:1024 + (h + 1) * 512], sh.ones_col[64 * h:64 * h + 64, 0:1],
                    qsq[64 * h:64 * h + 64, 0:512],
                    start=True, stop=True, tile_position=(64 * h, 0)),
                    waits=[dq, C.mark("vector")], inc=True)
                mmd = C.mark("tensor")
                dt1 = C.op("vector", lambda e, rrow=rrow, ps_s=ps_s, h=h: e.tensor_scalar(
                    rrow[0:1, 0:512], ps_s[0:1, 1024 + h * 512:1024 + (h + 1) * 512],
                    1.0 / 64.0, 1.1920929e-07, ALU.mult, ALU.add), waits=[mmd], inc=True)
                dsq = C.op("scalar", lambda e, rrow=rrow: e.activation(
                    out=rrow[0:1, 0:512], in_=rrow[0:1, 0:512], func=AF.Sqrt),
                    waits=[dt1], inc=True)
                drc = C.op("vector", lambda e, rbf=rbf, rrow=rrow: e.reciprocal(
                    rbf[0:1, 0:512], rrow[0:1, 0:512]), waits=[dsq], inc=True)
                dbb = C.op("tensor", lambda e, rbf=rbf, ps_p=ps_p: e.matmul(
                    ps_p[0][0:64, 0:512], sh.onesb_row[0:1, 0:64], rbf[0:1, 0:512],
                    start=True, stop=True), waits=[drc, C.mark("vector")], inc=True)
                mm2 = C.mark("tensor")
                C.op("vector", lambda e, Q_cm=Q_cm, ps_p=ps_p, h=h, t0=t0: e.tensor_tensor(
                    Q_cm[64 * h:64 * h + 64, t0:t0 + 512],
                    Q_cm[64 * h:64 * h + 64, t0:t0 + 512],
                    ps_p[0][0:64, 0:512], ALU.mult), waits=[mm2], inc=True)
        rms_dep = [C.mark("vector"), C.mark("scalar")]
        vt_dep = emit_vtok(C, sh,
                           lambda kt, V_cm=V_cm: V_cm[:, kt * 128:(kt + 1) * 128], vtok, 2,
                           deps=kvq_dep + [C.mark("vector")])
        pst.close()
        ab = make_attn_bufs(C, gst, [0, 1], 65)
        capt = {h: C.sbuf("capt", [128, 512], F32, gst) for h in (0, 1)}

        def pre_exp(h, s_, kt, src, dep, capt=capt, rcol=rcol):
            dd = C.op("scalar", lambda e, h=h, kt=kt, src=src, capt=capt, rcol=rcol: e.activation(
                out=capt[h][:], in_=src, func=AF.Tanh,
                scale=rcol[:, h * NKT + kt: h * NKT + kt + 1]),
                waits=[dep], inc=True)
            return capt[h][:], dd
        emit_attn_group(
            C, sh, ab, [0, 1],
            lambda h, K_cm=K_cm, Q_cm=Q_cm: [
                (lambda kt, h=h, K_cm=K_cm: K_cm[64 * h:64 * h + 64, kt * 128:(kt + 1) * 128],
                 lambda s_, h=h, Q_cm=Q_cm: Q_cm[64 * h:64 * h + 64, s_ * 512:(s_ + 1) * 512],
                 64 * h)],
            lambda h, kt, vtok=vtok: vtok[h][:, kt * 65:kt * 65 + 65],
            65,
            lambda h, s_, g=g: att_tiles(g, h, s_),
            50.0, pre_exp=pre_exp,
            deps=kvq_dep + vt_dep + rms_dep)
        gst.close()
        C.barrier()
    wst.close()
    return C.mark("vector")



def emit_attn_l0(C, sh, xnf_tiles, att_tiles, wins, deps=()):
    wst = ExitStack()
    wbf = [C.sbuf("awb", [128, 8 * 128], BF16, wst) for _ in range(2)]
    K = [C.sbuf("K0", [128, SEQ], BF16, wst) for _ in range(4)]
    Q = [C.sbuf("Q0", [128, SEQ], BF16, wst) for _ in range(4)]
    bcols = C.sbuf("bcols", [128, 12], F32, wst)
    rst = ExitStack()
    cosT = C.sbuf("cosT", [128, SEQ], BF16, rst)
    sinT = C.sbuf("sinT", [128, SEQ], BF16, rst)
    S1 = C.sbuf("ropeS1", [128, 2048], BF16, rst)
    S2 = C.sbuf("ropeS2", [128, 2048], BF16, rst)
    dc1 = C.dma("sync", cosT[:], wins["cos"], key="c0")
    dc2 = C.dma("sync", sinT[:], wins["sin"], key="c1")
    dbc = C.dma("sync", bcols[:], wins["bcols"], key="c2")
    pst = ExitStack()
    ps_p = [C.psum("pp", [128, 512], F32, pst) for _ in range(2)]
    toks = [(t0, t0 + 512) for t0 in range(0, SEQ, 512)]

    def mk_evac(dest, bi):
        def evac(ot, t0, t1, ps, dep):
            return C.op("vector", lambda e, dest=dest, t0=t0, t1=t1, ps=ps, bi=bi: e.tensor_scalar(
                dest[:, t0:t1], ps[:, 0:t1 - t0], bcols[:, bi:bi + 1], None, ALU.add),
                waits=[dep, dbc], inc=True)
        return evac
    for rp in range(4):
        emit_stream_linear(C, wins["qkv"], NT, 1, xnf_tiles, toks,
                           mk_evac(K[rp], rp), (None, wbf), ps_p,
                           rhs_deps=list(deps) + [C.mark("vector")],
                           ot_list=[rp], w_waits=[C.mark("vector"), C.mark("tensor")])
    for rp in range(4):
        emit_stream_linear(C, wins["qkv"], NT, 1, xnf_tiles, toks,
                           mk_evac(Q[rp], 4 + rp), (None, wbf), ps_p,
                           rhs_deps=list(deps) + [C.mark("vector")],
                           ot_list=[4 + rp], w_waits=[C.mark("vector"), C.mark("tensor")])
    kq_dep = [C.mark("vector")]

    def rope_pair(ebuf, obuf, dep0):
        dd = dep0
        for c0 in range(0, SEQ, 2048):
            c1 = min(SEQ, c0 + 2048)
            w = c1 - c0
            eb, ob = ebuf[:, c0:c1], obuf[:, c0:c1]
            cs, sn = cosT[:, c0:c1], sinT[:, c0:c1]
            d1 = C.op("vector", lambda e, ob=ob, sn=sn, w=w: e.tensor_tensor(
                S1[:, 0:w], ob, sn, ALU.mult), waits=[dd, dc1, dc2], inc=True)
            d2 = C.op("vector", lambda e, ob=ob, cs=cs, w=w: e.tensor_tensor(
                S2[:, 0:w], ob, cs, ALU.mult), waits=[d1], inc=True)
            d3 = C.op("vector", lambda e, eb=eb, ob=ob, sn=sn: e.tensor_tensor(
                ob, eb, sn, ALU.mult), waits=[d2], inc=True)
            d4 = C.op("vector", lambda e, ob=ob, w=w: e.tensor_tensor(
                ob, ob, S2[:, 0:w], ALU.add), waits=[d3], inc=True)
            d5 = C.op("vector", lambda e, eb=eb, cs=cs: e.tensor_tensor(
                eb, eb, cs, ALU.mult), waits=[d4], inc=True)
            dd = C.op("vector", lambda e, eb=eb, w=w: e.tensor_tensor(
                eb, eb, S1[:, 0:w], ALU.subtract), waits=[d5], inc=True)
        return dd
    rd = kq_dep[0]
    for i in range(2):
        rd = rope_pair(K[i], K[2 + i], rd)
        rd = rope_pair(Q[i], Q[2 + i], rd)
    rope_dep = [rd]
    rst.close()
    pst.close()
    C.barrier()

    for g in range(4):
        gst = ExitStack()
        V_cm = C.sbuf("Vcm", [128, SEQ], BF16, gst)
        vtok = [C.sbuf("vtok", [128, NKT * 65], BF16, gst) for _ in range(2)]
        pst2 = ExitStack()
        ps_p2 = [C.psum("pp2", [128, 512], F32, pst2) for _ in range(2)]
        emit_stream_linear(C, wins["qkv"], NT, 1, xnf_tiles, toks,
                           mk_evac(V_cm, 8 + g), (None, wbf), ps_p2,
                           rhs_deps=list(deps) + [C.mark("vector")],
                           ot_list=[8 + g], w_waits=[C.mark("vector"), C.mark("tensor")])
        v_dep = [C.mark("vector")]
        vt_dep = emit_vtok(C, sh,
                           lambda kt, V_cm=V_cm: V_cm[:, kt * 128:(kt + 1) * 128], vtok, 2,
                           deps=v_dep)
        pst2.close()
        ab = make_attn_bufs(C, gst, [0, 1], 65)

        def chunks_l0(h, g=g):
            hh = 2 * g + h
            pi, rb = hh // 4, 32 * (hh % 4)
            return [
                (lambda kt, pi=pi, rb=rb: K[pi][rb:rb + 32, kt * 128:(kt + 1) * 128],
                 lambda s_, pi=pi, rb=rb: Q[pi][rb:rb + 32, s_ * 512:(s_ + 1) * 512], rb),
                (lambda kt, pi=pi, rb=rb: K[2 + pi][rb:rb + 32, kt * 128:(kt + 1) * 128],
                 lambda s_, pi=pi, rb=rb: Q[2 + pi][rb:rb + 32, s_ * 512:(s_ + 1) * 512], rb),
            ]
        emit_attn_group(
            C, sh, ab, [0, 1], chunks_l0,
            lambda h, kt, vtok=vtok: vtok[h][:, kt * 65:kt * 65 + 65],
            65,
            lambda h, s_, g=g: att_tiles(g, h, s_),
            1.0 / 8.0,
            deps=rope_dep + vt_dep)
        gst.close()
        C.barrier()
    wst.close()
    return C.mark("vector")



def _wins_l2(C):
    return {
        "qkv": C.din("w_qkv_strips", [12, 128, NT * 128], F32),
        "router": C.din("router_tiles", [128, NT * 8], F32),
        "w1": C.din("w1_strips", [8, 22, 128, NT * 128], F32),
        "w3": C.din("w3_strips", [8, 22, 128, NT * 128], F32),
        "w2": C.din("w2_strips", [8, 8, 128, 22 * 128], F32),
        "ws1": C.din("ws1_strips", [22, 128, NT * 128], F32),
        "ws3": C.din("ws3_strips", [22, 128, NT * 128], F32),
        "ws2": C.din("ws2_strips", [8, 128, 22 * 128], F32),
    }


def _wins_l0(C):
    return {
        "qkv": C.din("w_qkv_strips", [12, 128, NT * 128], F32),
        "cos": C.din("rope_cos", [128, SEQ], BF16),
        "sin": C.din("rope_sin", [128, SEQ], BF16),
        "bcols": C.din("attn_bcols", [128, 12], F32),
        "f0_cols": C.din("f0_cols", [128, 1 + NT + 8 * 32], F32),
        "gate_w": C.din("gate_w_tiles", [128, NT * 8], F32),
        "gate_b": C.din("gate_b_row", [1, 8], F32),
        "gate_b_col": C.din("gate_b_col", [8, 1], F32),
        "mlp1": C.din("mlp1_strips", [8 * 16, 128, NT * 128], F32),
        "mlp2": C.din("mlp2_strips", [8 * 8, 128, NT * 128], F32),
        "mlp2_b": C.din("mlp2_b_rows", [8, NT * 128], F32),
    }


def tiles_cm(w):
    ci, k = w.shape
    return np.ascontiguousarray(w.reshape(ci // 128, 128, k).transpose(1, 0, 2).reshape(128, -1))


def prep_l2(ins, hp):
    w_q, w_k, w_v = (np.asarray(ins["L2_attn_w_q"]), np.asarray(ins["L2_attn_w_k"]),
                     np.asarray(ins["L2_attn_w_v"]))
    strips = []
    for g in range(4):
        gh = 8 * hp + 2 * g
        for w in (w_k, w_v, w_q):
            blk = w[64 * gh: 64 * gh + 128, :]
            strips.append(host_tile_w(np.ascontiguousarray(blk.T))[0])
    out = {"w_qkv_strips": np.ascontiguousarray(np.stack(strips)),
           "router_tiles": tiles_cm(np.asarray(ins["L2_ffn_router_DE"]))}
    out["w1_strips"] = np.stack([host_tile_w(np.asarray(ins["L2_ffn_w1_eDF"][e])) for e in range(8)])
    out["w3_strips"] = np.stack([host_tile_w(np.asarray(ins["L2_ffn_w3_eDF"][e])) for e in range(8)])
    out["w2_strips"] = np.stack([host_tile_w(np.asarray(ins["L2_ffn_w2_eFD"][e])) for e in range(8)])
    out["ws1_strips"] = host_tile_w(np.asarray(ins["L2_ffn_ws1_DF"]))
    out["ws3_strips"] = host_tile_w(np.asarray(ins["L2_ffn_ws3_DF"]))
    out["ws2_strips"] = host_tile_w(np.asarray(ins["L2_ffn_ws2_FD"]))
    out["wo_strips"] = host_tile_w(np.ascontiguousarray(np.asarray(ins["L2_attn_w_o"]).T))
    out["bo_cols"] = np.zeros((128, NT), np.float32)
    return out


def prep_l0(ins, hp):
    Wq, Wk, Wv = (np.asarray(ins["L0_attn_Wq"]), np.asarray(ins["L0_attn_Wk"]),
                  np.asarray(ins["L0_attn_Wv"]))
    bq, bk, bv = (np.asarray(ins["L0_attn_bq"]), np.asarray(ins["L0_attn_bk"]),
                  np.asarray(ins["L0_attn_bv"]))
    strips = []
    bcols = np.zeros((128, 12), np.float32)
    for j, (w, b) in enumerate(((Wk, bk), (Wq, bq))):
        for par in (0, 1):
            for pi in range(2):
                rows = []
                for hh in range(4 * pi, 4 * pi + 4):
                    gh = 8 * hp + hh
                    rows += [64 * gh + 2 * jj + par for jj in range(32)]
                strips_idx = j * 4 + par * 2 + pi if False else None
                strips.append((j * 4 + par * 2 + pi, host_tile_w(np.ascontiguousarray(w[rows, :].T))[0]))
                bcols[:, j * 4 + par * 2 + pi] = b[rows]
    for g in range(4):
        gh = 8 * hp + 2 * g
        rows = list(range(64 * gh, 64 * gh + 128))
        strips.append((8 + g, host_tile_w(np.ascontiguousarray(Wv[rows, :].T))[0]))
        bcols[:, 8 + g] = bv[rows]
    strips.sort(key=lambda t: t[0])
    out = {"w_qkv_strips": np.ascontiguousarray(np.stack([t[1] for t in strips])),
           "attn_bcols": bcols}
    p = np.arange(32)
    invf = 10000.0 ** (-2.0 * p / 64.0)
    t = np.arange(SEQ)
    ang = t[None, :] * invf[:, None]
    out["rope_cos"] = np.ascontiguousarray(np.tile(np.cos(ang), (4, 1))).astype(BFNP)
    out["rope_sin"] = np.ascontiguousarray(np.tile(np.sin(ang), (4, 1))).astype(BFNP)
    cols = np.zeros((128, 1 + NT + 8 * 32), np.float32)
    cols[:, 1:1 + NT] = np.asarray(ins["L0_ffn_norm_w"]).reshape(NT, 128).T
    cols[0:8, 0] = np.asarray(ins["L0_ffn_gate_b"])
    m1b = np.asarray(ins["L0_ffn_mlp1_b"])
    for e in range(8):
        glu_b, lin_b = m1b[e][0::2], m1b[e][1::2]
        for i in range(8):
            cols[:, 1 + NT + e * 32 + i] = glu_b[i * 128:(i + 1) * 128]
            cols[:, 1 + NT + e * 32 + 16 + i] = lin_b[i * 128:(i + 1) * 128]
    out["f0_cols"] = cols
    out["gate_w_tiles"] = tiles_cm(np.ascontiguousarray(np.asarray(ins["L0_ffn_gate_w"]).T))
    out["gate_b_row"] = np.asarray(ins["L0_ffn_gate_b"]).reshape(1, 8)
    out["gate_b_col"] = np.asarray(ins["L0_ffn_gate_b"]).reshape(8, 1)
    m1w = np.asarray(ins["L0_ffn_mlp1_w"])
    mlp1 = []
    for e in range(8):
        perm = np.concatenate([m1w[e][0::2], m1w[e][1::2]], axis=0)
        mlp1.append(host_tile_w(np.ascontiguousarray(perm.T)))
    out["mlp1_strips"] = np.ascontiguousarray(np.concatenate(mlp1))
    m2w = np.asarray(ins["L0_ffn_mlp2_w"])
    mlp2 = [host_tile_w(np.ascontiguousarray(m2w[e].T)) for e in range(8)]
    out["mlp2_strips"] = np.ascontiguousarray(np.concatenate(mlp2))
    out["mlp2_b_rows"] = np.asarray(ins["L0_ffn_mlp2_b"]).reshape(8, NT * 128)
    out["wo_strips"] = host_tile_w(np.ascontiguousarray(np.asarray(ins["L0_attn_Wo"]).T))
    out["bo_cols"] = np.asarray(ins["L0_attn_bo"]).reshape(NT, 128).T.copy()
    return out



def _wins_l3(C):
    return {
        "qkv": C.din("w_qkv_strips", [12, 128, NT * 128], F32),
        "qkv_b": C.din("qkv_bcols", [128, 12], F32),
        "lat": C.din("lat_bd", [8, 128, 128], F32),
        "v2l": C.din("v2l_bd", [128, 64], F32),
        "l2o": C.din("l2o_bd", [64, 128], F32),
        "lat_b": C.din("lat_bcols", [128, 8], F32),
        "l2o_b": C.din("l2o_bcols", [128, 1], F32),
        "cos": C.din("rope_cos", [128, SEQ], BF16),
        "sin": C.din("rope_sin", [128, SEQ], BF16),
        "Wr": C.din("wr_tiles", [128, NT * 8], F32),
        "A": C.din("a_tiles", [128, 8 * NT * 64], F32),
        "Bw": C.din("bw_strips", [8, 64, 8 * 128], F32),
    }


def emit_ffn3_pre(C, sh, xn_own, x_own, MIDs, wins, deps, dram):
    mid_dram, _ = dram
    fst = ExitStack()
    Asb = C.sbuf("f3_A", [128, NT * 64], BF16, fst)
    ps = [C.psum("f3_ps", [128, 512], F32, fst) for _ in range(2)]
    dc = None
    for e_i in range(8):
        dc = C.dma("gpsimd", Asb[:], wins["A"][:, e_i * NT * 64:(e_i + 1) * NT * 64],
                   key="sw0", waits=[C.mark("tensor")])
        sdone = []
        for tg in range(NTG):
            pp = ps[tg % 2]
            for ct in range(NT):
                C.op("tensor", lambda e, ct=ct, tg=tg, pp=pp: e.matmul(
                    pp[0:64, 0:512], Asb[:, ct * 64:(ct + 1) * 64],
                    xn_own(ct, tg * 512, (tg + 1) * 512),
                    start=(ct == 0), stop=(ct == NT - 1)),
                    waits=[dc, C.mark("scalar"), (C.dsem.get("mb"), C.dsem["mb"].n) if "mb" in C.dsem else None] + list(deps) if ct == 0 else (),
                    inc=(ct == NT - 1))
            mmd = C.mark("tensor")
            sdone.append(C.op("scalar", lambda e, tg=tg, pp=pp: e.copy(
                MIDs[0:64, tg * 512:(tg + 1) * 512], pp[0:64, 0:512]),
                waits=[mmd], inc=True))
        C.dma("gpsimd", mid_dram[0:64, e_i * NTOK:(e_i + 1) * NTOK], MIDs[0:64, :],
              waits=[sdone[-1]], key="mb")
    last2 = None
    for ct in range(NT):
        last2 = C.op("vector", lambda e, ct=ct: e.tensor_tensor(
            x_own(ct, 0, NTOK), x_own(ct, 0, NTOK), xn_own(ct, 0, NTOK), ALU.add),
            waits=list(deps), inc=True)
    fst.close()
    return [last2, (C.dsem["mb"], C.dsem["mb"].n)]


def emit_ffn3_post(C, sh, cls_buf, x_own, mid_dram, wins, deps):
    fst = ExitStack()
    wr = C.sbuf("f3_wr", [128, NT * 8], F32, fst)
    wrb = C.sbuf("f3_wrb", [128, NT * 8], BF16, fst)
    L8 = C.sbuf("f3_L8", [1, 48], F32, fst)
    g8b = C.sbuf("f3_g8b", [1, 8], BF16, fst)
    gcolf = C.sbuf("f3_gcol", [64, 8], F32, fst)
    MID = C.sbuf("f3_mid", [64, 8 * NTOK], BF16, fst)
    Bb = C.sbuf("f3_B", [64, 8 * 128], BF16, fst)
    ps = [C.psum("f3_ps2", [128, 512], F32, fst) for _ in range(2)]
    mld = [C.dma("gpsimd", MID[0:64, :], mid_dram[0:64, :], waits=list(deps), key="mr")]
    dwr = C.dma("sync", wr[:], wins["Wr"], key="c1", waits=list(deps))
    dwrb = C.op("vector", lambda e: e.tensor_copy(wrb[:], wr[:]), waits=[dwr], inc=True)
    for ct in range(NT):
        C.op("tensor", lambda e, ct=ct: e.matmul(
            ps[0][0:1, 0:8], cls_buf[:, ct:ct + 1], wrb[:, ct * 8:(ct + 1) * 8],
            start=(ct == 0), stop=(ct == NT - 1)),
            waits=[dwrb] + list(deps), inc=(ct == NT - 1))
    mmd = C.mark("tensor")
    l = L8[0:1, 0:8]; m1 = L8[0:1, 8:9]; m1n = L8[0:1, 9:10]; m2 = L8[0:1, 10:11]
    sel = L8[0:1, 16:24]; num = L8[0:1, 24:32]; den = L8[0:1, 32:33]
    d0 = C.op("vector", lambda e: e.tensor_copy(l, ps[0][0:1, 0:8]), waits=[mmd], inc=True)
    d1 = C.op("vector", lambda e: e.reduce_max(m1, l, axis=AXX), waits=[d0], inc=True)
    d1n = C.op("vector", lambda e: e.reduce_max(m1n, l, axis=AXX, negate=True), waits=[d1], inc=True)
    d2 = C.op("vector", lambda e: e.tensor_scalar(sel, l, m1, None, ALU.is_ge), waits=[d1], inc=True)
    d3 = C.op("vector", lambda e: e.scalar_tensor_tensor(num, sel, -1e9, l, ALU.mult, ALU.add),
              waits=[d2], inc=True)
    d4 = C.op("vector", lambda e: e.reduce_max(m2, num, axis=AXX), waits=[d3], inc=True)
    d5 = C.op("scalar", lambda e: e.activation(out=num, in_=l, func=AF.Exp, bias=m1n),
              waits=[d4, d1n], inc=True)
    d6 = C.op("vector", lambda e: e.tensor_scalar(sel, l, m2, None, ALU.is_ge), waits=[d4, d5], inc=True)
    d7 = C.op("vector", lambda e: e.tensor_tensor(num, num, sel, ALU.mult), waits=[d6], inc=True)
    d8 = C.op("vector", lambda e: e.reduce_sum(den, num, axis=AXX), waits=[d7], inc=True)
    d9 = C.op("vector", lambda e: e.reciprocal(den, den), waits=[d8], inc=True)
    d10 = C.op("vector", lambda e: e.tensor_scalar(g8b[0:1, 0:8], num, den, None, ALU.mult), waits=[d9], inc=True)
    d11 = C.op("tensor", lambda e: e.matmul(ps[0][0:64, 0:8], sh.onesb_row[0:1, 0:64],
                                            g8b[0:1, 0:8], start=True, stop=True),
               waits=[d10, C.mark("vector")], inc=True)
    d12 = C.op("vector", lambda e: e.tensor_copy(gcolf[:], ps[0][0:64, 0:8]), waits=[d11], inc=True)
    sdeps = []
    for e_i in range(8):
        sdeps.append(C.op("vector", lambda e, e_i=e_i: e.tensor_scalar(
            MID[0:64, e_i * NTOK:(e_i + 1) * NTOK], MID[0:64, e_i * NTOK:(e_i + 1) * NTOK],
            gcolf[:, e_i:e_i + 1], None, ALU.mult), waits=[d12] + mld, inc=True))
    last = None
    dc = None
    for ot in range(NT):
        dc = C.dma("gpsimd", Bb[:], wins["Bw"][ot], key="sw1", waits=[C.mark("tensor")])
        for tg in range(NTG):
            pp = ps[(ot * NTG + tg) % 2]
            for e_i in range(8):
                C.op("tensor", lambda e, e_i=e_i, ot=ot, tg=tg, pp=pp: e.matmul(
                    pp[:, 0:512], Bb[0:64, e_i * 128:(e_i + 1) * 128],
                    MID[0:64, e_i * NTOK + tg * 512: e_i * NTOK + (tg + 1) * 512],
                    start=(e_i == 0), stop=(e_i == 7)),
                    waits=[dc, sdeps[-1], C.mark("vector")] if e_i == 0 else (),
                    inc=(e_i == 7))
            mmd = C.mark("tensor")
            last = C.op("vector", lambda e, ot=ot, tg=tg, pp=pp: e.tensor_tensor(
                x_own(ot, tg * 512, (tg + 1) * 512), x_own(ot, tg * 512, (tg + 1) * 512),
                pp[:, 0:512], ALU.add), waits=[mmd], inc=True)
    fst.close()
    return last


def emit_attn_l3(C, sh, xnf_tiles, att_tiles, wins, deps=()):
    wst = ExitStack()
    wbf = [C.sbuf("awb", [128, 8 * 128], BF16, wst) for _ in range(2)]
    bcols = C.sbuf("bcols", [128, 12], F32, wst)
    latw_f = C.sbuf("latwf", [128, 128], F32, wst)
    LW = {}
    for nm in ("q2l_e", "q2l_o", "kv2l_e", "kv2l_o", "wqk_ee", "wqk_oe", "wqk_eo", "wqk_oo"):
        LW[nm] = C.sbuf(nm, [128, 128], BF16, wst)
    v2l = C.sbuf("v2l", [128, 64], BF16, wst)
    l2o = C.sbuf("l2o", [64, 128], BF16, wst)
    latb = C.sbuf("latb", [128, 8], F32, wst)
    l2ob = C.sbuf("l2ob", [128, 1], F32, wst)
    cosT = C.sbuf("cosT", [128, SEQ], BF16, wst)
    sinT = C.sbuf("sinT", [128, SEQ], BF16, wst)
    dl = C.dma("sync", bcols[:], wins["qkv_b"], key="c2")
    cv = None
    for i, nm in enumerate(("q2l_e", "q2l_o", "kv2l_e", "kv2l_o", "wqk_ee", "wqk_oe", "wqk_eo", "wqk_oo")):
        dd = C.dma("sync", latw_f[:], wins["lat"][i], key="c0", waits=[cv] if cv else [])
        cv = C.op("vector", lambda e, nm=nm: e.tensor_copy(LW[nm][:], latw_f[:]), waits=[dd], inc=True)
    dd = C.dma("sync", latw_f[:, 0:64], wins["v2l"], key="c0", waits=[cv])
    cv = C.op("vector", lambda e: e.tensor_copy(v2l[:], latw_f[:, 0:64]), waits=[dd], inc=True)
    dd = C.dma("sync", latw_f[0:64, 0:128], wins["l2o"], key="c0", waits=[cv])
    cv = C.op("vector", lambda e: e.tensor_copy(l2o[:], latw_f[0:64, 0:128]), waits=[dd], inc=True)
    d_lb = C.dma("sync", latb[:], wins["lat_b"], key="c1")
    d_ob = C.dma("sync", l2ob[:], wins["l2o_b"], key="c3")
    d_c1 = C.dma("sync", cosT[:], wins["cos"], key="c4")
    d_c2 = C.dma("sync", sinT[:], wins["sin"], key="c5")
    wdep = [cv, dl, d_lb, d_ob, d_c1, d_c2]
    toks = [(t0, t0 + 512) for t0 in range(0, SEQ, 512)]

    for g in range(4):
        gst = ExitStack()
        QLe = C.sbuf("QLe", [128, SEQ], BF16, gst)
        QLo = C.sbuf("QLo", [128, SEQ], BF16, gst)
        KLe = C.sbuf("KLe", [128, SEQ], BF16, gst)
        KLo = C.sbuf("KLo", [128, SEQ], BF16, gst)
        VL = C.sbuf("VL", [64, SEQ], BF16, gst)
        YL = C.sbuf("YL", [64, SEQ], BF16, gst)
        vtok = [C.sbuf("vtok", [128, NKT * 33], BF16, gst) for _ in range(2)]
        rS = C.sbuf("rS", [128, 1024], BF16, gst)
        HC = C.sbuf("HC", [128, 512], BF16, gst)
        pst = ExitStack()
        ps_p = [C.psum("pp", [128, 512], F32, pst) for _ in range(2)]
        ps_l = [C.psum("pl", [128, 512], F32, pst) for _ in range(2)]

        plan = ((0, ((LW["kv2l_e"], KLe, 2, 128), (LW["kv2l_o"], KLo, 3, 128))),
                (1, ((v2l, VL, 4, 64),)),
                (2, ((LW["q2l_e"], QLe, 0, 128), (LW["q2l_o"], QLo, 1, 128))))
        for j, lats in plan:
            b = j % 2
            wc = C.dma("gpsimd", wbf[b][:], wins["qkv"][g * 3 + j], key="sw%d" % b,
                       waits=[C.mark("vector"), C.mark("tensor")])
            for t0, t1 in toks:
                pc = ps_p[(t0 // 512) % 2]
                for ct in range(NT):
                    rap = xnf_tiles(ct, t0, t1)
                    C.op("tensor", lambda e, rap=rap, ct=ct, b=b, pc=pc: e.matmul(
                        pc[:, 0:512], wbf[b][:, ct * 128:(ct + 1) * 128], rap,
                        start=(ct == 0), stop=(ct == NT - 1)),
                        waits=[wc, C.mark("vector")] + list(deps) if ct == 0 else (),
                        inc=(ct == NT - 1))
                mmd = C.mark("tensor")
                hcd = C.op("vector", lambda e, HC=HC, pc=pc, j=j, g=g: e.tensor_scalar(
                    HC[:], pc[:, 0:512], bcols[:, g * 3 + j:g * 3 + j + 1], None, ALU.add),
                    waits=[mmd, dl, C.mark("tensor")], inc=True)
                for (W, DST, bi, rows) in lats:
                    pl = ps_l[bi % 2]
                    C.op("tensor", lambda e, W=W, pl=pl, HC=HC, rows=rows: e.matmul(
                        pl[0:rows, 0:512], W[:, 0:rows], HC[:],
                        start=True, stop=True),
                        waits=[hcd, cv, C.mark("vector")], inc=True)
                    mm2 = C.mark("tensor")
                    C.op("vector", lambda e, DST=DST, pl=pl, t0=t0, t1=t1, bi=bi, rows=rows: e.tensor_scalar(
                        DST[0:rows, t0:t1], pl[0:rows, 0:512], latb[0:rows, bi:bi + 1],
                        None, ALU.add),
                        waits=[mm2, d_lb], inc=True)
        kl_dep = [C.mark("vector")]

        def rope_pair(EB, OB, dep0):
            dd = dep0
            for c0 in range(0, SEQ, 512):
                w = 512
                for bb in (0, 64):
                    eb, ob = EB[bb:bb + 16, c0:c0 + w], OB[bb:bb + 16, c0:c0 + w]
                    cs, sn = cosT[bb:bb + 16, c0:c0 + w], sinT[bb:bb + 16, c0:c0 + w]
                    s1 = rS[bb:bb + 16, 0:w]
                    s2 = rS[bb:bb + 16, 512:512 + w]
                    d1 = C.op("vector", lambda e, s1=s1, ob=ob, sn=sn: e.tensor_tensor(s1, ob, sn, ALU.mult),
                              waits=[dd, d_c1, d_c2], inc=True)
                    d2 = C.op("vector", lambda e, s2=s2, ob=ob, cs=cs: e.tensor_tensor(s2, ob, cs, ALU.mult),
                              waits=[d1], inc=True)
                    d3 = C.op("vector", lambda e, ob=ob, eb=eb, sn=sn: e.tensor_tensor(ob, eb, sn, ALU.mult),
                              waits=[d2], inc=True)
                    d4 = C.op("vector", lambda e, ob=ob, s2=s2: e.tensor_tensor(ob, ob, s2, ALU.add),
                              waits=[d3], inc=True)
                    d5 = C.op("vector", lambda e, eb=eb, cs=cs: e.tensor_tensor(eb, eb, cs, ALU.mult),
                              waits=[d4], inc=True)
                    dd = C.op("vector", lambda e, eb=eb, s1=s1: e.tensor_tensor(eb, eb, s1, ALU.subtract),
                              waits=[d5], inc=True)
            return dd
        rd = C.mark("vector")
        rd = rope_pair(QLe, QLo, rd)
        rd = rope_pair(KLe, KLo, rd)
        qfk_dep = rd
        for t0, t1 in toks:
            m1w = [qfk_dep, cv, C.mark("vector")]
            C.op("tensor", lambda e, QLe=QLe, pl=ps_l[0], t0=t0, t1=t1: e.matmul(
                pl[:, 0:512], LW["wqk_ee"][:, 0:128], QLe[:, t0:t1],
                start=True, stop=False), waits=m1w, inc=False)
            C.op("tensor", lambda e, QLo=QLo, pl=ps_l[0], t0=t0, t1=t1: e.matmul(
                pl[:, 0:512], LW["wqk_oe"][:, 0:128], QLo[:, t0:t1],
                start=False, stop=True), waits=(), inc=True)
            C.op("tensor", lambda e, QLe=QLe, pl=ps_l[1], t0=t0, t1=t1: e.matmul(
                pl[:, 0:512], LW["wqk_eo"][:, 0:128], QLe[:, t0:t1],
                start=True, stop=False), waits=(), inc=False)
            C.op("tensor", lambda e, QLo=QLo, pl=ps_l[1], t0=t0, t1=t1: e.matmul(
                pl[:, 0:512], LW["wqk_oo"][:, 0:128], QLo[:, t0:t1],
                start=False, stop=True), waits=(), inc=True)
            mmq = C.mark("tensor")
            C.op("vector", lambda e, QLe=QLe, pl=ps_l[0], t0=t0, t1=t1: e.tensor_scalar(
                QLe[:, t0:t1], pl[:, 0:512], latb[:, 5:6], None, ALU.add),
                waits=[mmq, d_lb], inc=True)
            qfk_dep = C.op("vector", lambda e, QLo=QLo, pl=ps_l[1], t0=t0, t1=t1: e.tensor_scalar(
                QLo[:, t0:t1], pl[:, 0:512], latb[:, 6:7], None, ALU.add),
                waits=[mmq], inc=True)
        lat_dep = [C.mark("vector")]
        vt_dep = emit_vtok33(C, sh, VL, vtok, deps=lat_dep)
        pst.close()
        ab = make_attn_bufs(C, gst, [0, 1], 33)

        def chunks_l3(h):
            b = 64 * h
            return [
                (lambda kt, b=b, KLe=KLe: KLe[b:b + 16, kt * 128:(kt + 1) * 128],
                 lambda s_, b=b, QLe=QLe: QLe[b:b + 16, s_ * 512:(s_ + 1) * 512], b),
                (lambda kt, b=b, KLo=KLo: KLo[b:b + 16, kt * 128:(kt + 1) * 128],
                 lambda s_, b=b, QLo=QLo: QLo[b:b + 16, s_ * 512:(s_ + 1) * 512], b),
            ]
        emit_attn_group(
            C, sh, ab, [0, 1], chunks_l3,
            lambda h, kt, vtok=vtok: vtok[h][:, kt * 33:kt * 33 + 33],
            33,
            lambda h, s_, YL=YL: YL[32 * h:32 * h + 32, s_ * 512:(s_ + 1) * 512],
            float(32 ** -0.5),
            deps=lat_dep + vt_dep)
        yd = C.mark("vector")
        for s_, (t0, t1) in enumerate(toks):
            pv = ab["av"][0]
            C.op("tensor", lambda e, pv=pv, YL=YL, t0=t0, t1=t1: e.matmul(
                pv[:, 0:512], l2o[:, 0:128], YL[0:64, t0:t1],
                start=True, stop=True),
                waits=[yd, C.mark("scalar"), C.mark("vector")], inc=True)
            mmd = C.mark("tensor")
            oap = att_tiles(g, t0, t1)
            C.op("vector", lambda e, oap=oap, pv=pv: e.tensor_scalar(
                oap, pv[:, 0:512], l2ob[:, 0:1], None, ALU.add),
                waits=[mmd, d_ob], inc=True)
        gst.close()
        C.barrier()
    wst.close()
    return C.mark("vector")


def emit_vtok33(C, sh, VL, vtok, deps=()):
    vst = ExitStack()
    ptr = C.psum("ptr", [128, 64], BF16, vst)
    d_ones = []
    for h in range(2):
        d_ones.append(C.op("vector", lambda e, h=h: e.memset(
            vtok[h].rearrange("p (k c) -> p k c", c=33)[:, :, 32:33], 1.0), inc=True))
    last = None
    for kt in range(NKT):
        vap = VL[0:64, kt * 128:(kt + 1) * 128]
        dt = C.op("tensor", lambda e, vap=vap: e.transpose(ptr[:, 0:64], vap, sh.ident[0:64, 0:64]),
                  waits=list(deps) + ([last] if last else []), inc=True)
        cps = []
        for h in range(2):
            cps.append(C.op("scalar", lambda e, kt=kt, h=h: e.copy(
                vtok[h][:, kt * 33:kt * 33 + 32], ptr[:, h * 32:(h + 1) * 32]),
                waits=[dt], inc=True))
        last = cps[-1]
    vst.close()
    return [last] + d_ones


def build_A3():
    li = 3
    nc = bass.Bass("TRN2", target_bir_lowering=False)
    es = ExitStack()
    C = Ctx(nc, es)
    x_own_in = C.din("x_own", [D, NTOK], F32)
    x_row_in = C.din("x_row", [D, SEQ], F32)
    x_mid = C.dout("x_mid", [D, NTOK], F32)
    att_out = C.dout("att", [4, 128, SEQ], BF16)
    mid_dram = nc.dram_tensor("mid_bounce", [64, 8 * NTOK], BF16, kind="Internal").ap()
    xnf_dram = nc.dram_tensor("xnf_bounce", [D, SEQ], BF16, kind="Internal").ap()

    cn = Consts(C)
    cn.load(C)
    C.barrier()
    C.shared = cn
    sh = cn
    wins = host_wins_declare(C, li)
    cls_buf = C.sbuf("cls_keep", [128, NT], BF16)

    xst = ExitStack()
    X = C.sbuf("X", [128, NT * NTOK], F32, xst)
    xld = []
    for ct in range(NT):
        xld.append(C.dma("gpsimd", X[:, ct * NTOK:(ct + 1) * NTOK],
                         x_own_in[ct * 128:(ct + 1) * 128, :], key="x"))

    def x_own(ct, t0, t1):
        return X[:, ct * NTOK + t0: ct * NTOK + t1]

    ph1 = ExitStack()
    XN_OWN = C.sbuf("XN_OWN", [128, NT * NTOK], BF16, ph1)

    def xn_own(ct, t0, t1):
        return XN_OWN[:, ct * NTOK + t0: ct * NTOK + t1]

    lns1 = ExitStack()
    alloc_lns(C, lns1)
    ln_dep = emit_ln_chunks(C, lambda ct, o, n: X[:, ct * NTOK + o:ct * NTOK + o + n],
                            lambda ct, o, n: XN_OWN[:, ct * NTOK + o:ct * NTOK + o + n],
                            NTOK // LCH, deps=[cn.dep[-1]] + xld)
    lns1.close()
    C.barrier()
    MIDs = C.sbuf("MIDstage", [64, NTOK], BF16, ph1)
    pre_dep = emit_ffn3_pre(C, cn, xn_own, x_own, MIDs, wins, deps=[ln_dep],
                            dram=(mid_dram, C))
    ph1.close()
    C.barrier()

    ph2big = ExitStack()
    XNF = C.sbuf("XNF", [128, NT * SEQ], BF16, ph2big)
    ph2 = ExitStack()
    alloc_lns(C, ph2)
    XPART = C.sbuf("XPART", [128, NT * LCH], F32, ph2)
    for c in range(SEQ // LCH):
        o = c * LCH
        pld = [C.dma("gpsimd", XPART[:, ct * LCH:(ct + 1) * LCH],
                     x_row_in[ct * 128:(ct + 1) * 128, o:o + LCH],
                     waits=[C.mark("vector"), C.mark("scalar")], key="xp")
               for ct in range(NT)]
        emit_layernorm(C, sh,
                       lambda ct: XPART[:, ct * LCH:(ct + 1) * LCH],
                       lambda ct, o=o: XNF[:, ct * SEQ + o: ct * SEQ + o + LCH],
                       LCH, deps=[cn.dep[-1], C.mark("tensor")] + pld)
    xnf_dep = [C.mark("vector")]
    ph2.close()
    for ct in range(NT):
        C.op("vector", lambda e, ct=ct: e.tensor_copy(cls_buf[:, ct:ct + 1],
                                                      XNF[:, ct * SEQ:ct * SEQ + 1]),
             waits=xnf_dep, inc=True)
        C.dma("gpsimd", xnf_dram[ct * 128:(ct + 1) * 128, :], XNF[:, ct * SEQ:(ct + 1) * SEQ],
              waits=xnf_dep, key="xb")
    cls_dep = [C.mark("vector"), (C.dsem["xb"], C.dsem["xb"].n)]
    ph2big.close()
    C.barrier()

    fp = emit_ffn3_post(C, cn, cls_buf, x_own, mid_dram, wins, deps=cls_dep)
    vend0 = C.mark("vector")
    for ct in range(NT):
        C.dma("gpsimd", x_mid[ct * 128:(ct + 1) * 128, :], X[:, ct * NTOK:(ct + 1) * NTOK],
              waits=[vend0, fp], key="out")
    xst.close()
    C.barrier()

    ast = ExitStack()
    ATT = C.sbuf("ATT", [128, 4 * SEQ], BF16, ast)
    XNF2 = C.sbuf("XNF2", [128, NT * SEQ], BF16, ast)
    xrl = []
    for ct in range(NT):
        xrl.append(C.dma("gpsimd", XNF2[:, ct * SEQ:(ct + 1) * SEQ],
                         xnf_dram[ct * 128:(ct + 1) * 128, :], key="xr"))

    def xnf2(ct, t0, t1):
        return XNF2[:, ct * SEQ + t0: ct * SEQ + t1]
    emit_attn_l3(C, cn, xnf2,
                 lambda g, t0, t1: ATT[:, g * SEQ + t0: g * SEQ + t1],
                 wins, deps=xrl)
    vend = C.mark("vector")
    send = C.mark("scalar")
    for g4 in range(4):
        C.dma("gpsimd", att_out[g4], ATT[:, g4 * SEQ:(g4 + 1) * SEQ],
              waits=[vend, send], key="out")
    fin = (C.dsem["out"], C.dsem["out"].n)
    C.op("gpsimd", lambda e: e.nop(), waits=[fin])
    ast.close()
    C.replay()
    es.close()
    return nc, C


def prep_l3(ins, hp):
    caw = np.asarray(ins["L3_attn_c_attn_w"])
    cab = np.asarray(ins["L3_attn_c_attn_b"])
    strips = []
    bcols = np.zeros((128, 12), np.float32)
    for g in range(4):
        gh = 8 * hp + 2 * g
        for j, base in enumerate((D, 2 * D, 0)):
            rows = slice(base + 64 * gh, base + 64 * gh + 128)
            strips.append(host_tile_w(np.ascontiguousarray(caw[rows, :].T))[0])
            bcols[:, g * 3 + j] = cab[rows]
    out = {"w_qkv_strips": np.ascontiguousarray(np.stack(strips)), "qkv_bcols": bcols}
    q2l = np.asarray(ins["L3_attn_q2l_w"])
    kv2l = np.asarray(ins["L3_attn_kv2l_w"])
    wqk = np.asarray(ins["L3_attn_wqk_w"])
    l2ow = np.asarray(ins["L3_attn_l2o_w"])
    lat = np.zeros((8, 128, 128), np.float32)
    for i, (w, par) in enumerate(((q2l, 0), (q2l, 1), (kv2l, 0), (kv2l, 1))):
        for hb in (0, 1):
            for j in range(16):
                lat[i, hb * 64:(hb + 1) * 64, hb * 64 + j] = w[2 * j + par, :]
    for idx, (pin, pout) in enumerate(((0, 0), (1, 0), (0, 1), (1, 1))):
        for hb in (0, 1):
            for jin in range(16):
                for jout in range(16):
                    lat[4 + idx, hb * 64 + jin, hb * 64 + jout] = wqk[2 * jout + pout, 2 * jin + pin]
    out["lat_bd"] = lat
    v2l = np.zeros((128, 64), np.float32)
    for hb in (0, 1):
        for j in range(32):
            v2l[hb * 64:(hb + 1) * 64, hb * 32 + j] = kv2l[j, :]
    out["v2l_bd"] = v2l
    l2o_bd = np.zeros((64, 128), np.float32)
    for hb in (0, 1):
        for l in range(32):
            l2o_bd[hb * 32 + l, hb * 64:(hb + 1) * 64] = l2ow[:, l]
    out["l2o_bd"] = l2o_bd
    latb = np.zeros((128, 8), np.float32)
    q2lb = np.asarray(ins["L3_attn_q2l_b"])
    kv2lb = np.asarray(ins["L3_attn_kv2l_b"])
    wqkb = np.asarray(ins["L3_attn_wqk_b"])
    for hb in (0, 1):
        for j in range(16):
            latb[hb * 64 + j, 0] = q2lb[2 * j]
            latb[hb * 64 + j, 1] = q2lb[2 * j + 1]
            latb[hb * 64 + j, 2] = kv2lb[2 * j]
            latb[hb * 64 + j, 3] = kv2lb[2 * j + 1]
            latb[hb * 64 + j, 5] = wqkb[2 * j]
            latb[hb * 64 + j, 6] = wqkb[2 * j + 1]
        for j in range(32):
            latb[hb * 64 + j, 4] = kv2lb[j]
    latb[:, 4] = 0.0
    vb = np.zeros(128, np.float32)
    vb[0:32] = kv2lb
    vb[32:64] = kv2lb
    latb[:, 4] = vb
    out["lat_bcols"] = latb
    l2obc = np.zeros((128, 1), np.float32)
    l2obc[0:64, 0] = np.asarray(ins["L3_attn_l2o_b"])
    l2obc[64:128, 0] = np.asarray(ins["L3_attn_l2o_b"])
    out["l2o_bcols"] = l2obc
    i = np.arange(16)
    invf = 10000.0 ** (-2.0 * i / 32.0)
    t = np.arange(SEQ)
    ang = t[None, :] * invf[:, None]
    blk = np.zeros((32, SEQ), np.float32)
    cs = blk.copy(); cs[0:16] = np.cos(ang)
    sn = blk.copy(); sn[0:16] = np.sin(ang)
    out["rope_cos"] = np.ascontiguousarray(np.tile(cs, (4, 1))).astype(BFNP)
    out["rope_sin"] = np.ascontiguousarray(np.tile(sn, (4, 1))).astype(BFNP)
    out["wr_tiles"] = tiles_cm(np.ascontiguousarray(np.asarray(ins["L3_ffn_Wr"]).T))
    A = np.asarray(ins["L3_ffn_A"])
    a_tiles = np.concatenate([tiles_cm(np.ascontiguousarray(A[e].T)) for e in range(8)], axis=1)
    out["a_tiles"] = np.ascontiguousarray(a_tiles)
    Bw = np.asarray(ins["L3_ffn_Bw"])
    bw = np.zeros((8, 64, 8 * 128), np.float32)
    for ot in range(8):
        for e in range(8):
            bw[ot, :, e * 128:(e + 1) * 128] = Bw[e, ot * 128:(ot + 1) * 128, :].T
    out["bw_strips"] = bw
    out["wo_strips"] = host_tile_w(np.ascontiguousarray(np.asarray(ins["L3_attn_cproj_w"]).T))
    out["bo_cols"] = np.asarray(ins["L3_attn_cproj_b"]).reshape(NT, 128).T.copy()
    return out



RG_PAIRS = [[0, 1], [2, 3], [4, 5], [6, 7]]


def build_fused():
    nc = bass.Bass("TRN2", target_bir_lowering=False)
    es = ExitStack()
    C = Ctx(nc, es)
    x_in = C.din("x_own", [D, NTOK], F32)
    x_out = C.dout("x_next", [D, NTOK], F32)
    x_cur = nc.dram_tensor("x_cur", [D, NTOK], F32, kind="Internal").ap()
    xn_gin = nc.dram_tensor("xn_gin", [D, NTOK], BF16, kind="Internal").ap()
    xn_gout = nc.dram_tensor("xn_gout", [2 * D, NTOK], BF16, kind="Internal").ap()
    u_in = nc.dram_tensor("u_in", [2 * D, NTOK], BF16, kind="Internal").ap()
    u_out = nc.dram_tensor("u_out", [D, NTOK], BF16, kind="Internal").ap()
    mid_dram = nc.dram_tensor("mid_bounce", [64, 8 * NTOK], BF16, kind="Internal").ap()

    cn = Consts(C)
    cn.load(C)
    C.barrier()
    C.shared = cn
    sh = cn
    WINS = {}
    for li in range(4):
        C.prefix = "L%d_" % li
        WINS[li] = host_wins_declare(C, li)
        WINS[li]["wo_my"] = C.din("wo_my", [NT, 128, 4 * 128], F32)
        WINS[li]["bo"] = C.din("bo_cols", [128, NT], F32)
    C.prefix = ""

    for li in range(4):
        wins = WINS[li]
        st1 = ExitStack()
        X = C.sbuf("X", [128, NT * NTOK], F32, st1)
        XN_OWN = C.sbuf("XN_OWN", [128, NT * NTOK], BF16, st1)
        xld = []
        src = x_in if li == 0 else x_cur
        for ct in range(NT):
            xld.append(C.dma("gpsimd", X[:, ct * NTOK:(ct + 1) * NTOK],
                             src[ct * 128:(ct + 1) * 128, :], key="x",
                             waits=C.fence() if li > 0 else []))

        def x_own(ct, t0, t1, X=X):
            return X[:, ct * NTOK + t0: ct * NTOK + t1]

        def xn_own(ct, t0, t1, XN_OWN=XN_OWN):
            return XN_OWN[:, ct * NTOK + t0: ct * NTOK + t1]

        lns1 = ExitStack()
        alloc_lns(C, lns1)
        ln_dep = emit_ln_chunks(C, lambda ct, o, n, X=X: X[:, ct * NTOK + o:ct * NTOK + o + n],
                                lambda ct, o, n, XN_OWN=XN_OWN: XN_OWN[:, ct * NTOK + o:ct * NTOK + o + n],
                                NTOK // LCH, deps=[cn.dep[-1]] + xld)
        lns1.close()
        C.barrier()
        gi = []
        for ct in range(NT):
            gi.append(C.dma("gpsimd", xn_gin[ct * 128:(ct + 1) * 128, :],
                            XN_OWN[:, ct * NTOK:(ct + 1) * NTOK], key="gi",
                            waits=[ln_dep]))
        ag = C.cc("AllGather", ALU.bypass, RG_PAIRS, [xn_gin], [xn_gout],
                  waits=gi, key="ag")
        if li == 3:
            m3 = ExitStack()
            MIDs = C.sbuf("MIDstage", [64, NTOK], BF16, m3)
            pre = emit_ffn3_pre(C, cn, xn_own, x_own, MIDs, wins, deps=[ln_dep],
                                dram=(mid_dram, C))
            m3.close()
            p3 = ExitStack()
            cls_buf = C.sbuf("cls3", [128, NT], BF16, p3)
            cl = []
            for ct in range(NT):
                cl.append(C.dma("gpsimd", cls_buf[:, ct:ct + 1],
                                xn_gout[ct * 128:ct * 128 + 128, 0:1], key="cl",
                                waits=[ag] + pre, allow_slow_non_contiguous=True))
            fp = emit_ffn3_post(C, cn, cls_buf, x_own, mid_dram, wins,
                                deps=cl + pre)
            p3.close()
            ffn_dep = fp
        else:
            ffn_dep = emit_ffn(C, li, xn_own, x_own, wins, deps=[ln_dep])
        vend0 = C.mark("vector")
        for ct in range(NT):
            C.dma("gpsimd", x_cur[ct * 128:(ct + 1) * 128, :], X[:, ct * NTOK:(ct + 1) * NTOK],
                  waits=[vend0, ffn_dep], key="xs")
        st1.close()
        C.barrier()

        st2 = ExitStack()
        ATT = C.sbuf("ATT", [128, 4 * SEQ], BF16, st2)
        sub = ExitStack()
        XNF = C.sbuf("XNF", [128, NT * SEQ], BF16, sub)
        xfl = []
        for ct in range(NT):
            for half in range(2):
                xfl.append(C.dma("gpsimd", XNF[:, ct * SEQ + half * NTOK: ct * SEQ + (half + 1) * NTOK],
                                 xn_gout[half * D + ct * 128: half * D + (ct + 1) * 128, :],
                                 key="xf", waits=[ag]))
        xnf_dep = list(xfl[-1:])
        if li == 1:
            lns2 = ExitStack()
            alloc_lns(C, lns2)
            gcol = C.sbuf("gcol", [128, NT], F32, lns2)
            bcol = C.sbuf("bcol", [128, NT], F32, lns2)
            dg = C.dma("sync", gcol[:], wins["gamma"], key="c1")
            db = C.dma("sync", bcol[:], wins["beta"], key="c2")
            emit_ln_chunks(C, lambda ct, o, n, XNF=XNF: XNF[:, ct * SEQ + o:ct * SEQ + o + n],
                           lambda ct, o, n, XNF=XNF: XNF[:, ct * SEQ + o:ct * SEQ + o + n],
                           SEQ // LCH, deps=[C.mark("vector"), dg, db] + xnf_dep,
                           affine=(lambda ct, gcol=gcol: gcol[:, ct:ct + 1],
                                   lambda ct, bcol=bcol: bcol[:, ct:ct + 1]))
            lns2.close()
            xnf_dep = [C.mark("vector")]

        def xnf(ct, t0, t1, XNF=XNF):
            return XNF[:, ct * SEQ + t0: ct * SEQ + t1]
        if li == 3:
            emit_attn_l3(C, cn, xnf,
                         lambda g, t0, t1, ATT=ATT: ATT[:, g * SEQ + t0: g * SEQ + t1],
                         wins, deps=xnf_dep + xfl)
        else:
            emit_attn(C, li, xnf,
                      lambda g, h, s, ATT=ATT: ATT[64 * h:64 * h + 64, g * SEQ + s * 512: g * SEQ + (s + 1) * 512],
                      wins, deps=xnf_dep + xfl)
        att_dep = [C.mark("vector"), C.mark("scalar")]
        sub.close()
        C.barrier()
        ust = ExitStack()
        wbf = [C.sbuf("uwb", [128, 4 * 128], BF16, ust) for _ in range(2)]
        urow = [C.sbuf("urow", [128, 512], BF16, ust) for _ in range(2)]
        psu = [C.psum("psu", [128, 512], F32, ust) for _ in range(2)]
        wcast = [None, None]
        wlastu = [None, None]
        ud = []
        for ot in range(NT):
            b = ot % 2
            wcast[b] = C.dma("gpsimd", wbf[b][:], wins["wo_my"][ot], key="sw%d" % b,
                             waits=[wlastu[b]] if wlastu[b] else
                             [C.mark("tensor"), C.mark("vector")])
            for sg in range(NSG):
                pp = psu[sg % 2]
                for g4 in range(4):
                    C.op("tensor", lambda e, g4=g4, sg=sg, b=b, pp=pp, ATT=ATT, wbf=wbf: e.matmul(
                        pp[:, 0:512], wbf[b][:, g4 * 128:(g4 + 1) * 128],
                        ATT[:, g4 * SEQ + sg * 512:g4 * SEQ + (sg + 1) * 512],
                        start=(g4 == 0), stop=(g4 == 3)),
                        waits=[wcast[b], C.mark("scalar")] + att_dep if g4 == 0 else (),
                        inc=(g4 == 3))
                mmd = C.mark("tensor")
                wlastu[b] = mmd
                uiw = (C.dsem["ui"], C.dsem["ui"].n) if "ui" in C.dsem else None
                ue = C.op("scalar", lambda e, sg=sg, pp=pp, urow=urow: e.copy(
                    urow[sg % 2][:], pp[:, 0:512]),
                          waits=[mmd, uiw], inc=True)
                half = (sg * 512) // NTOK
                tg = (sg * 512 - half * NTOK) // 512
                ud.append(C.dma("gpsimd", u_in[half * D + ot * 128: half * D + (ot + 1) * 128,
                                               tg * 512:(tg + 1) * 512],
                                urow[sg % 2][:], key="ui", waits=[ue]))
        ust.close()
        st2.close()
        C.barrier()
        rs = C.cc("ReduceScatter", ALU.add, RG_PAIRS, [u_in], [u_out],
                  waits=ud, key="rs")
        st3 = ExitStack()
        X2 = C.sbuf("X2", [128, NT * NTOK], F32, st3)
        R = C.sbuf("R", [128, NT * NTOK], BF16, st3)
        bocol = C.sbuf("bocol", [128, NT], F32, st3)
        dbo = C.dma("sync", bocol[:], wins["bo"], key="c0", waits=C.fence())
        x2l, rl = [], []
        for ct in range(NT):
            x2l.append(C.dma("gpsimd", X2[:, ct * NTOK:(ct + 1) * NTOK],
                             x_cur[ct * 128:(ct + 1) * 128, :], key="x2",
                             waits=[(C.dsem["xs"], C.dsem["xs"].n)]))
            rl.append(C.dma("gpsimd", R[:, ct * NTOK:(ct + 1) * NTOK],
                            u_out[ct * 128:(ct + 1) * 128, :], key="r", waits=[rs]))
        dst = x_out if li == 3 else x_cur
        for ct in range(NT):
            d1 = C.op("vector", lambda e, ct=ct, X2=X2, R=R, bocol=bocol: e.scalar_tensor_tensor(
                X2[:, ct * NTOK:(ct + 1) * NTOK], R[:, ct * NTOK:(ct + 1) * NTOK],
                bocol[:, ct:ct + 1], X2[:, ct * NTOK:(ct + 1) * NTOK], ALU.add, ALU.add),
                waits=[x2l[-1], rl[-1], dbo], inc=True)
            C.dma("gpsimd", dst[ct * 128:(ct + 1) * 128, :], X2[:, ct * NTOK:(ct + 1) * NTOK],
                  waits=[d1], key="out" if li == 3 else "xs")
        st3.close()
        C.barrier()
    fin = (C.dsem["out"], C.dsem["out"].n)
    C.op("gpsimd", lambda e: e.nop(), waits=[fin])
    C.replay()
    es.close()
    return nc, C



_PREP = {0: prep_l0, 1: prep_l1, 2: prep_l2, 3: prep_l3}


def kernel(**inputs):
    set_size(4096)
    consts = host_consts()
    x = np.asarray(inputs["L0_attn_x"], dtype=np.float32)
    nc, C = build_fused()
    in_maps = []
    for c in range(8):
        r, hp = c // 2, c % 2
        d = dict(consts)
        for li in range(4):
            p = _PREP[li](inputs, hp)
            if li == 1:
                p["wo_strips"] = host_tile_w(np.eye(D, dtype=np.float32))
                p["bo_cols"] = np.zeros((128, NT), np.float32)
            p["wo_my"] = np.ascontiguousarray(p["wo_strips"][:, :, hp * 512:(hp + 1) * 512])
            for k, v in p.items():
                d["L%d_%s" % (li, k)] = v
        d = {k: v for k, v in d.items() if k in C.inputs}
        d["x_own"] = np.ascontiguousarray(x[r, hp * NTOK:(hp + 1) * NTOK, :].T)
        in_maps.append(d)
    res = run_bass_kernel_spmd(nc, in_maps, core_ids=list(range(8))).results
    out = np.empty((4, SEQ, D), np.float32)
    for c in range(8):
        r, hp = c // 2, c % 2
        out[r, hp * NTOK:(hp + 1) * NTOK, :] = np.asarray(res[c]["x_next"]).T
    return out
